# version 15
# baseline (speedup 1.0000x reference)
import contextlib
import numpy as np
import ml_dtypes
import concourse.bass as bass
import concourse.mybir as mybir
from concourse.bass_utils import run_bass_kernel_spmd

F32 = mybir.dt.float32
BF16 = mybir.dt.bfloat16
I32 = mybir.dt.int32
AF = mybir.ActivationFunctionType
ALU = mybir.AluOpType
AX = mybir.AxisListType.X
bfdt = ml_dtypes.bfloat16

T = 4096
V = 2048
D = 512
NI = 5504
NT = T // 128
NV = V // 128
ND = D // 128
NO = NI // 128
EPS = 1e-5

ENGINES = ("sync", "tensor", "vector", "scalar", "gpsimd")
SKIP = frozenset()


class Sem:
    __slots__ = ("h", "v", "name")

    def __init__(self, es, nc, name):
        self.h = es.enter_context(nc.semaphore(name))
        self.v = 0
        self.name = name


class EM:

    def __init__(self):
        self.q = {e: [] for e in ENGINES}

    def op(self, eng, fn, waits=(), incs=(), pw=(), pa=()):
        wl = [(s, int(v)) for (s, v) in waits]
        il = []
        for s, a in incs:
            s.v += int(a)
            il.append((s, int(a)))
        self.q[eng].append((fn, wl, il, tuple(pw), tuple(pa)))

    def dma(self, eng, sem, fn, waits=()):
        wl = [(sem, sem.v)] + list(waits)
        self.op(eng, fn, waits=wl, incs=[(sem, 16)])
        return sem.v

    def psum_check(self):
        eng_ix = {e: i for i, e in enumerate(ENGINES)}
        producers = {}
        for e in ENGINES:
            for i, (fn, wl, il, pw, pa) in enumerate(self.q[e]):
                for s, a in il:
                    producers.setdefault(s.name, []).append((e, i, a))
        import bisect
        cur = {}
        pos = {e: 0 for e in ENGINES}
        clocks = {e: [] for e in ENGINES}
        semhist = {}
        done = False
        while not done:
            done = True
            for e in ENGINES:
                while pos[e] < len(self.q[e]):
                    fn, wl, il, pw, pa = self.q[e][pos[e]]
                    if not all(cur.get(s.name, 0) >= v for s, v in wl):
                        break
                    vc = [0] * len(ENGINES)
                    if pos[e] > 0:
                        vc = list(clocks[e][pos[e] - 1])
                    vc[eng_ix[e]] = pos[e] + 1
                    for s, v in wl:
                        if v <= 0:
                            continue
                        hist = semhist.get(s.name, [])
                        lo, hi = 0, len(hist)
                        while lo < hi:
                            midi = (lo + hi) // 2
                            if hist[midi][0] >= v:
                                hi = midi
                            else:
                                lo = midi + 1
                        if lo < len(hist):
                            pvc = hist[lo][1]
                            vc = [max(a, b) for a, b in zip(vc, pvc)]
                        else:
                            raise RuntimeError(f"wait on {s.name}>={v} never produced")
                    clocks[e].append(vc)
                    for s, a in il:
                        cur[s.name] = cur.get(s.name, 0) + a
                        semhist.setdefault(s.name, []).append((cur[s.name], vc))
                    pos[e] += 1
                    done = False
        bank_pe = {}
        bank_va = {}
        for e in ENGINES:
            for i, (fn, wl, il, pw, pa) in enumerate(self.q[e]):
                for b in pw:
                    bank_pe.setdefault(b, []).append((e, i))
                for b in pa:
                    bank_va.setdefault(b, []).append((e, i))
        def hb(e1, i1, e2, i2):
            return clocks[e2][i2][eng_ix[e1]] >= i1 + 1
        bad = []
        for b in sorted(set(bank_pe) | set(bank_va)):
            for (e1, i1) in bank_pe.get(b, []):
                for (e2, i2) in bank_va.get(b, []):
                    if not (hb(e1, i1, e2, i2) or hb(e2, i2, e1, i1)):
                        bad.append((b, e1, i1, e2, i2))
        if bad:
            raise RuntimeError(f"PSUM collisions (bank, pe_op, other): {bad[:10]} (total {len(bad)})")

    def check(self):
        cur = {}
        pos = {e: 0 for e in ENGINES}
        progress = True
        while progress:
            progress = False
            for e in ENGINES:
                while pos[e] < len(self.q[e]):
                    fn, wl, il = self.q[e][pos[e]][:3]
                    if all(cur.get(s.name, 0) >= v for s, v in wl):
                        for s, a in il:
                            cur[s.name] = cur.get(s.name, 0) + a
                        pos[e] += 1
                        progress = True
                    else:
                        break
        stuck = {e: (len(self.q[e]) - pos[e]) for e in ENGINES if pos[e] < len(self.q[e])}
        if stuck:
            msgs = []
            for e in ENGINES:
                if pos[e] < len(self.q[e]):
                    fn, wl, il = self.q[e][pos[e]]
                    missing = [(s.name, v, cur.get(s.name, 0)) for s, v in wl
                               if cur.get(s.name, 0) < v]
                    msgs.append(f"{e}@{pos[e]}/{len(self.q[e])} needs {missing}")
            raise RuntimeError("deadlock: " + "; ".join(msgs))

    def replay(self, block):
        def mk(eng):
            ops = self.q[eng]

            def body(e):
                drain_incs = eng in ("vector", "scalar")
                for fn, wl, il, pw, pa in ops:
                    for s, v in wl:
                        if v > 0:
                            e.wait_ge(s.h, v)
                    inst = fn(e)
                    if il:
                        if inst is None:
                            inst = e.nop()
                        elif drain_incs:
                            inst = e.drain()
                        for s, a in il:
                            inst = inst.then_inc(s.h, a)
            return body
        block.sync(mk("sync"))
        block.tensor(mk("tensor"))
        block.vector(mk("vector"))
        block.scalar(mk("scalar"))
        block.gpsimd(mk("gpsimd"))




class GatherFeeder:

    def __init__(self, em, sems, bufs, lutc, jobs_gu, jobs_dn, NI):
        self.em = em
        self.sv, self.sgi, self.sgw, self.sgp = sems
        self.idxb, self.accb, self.tmpb = bufs
        self.lutc = lutc
        self.subs = []
        oun = (NI + 1023) // 1024
        for ou in range(oun):
            ob = ou * 1024
            ow = min(1024, NI - ob)
            for scr, idxd in jobs_gu:
                for hc in range(16):
                    self.subs.append((scr, idxd, hc, ob, ow, "ou", ou))
        for scr, idxd, nblk, FD in jobs_dn:
            for oc in range(nblk):
                for half in range(2):
                    self.subs.append((scr, idxd, oc, half * 1024, min(1024, FD - half * 1024),
                                      "dn", oc))
        self.n = len(self.subs)
        self.pos = 0
        self.opi = 0
        self.loaded = -1
        self.slot_tag = [None, None, None, None]
        self.acc_tag = [0, 0]
        self.ou_tag = {}
        self.wdn_tag = {}
        self._cnt = {}
        self.done = False

    def _load(self, k):
        if k >= self.n or k <= self.loaded:
            return
        scr, idxd, blk, ob, ow, kind, key = self.subs[k]
        sl = k % 4
        w = []
        if self.slot_tag[sl] is not None:
            w.append(self.slot_tag[sl])
        self.em.dma("gpsimd", self.sgi, lambda e, idxd=idxd, blk=blk, ob=ob, ow=ow, sl=sl:
                    e.dma_start(out=self.idxb[:, sl, 0:ow], in_=idxd[blk, :, ob:ob + ow]),
                    waits=w)
        self.loaded = k

    def feed(self, nops):
        em, sv, sgi, sgw = self.em, self.sv, self.sgi, self.sgw
        while nops > 0 and not self.done:
            p = self.pos
            a, b = 2 * p, 2 * p + 1
            eng = "gpsimd" if (p % 4 == 3) else "vector"
            gcost = 4 if eng == "gpsimd" else 1
            if self.opi == 0:
                for k in (a, b, a + 2, b + 2):
                    self._load(k)
            two = b < self.n
            seq = []
            for j in range(16):
                seq.append((a, j, "ts"))
                if two:
                    seq.append((b, j, "ts"))
                seq.append((a, j, "tt"))
                if two:
                    seq.append((b, j, "tt"))
            while nops > 0 and self.opi < len(seq):
                k, j, op = seq[self.opi]
                scr, idxd, blk, ob, ow, kind, key = self.subs[k]
                sl = k % 4
                asl = k % 2
                gi_tag = 16 * (self.loaded + 1)
                if op == "ts" and j == 0:
                    w = [(sgi, 16 * (k + 1))]
                    if self.acc_tag[asl]:
                        w.append((sgw, self.acc_tag[asl]))
                    em.op(eng, lambda e, sl=sl, asl=asl, ow=ow: e.tensor_scalar(
                        out=self.accb[:, asl, 0:ow], in0=self.idxb[:, sl, 0:ow],
                        scalar1=0.0, scalar2=self.lutc[:, 0:1], op0=ALU.is_equal, op1=ALU.mult),
                        waits=w)
                elif op == "ts":
                    t = (k % 2) + (2 if eng == "gpsimd" else 0)
                    em.op(eng, lambda e, sl=sl, j=j, ow=ow, t=t: e.tensor_scalar(
                        out=self.tmpb[:, t, 0:ow], in0=self.idxb[:, sl, 0:ow],
                        scalar1=float(j), scalar2=self.lutc[:, j:j + 1], op0=ALU.is_equal, op1=ALU.mult))
                elif op == "tt" and j == 0:
                    pass
                else:
                    last = (j == 15)
                    t = (k % 2) + (2 if eng == "gpsimd" else 0)
                    em.op(eng, lambda e, asl=asl, ow=ow, t=t: e.tensor_add(
                        out=self.accb[:, asl, 0:ow], in0=self.accb[:, asl, 0:ow],
                        in1=self.tmpb[:, t, 0:ow]),
                        incs=([(sv, 1)] if (last and eng == "vector") else []))
                    if last:
                        if eng == "gpsimd":
                            em.op("gpsimd", lambda e: e.drain(), incs=[(self.sgp, 1)])
                            self.slot_tag[sl] = (self.sgp, self.sgp.v)
                            stw = [(self.sgp, self.sgp.v)]
                        else:
                            self.slot_tag[sl] = (sv, sv.v)
                            stw = [(sv, sv.v)]
                        em.dma("gpsimd", sgw, lambda e, scr=scr, blk=blk, ob=ob, ow=ow, asl=asl:
                               e.dma_start(out=scr[blk, :, ob:ob + ow], in_=self.accb[:, asl, 0:ow]),
                               waits=stw)
                        self.acc_tag[asl] = sgw.v
                        ck = (kind, key)
                        self._cnt[ck] = self._cnt.get(ck, 0) + 1
                        if kind == "ou":
                            if self._cnt[ck] == 32:
                                self.ou_tag[key] = sgw.v
                        else:
                            if self._cnt[ck] == 2:
                                self.wdn_tag[key] = sgw.v
                self.opi += 1
                if op == "tt" and j == 0:
                    continue
                nops -= gcost
            if self.opi >= len(seq):
                self.pos += 1
                self.opi = 0
                if 2 * self.pos >= self.n:
                    self.done = True

    def finish(self):
        while not self.done:
            self.feed(1 << 20)


def build_nc(upto=5, debug=False, p1_tiles=NT):
    nc = bass.Bass("TRN2", target_bir_lowering=False, detect_race_conditions=False)
    es = contextlib.ExitStack()

    x_d = nc.dram_tensor("x", [T, V], F32, kind="ExternalInput").ap()
    wq_d = nc.dram_tensor("wq", [V, D], F32, kind="ExternalInput").ap()
    wk_d = nc.dram_tensor("wk", [3, V, D], F32, kind="ExternalInput").ap()
    wv_d = nc.dram_tensor("wv", [3, V, D], F32, kind="ExternalInput").ap()
    wo_d = nc.dram_tensor("wo", [D, V], F32, kind="ExternalInput").ap()
    wg_d = nc.dram_tensor("wg", [V, 3], F32, kind="ExternalInput").ap()
    dlog_d = nc.dram_tensor("dlog", [1, 3], F32, kind="ExternalInput").ap()
    osc_d = nc.dram_tensor("osc", [1, 1], F32, kind="ExternalInput").ap()
    lut_d = nc.dram_tensor("lut", [1, 16], F32, kind="ExternalInput").ap()
    gidx_d = nc.dram_tensor("gidx", [NV, 128, NI], I32, kind="ExternalInput").ap()
    uidx_d = nc.dram_tensor("uidx", [NV, 128, NI], I32, kind="ExternalInput").ap()
    didx_d = nc.dram_tensor("didx", [NO, 128, V], I32, kind="ExternalInput").ap()
    ident_d = nc.dram_tensor("ident", [128, 128], BF16, kind="ExternalInput").ap()
    expnT_d = nc.dram_tensor("expnT", [128, 128], F32, kind="ExternalInput").ap()
    causT_d = nc.dram_tensor("causT", [128, 128], F32, kind="ExternalInput").ap()
    iota_d = nc.dram_tensor("iota", [128, 1], F32, kind="ExternalInput").ap()
    riota_d = nc.dram_tensor("riota", [128, 1], F32, kind="ExternalInput").ap()

    out_d = nc.dram_tensor("out", [T, V], F32, kind="ExternalOutput").ap()

    KSC = "ExternalOutput" if debug else "Internal"
    qT_scr = nc.dram_tensor("qT_scr", [ND, 128, T], BF16, kind=KSC).ap()
    ret_scr = nc.dram_tensor("ret_scr", [3, NT, 128, D], BF16, kind=KSC).ap()
    wgt_scr = nc.dram_tensor("wgt_scr", [NV, 128, NI], BF16, kind=KSC).ap()
    wup_scr = nc.dram_tensor("wup_scr", [NV, 128, NI], BF16, kind=KSC).ap()
    wdn_scr = nc.dram_tensor("wdn_scr", [NO, 128, V], BF16, kind=KSC).ap()
    hT_scr = nc.dram_tensor("hT_scr", [NO, 128, T], BF16, kind=KSC).ap()
    lnd_scr = nc.dram_tensor("lnd_scr", [1, 3], F32).ap()

    S = lambda n: Sem(es, nc, n)
    sp, sv, ss = S("sp"), S("sv"), S("ss")
    sy, sgp = S("sy"), S("sgp")
    sdc, sgb, sdx, sqw = S("sdc"), S("sgb"), S("sdx"), S("sqw")
    sgk, sdqt, srw = S("sgk"), S("sdqt"), S("srw")
    sgi, sgw = S("sgi"), S("sgw")
    sdw, shw = S("sdw"), S("shw")
    sdr, sdd, sdh, sdx2, sout = S("sdr"), S("sdd"), S("sdh"), S("sdx2"), S("sout")

    em = EM()

    def _finalize():
        if debug:
            xnT_dbg = nc.dram_tensor("xnT_dbg", [128, NV * T], BF16, kind="ExternalOutput").ap()
            gate_dbg = nc.dram_tensor("gate_dbg", [128, NT * 3], F32, kind="ExternalOutput").ap()
            if upto <= 4:
                for vc in range(NV):
                    em.dma("sync", sout, lambda e, vc=vc: e.dma_start(
                        out=xnT_dbg[:, vc * T:(vc + 1) * T], in_=xnT[:, vc, :]))
            em.dma("sync", sout, lambda e: e.dma_start(out=gate_dbg[:], in_=gate_sb[:]))
            em.op("sync", lambda e: None, waits=[(sout, sout.v)])
        em.check()
        em.psum_check()
        block = es.enter_context(nc.Block())
        em.replay(block)
        return nc, es

    ps_h = es.enter_context(nc.psum_tensor("ps", [128, 4096], F32))
    ps = ps_h[:]

    def PB(b, n=512):
        return ps[:, b * 512:b * 512 + n]

    psb = ps.bitcast(BF16)

    def PBb(b, off=0, n=1024):
        return psb[:, b * 1024 + off:b * 1024 + off + n]

    SBP = lambda name, shape, dt: es.enter_context(nc.sbuf_tensor("sb_" + name, shape, dt))
    ident = SBP("ident", [128, 128], BF16)
    expnT = SBP("expnT", [128, 128], F32)
    causT = SBP("causT", [128, 128], F32)
    iota = SBP("iotac", [128, 1], F32)
    riota = SBP("riotac", [128, 1], F32)
    lutc = SBP("lutc", [128, 16], F32)
    oscc = SBP("oscc", [128, 1], F32)
    lndc = SBP("lndc", [128, 3], F32)
    dpf = SBP("dpf", [128, 3], F32)
    dpr = SBP("dpr", [128, 3], F32)
    dCc = SBP("dCc", [128, 3], F32)
    wmask = SBP("wmask", [128, 3 * 128], F32)
    gate_sb = SBP("gate_sb", [128, NT * 3], F32)
    row3 = SBP("row3", [1, 3], F32)
    dec3 = SBP("dec3", [1, 3], F32)
    lnd3 = SBP("lnd3", [1, 3], F32)
    t128c = SBP("t128c", [128, 3], F32)
    dummy = SBP("dummy", [128, 4], F32)
    gf_idx = SBP("gf_idx", [128, 4, 1024], BF16)
    gf_acc = SBP("gf_acc", [128, 2, 1024], BF16)
    gf_tmp = SBP("gf_tmp", [128, 4, 1024], BF16)
    mid = contextlib.ExitStack()
    xnT = mid.enter_context(nc.sbuf_tensor("sb_xnT", [128, NV, T], BF16))

    em.dma("sync", sdc, lambda e: e.dma_start(out=ident[:], in_=ident_d[:]))
    em.dma("sync", sdc, lambda e: e.dma_start(out=expnT[:], in_=expnT_d[:]))
    em.dma("sync", sdc, lambda e: e.dma_start(out=causT[:], in_=causT_d[:]))
    em.dma("sync", sdc, lambda e: e.dma_start(out=iota[:], in_=iota_d[:]))
    em.dma("sync", sdc, lambda e: e.dma_start(out=riota[:], in_=riota_d[:]))
    em.dma("sync", sdc, lambda e: e.dma_start(out=row3[:], in_=dlog_d[:]))
    c_base = sdc.v
    if "nobcast" not in SKIP:
        em.dma("gpsimd", sgb, lambda e: e.dma_start(out=lutc[:], in_=lut_d[0:1, :].partition_broadcast(128)))
        em.dma("gpsimd", sgb, lambda e: e.dma_start(out=oscc[:], in_=osc_d[0:1, :].partition_broadcast(128)))

    em.op("vector", lambda e: e.memset(dummy[:], 0.0), incs=[(sv, 1)])
    if "nosetup" in SKIP:
        _skip_setup = True
    lnd_ready = sgb.v
    if "nosetup" not in SKIP:
     em.op("scalar", lambda e: e.activation(out=dec3[:], in_=row3[:], func=AF.Sigmoid),
          waits=[(sdc, c_base)], incs=[(ss, 1)])
     em.op("scalar", lambda e: e.activation(out=lnd3[:], in_=dec3[:], func=AF.Ln),
          waits=[(ss, ss.v)], incs=[(ss, 1)])
     em.dma("sync", sdc, lambda e: e.dma_start(out=lnd_scr[:], in_=lnd3[:]), waits=[(ss, ss.v)])
     em.dma("gpsimd", sgb, lambda e: e.dma_start(out=lndc[:], in_=lnd_scr[0:1, :].partition_broadcast(128)),
           waits=[(sdc, sdc.v)])
     lnd_ready = sgb.v
     for ch in range(3):
         em.op("scalar", lambda e, ch=ch: e.activation(out=dpf[:, ch:ch + 1], in_=iota[:], func=AF.Exp,
                                                       scale=lndc[:, ch:ch + 1]),
               waits=[(sgb, lnd_ready)], incs=[(ss, 1)])
         em.op("scalar", lambda e, ch=ch: e.activation(out=dpr[:, ch:ch + 1], in_=riota[:], func=AF.Exp,
                                                       scale=lndc[:, ch:ch + 1]),
               incs=[(ss, 1)])
         em.op("vector", lambda e, ch=ch: e.tensor_scalar(out=t128c[:, ch:ch + 1], in0=lndc[:, ch:ch + 1],
                                                          scalar1=128.0, scalar2=None, op0=ALU.mult),
               waits=[(sgb, lnd_ready)], incs=[(sv, 1)])
         em.op("scalar", lambda e, ch=ch: e.activation(out=dCc[:, ch:ch + 1], in_=t128c[:, ch:ch + 1], func=AF.Exp),
               waits=[(sv, sv.v)], incs=[(ss, 1)])
         em.op("vector", lambda e, ch=ch: e.tensor_scalar(out=PB(7, 128), in0=expnT[:],
                                                          scalar1=lndc[:, ch:ch + 1], scalar2=None, op0=ALU.mult),
               waits=[(sgb, lnd_ready), (ss, ss.v)], incs=[(sv, 1)], pa=[7])
         em.op("scalar", lambda e: e.activation(out=PB(6, 128), in_=PB(7, 128), func=AF.Exp),
               waits=[(sv, sv.v)], incs=[(ss, 1)], pa=[6, 7])
         em.op("vector", lambda e, ch=ch: e.tensor_mul(out=wmask[:, ch * 128:(ch + 1) * 128], in0=PB(6, 128),
                                                       in1=causT[:]),
               waits=[(ss, ss.v)], incs=[(sv, 1)], pa=[6])

     if upto == 0:
         em.op("sync", lambda e: None, waits=[(sdc, sdc.v)])
         em.op("gpsimd", lambda e: None, waits=[(sgb, sgb.v)])
         em.op("scalar", lambda e: None, waits=[(sv, sv.v)])
         em.op("vector", lambda e: None, waits=[(ss, ss.v)])
         em.check()
         em.psum_check()
         _blk = es.enter_context(nc.Block())
         em.replay(_blk)
         return nc, es

    gf = GatherFeeder(em, (sv, sgi, sgw, sgp), (gf_idx, gf_acc, gf_tmp), lutc,
                      [(wgt_scr, gidx_d), (wup_scr, uidx_d)], [(wdn_scr, didx_d, NO, V)], NI)

    p1 = contextlib.ExitStack()
    SB1 = lambda name, shape, dt: p1.enter_context(nc.sbuf_tensor("sb_" + name, shape, dt))
    x_f = SB1("x_f", [128, 2, V], F32)
    junk = SB1("junk", [128, V], F32)
    xn_nat = SB1("xn_nat", [128, 2, V], BF16)
    wq_bf = SB1("wq_bf", [128, NV, D], BF16)
    wg_bf = SB1("wg_bf", [128, NV, 3], BF16)
    qtt = SB1("qtt", [128, 2, D], BF16)
    exs = SB1("exs", [128, 3], F32)
    cols = SB1("cols", [128, 18], F32)
    sxa, sxb = cols[:, 0:1], cols[:, 1:2]
    sqa, sqb = cols[:, 2:3], cols[:, 3:4]
    m_c, var_c, a_c, b_c = cols[:, 4:5], cols[:, 5:6], cols[:, 6:7], cols[:, 7:8]
    lnv_c, t_c, t2_c, msq_c = cols[:, 8:9], cols[:, 9:10], cols[:, 10:11], cols[:, 11:12]
    mx_c, nmx_c, sm_c, iv_c = cols[:, 12:13], cols[:, 13:14], cols[:, 14:15], cols[:, 15:16]
    eps_c = cols[:, 16:17]
    em.op("vector", lambda e: e.memset(eps_c, EPS), incs=[(sv, 1)])

    if "wqg" not in SKIP:
        em.dma("gpsimd", sgb, lambda e: e.dma_start(
            out=wq_bf[:], in_=wq_d.rearrange("(c p) d -> p c d", p=128)))
        em.dma("gpsimd", sgb, lambda e: e.dma_start(
            out=wg_bf[:], in_=wg_d.rearrange("(c p) d -> p c d", p=128)))
    wqg_ready = sgb.v

    norm_done, stats_used, trevac, gate_ex = {}, {}, {}, {}
    if debug:
        cols_dbg = nc.dram_tensor("cols_dbg", [128, NT * 18], F32, kind="ExternalOutput").ap()
    qevac = 0
    qt_written = 0
    if "xact" in SKIP or "actonly" in SKIP:
        em.dma("sync", sdx, lambda e: e.dma_start(out=x_f[:, 0, :], in_=x_d[0:128, :]))
        em.op("scalar", lambda e: e.activation(out=xn_nat[:, 0, :], in_=x_f[:, 0, :],
                                               func=AF.Identity, scale=0.02),
              waits=[(sdx, sdx.v)], incs=[(ss, 1)])
        em.op("scalar", lambda e: e.activation(out=xn_nat[:, 1, :], in_=x_f[:, 0, :],
                                               func=AF.Identity, scale=0.02), incs=[(ss, 1)])
    for tI in range(p1_tiles):
        xs = tI % 2
        if "actonly" in SKIP:
            w = [(ss, ss.v)]
            if tI >= 2:
                w.append((sv, trevac[tI - 2]))
            em.op("scalar", lambda e, xs=xs: e.activation(out=xn_nat[:, xs, :], in_=x_f[:, 0, :],
                                                          func=AF.Identity, scale=0.02),
                  waits=w, incs=[(ss, 1)])
            norm_done[tI] = ss.v
            stats_used[tI] = sv.v
            tb = (tI % 2) * 2
            w = [(sdc, c_base), (ss, ss.v)]
            if tI >= 2:
                w.append((sv, trevac[tI - 2]))
            for vc in range(NV):
                em.op("tensor", lambda e, xs=xs, vc=vc, tb=tb: e.transpose(
                    PBb(tb + (vc // 8), (vc % 8) * 128, 128),
                    xn_nat[:, xs, vc * 128:(vc + 1) * 128], ident[:]),
                    waits=(w if vc == 0 else []), incs=([(sp, 1)] if vc == NV - 1 else []),
                    pw=[tb + (vc // 8)])
            em.op("vector", lambda e, tI=tI, tb=tb: e.tensor_copy(
                out=xnT[:, :, tI * 128:(tI + 1) * 128], in_=PBb(tb, 0, 2048)),
                waits=[(sp, sp.v)], incs=[(sv, 1)], pa=[tb, tb + 1])
            trevac[tI] = sv.v
            continue
        if "xact" in SKIP:
            norm_done[tI] = ss.v
            stats_used[tI] = sv.v
            tb = (tI % 2) * 2
            w = [(sdc, c_base), (ss, ss.v)]
            if tI >= 2:
                w.append((sv, trevac[tI - 2]))
            for vc in range(NV):
                em.op("tensor", lambda e, xs=xs, vc=vc, tb=tb: e.transpose(
                    PBb(tb + (vc // 8), (vc % 8) * 128, 128),
                    xn_nat[:, xs, vc * 128:(vc + 1) * 128], ident[:]),
                    waits=(w if vc == 0 else []), incs=([(sp, 1)] if vc == NV - 1 else []),
                    pw=[tb + (vc // 8)])
            em.op("vector", lambda e, tI=tI, tb=tb: e.tensor_copy(
                out=xnT[:, :, tI * 128:(tI + 1) * 128], in_=PBb(tb, 0, 2048)),
                waits=[(sp, sp.v)], incs=[(sv, 1)], pa=[tb, tb + 1])
            trevac[tI] = sv.v
            continue
        w = [(ss, norm_done[tI - 2])] if tI >= 2 else []
        em.dma("sync", sdx, lambda e, tI=tI, xs=xs: e.dma_start(
            out=x_f[:, xs, :], in_=x_d[tI * 128:(tI + 1) * 128, :]), waits=w)
        x_tag = sdx.v
        if "stats" in SKIP:
            em.op("scalar", lambda e, xs=xs: e.activation(out=xn_nat[:, xs, :], in_=x_f[:, xs, :],
                                                          func=AF.Identity, scale=0.02),
                  waits=[(sdx, x_tag)] + ([(sv, trevac[tI - 2])] if tI >= 2 else []), incs=[(ss, 1)])
            norm_done[tI] = ss.v
            stats_used[tI] = sv.v
        if "stats" not in SKIP:
            w = [(sdx, x_tag)] + ([(sv, stats_used[tI - 1])] if tI >= 1 else [])
            if tI >= 1:
                w.append((ss, stats_sq_done))
            em.op("vector", lambda e, xs=xs: e.reduce_sum(out=sxa, in_=x_f[:, xs, :], axis=AX),
                  waits=w)
            em.op("scalar", lambda e, xs=xs: e.activation(out=junk[:], in_=x_f[:, xs, :],
                                                          func=AF.Square),
                  waits=[(sdx, x_tag)] + ([(sv, stats_used[tI - 1])] if tI >= 1 else []),
                  incs=[(ss, 1)])
            stats_sq_done = ss.v
            em.op("vector", lambda e: e.reduce_sum(out=sqa, in_=junk[:], axis=AX),
                  waits=[(ss, stats_sq_done)])
            em.op("vector", lambda e: e.tensor_scalar(out=m_c, in0=sxa, scalar1=1.0 / V, scalar2=None, op0=ALU.mult),
                  incs=[(sv, 1)])
            em.op("vector", lambda e: e.tensor_scalar(out=t2_c, in0=sqa, scalar1=1.0 / V, scalar2=None, op0=ALU.mult))
            em.op("vector", lambda e: e.tensor_mul(out=msq_c, in0=m_c, in1=m_c),
                  waits=[(sv, sv.v)], incs=[(sv, 1)])
            em.op("vector", lambda e: e.tensor_sub(out=var_c, in0=t2_c, in1=msq_c),
                  waits=[(sv, sv.v)], incs=[(sv, 1)])
            stats_used[tI] = sv.v
            em.op("scalar", lambda e: e.activation(out=lnv_c, in_=var_c, func=AF.Ln, bias=eps_c),
                  waits=[(sv, sv.v)], incs=[(ss, 1)])
            em.op("scalar", lambda e: e.activation(out=a_c, in_=lnv_c, func=AF.Exp, scale=-0.5),
                  waits=[(ss, ss.v)], incs=[(ss, 1)])
            em.op("vector", lambda e: e.scalar_tensor_tensor(out=b_c, in0=m_c, scalar=-1.0,
                                                             in1=a_c, op0=ALU.mult, op1=ALU.mult),
                  waits=[(ss, ss.v)], incs=[(sv, 1)])
            w = [(sv, sv.v)] + ([(sv, trevac[tI - 2])] if tI >= 2 else [])
            em.op("scalar", lambda e, xs=xs: e.activation(out=xn_nat[:, xs, :], in_=x_f[:, xs, :],
                                                          func=AF.Identity, scale=a_c, bias=b_c),
                  waits=w, incs=[(ss, 1)])
            norm_done[tI] = ss.v
        if debug and "stats" not in SKIP:
            em.dma("sync", sqw, lambda e, tI=tI: e.dma_start(
                out=cols_dbg[:, tI * 18:(tI + 1) * 18], in_=cols[:]), waits=[(ss, norm_done[tI])])
        tb = (tI % 2) * 2
        w = [(ss, norm_done[tI]), (sdc, c_base)]
        if tI >= 2:
            w.append((sv, trevac[tI - 2]))
        if "tr" in SKIP:
            em.op("vector", lambda e: e.tensor_copy(out=dummy[:, 0:2], in_=dummy[:, 2:4]),
                  waits=w, incs=[(sv, 1)])
            trevac[tI] = sv.v
        else:
            for vc in range(NV):
                em.op("tensor", lambda e, xs=xs, vc=vc, tb=tb: e.transpose(
                    PBb(tb + (vc // 8), (vc % 8) * 128, 128),
                    xn_nat[:, xs, vc * 128:(vc + 1) * 128], ident[:]),
                    waits=(w if vc == 0 else []), incs=([(sp, 1)] if vc == NV - 1 else []),
                    pw=[tb + (vc // 8)])
            tr_tag = sp.v
            em.op("vector", lambda e, tI=tI, tb=tb: e.tensor_copy(
                out=xnT[:, :, tI * 128:(tI + 1) * 128], in_=PBb(tb, 0, 2048)),
                waits=[(sp, tr_tag)], incs=[(sv, 1)], pa=[tb, tb + 1])
            trevac[tI] = sv.v
        gf.feed(6)
        if "gate" in SKIP:
            em.op("vector", lambda e, tI=tI: e.memset(gate_sb[:, tI * 3:(tI + 1) * 3], 0.3), incs=[(sv, 1)])
        w = [(sv, trevac[tI]), (sgb, wqg_ready)]
        if tI >= 1 and (tI - 1) in gate_ex:
            w.append((ss, gate_ex[tI - 1]))
        if "gate" not in SKIP:
            for vc in range(NV):
                em.op("tensor", lambda e, tI=tI, vc=vc: e.matmul(
                    PB(4, 3), xnT[:, vc, tI * 128:(tI + 1) * 128], wg_bf[:, vc, :],
                    start=(vc == 0), stop=(vc == NV - 1)),
                    waits=(w if vc == 0 else []), incs=([(sp, 1)] if vc == NV - 1 else []),
                    pw=[4])
            em.op("vector", lambda e: e.reduce_max(out=mx_c, in_=PB(4, 3), axis=AX), waits=[(sp, sp.v)],
                  incs=[(sv, 1)], pa=[4])
            em.op("vector", lambda e: e.tensor_scalar(out=nmx_c, in0=mx_c, scalar1=-1.0, scalar2=None, op0=ALU.mult),
                  waits=[(sv, sv.v)], incs=[(sv, 1)])
            em.op("scalar", lambda e: e.activation(out=exs[:], in_=PB(4, 3), func=AF.Exp, bias=nmx_c),
                  waits=[(sv, sv.v)], incs=[(ss, 1)], pa=[4])
            gate_ex[tI] = ss.v
            em.op("vector", lambda e: e.reduce_sum(out=sm_c, in_=exs[:], axis=AX),
                  waits=[(ss, gate_ex[tI])], incs=[(sv, 1)])
            em.op("scalar", lambda e: e.activation(out=lnv_c, in_=sm_c, func=AF.Ln),
                  waits=[(sv, sv.v)], incs=[(ss, 1)])
            em.op("scalar", lambda e: e.activation(out=iv_c, in_=lnv_c, func=AF.Exp, scale=-1.0),
                  waits=[(ss, ss.v)], incs=[(ss, 1)])
            em.op("vector", lambda e, tI=tI: e.tensor_scalar(out=gate_sb[:, tI * 3:(tI + 1) * 3], in0=exs[:],
                                                             scalar1=iv_c, scalar2=None, op0=ALU.mult),
                  waits=[(ss, ss.v)], incs=[(sv, 1)])
        if tI % 4 == 3 and "qt" not in SKIP:
            tg = tI // 4
            for dc in range(ND):
                qs = qt_written % 2
                w = [(sv, trevac[tI]), (sgb, wqg_ready)]
                if qt_written >= 2:
                    w.append((sqw, 16 * (qt_written - 1)))
                if qt_written >= 1:
                    w.append((sv, qevac))
                for vc in range(NV):
                    em.op("tensor", lambda e, tg=tg, dc=dc, vc=vc: e.matmul(
                        PB(5), wq_bf[:, vc, dc * 128:(dc + 1) * 128],
                        xnT[:, vc, tg * 512:(tg + 1) * 512],
                        start=(vc == 0), stop=(vc == NV - 1)),
                        waits=(w if vc == 0 else []), incs=([(sp, 1)] if vc == NV - 1 else []),
                        pw=[5])
                em.op("vector", lambda e, qs=qs: e.tensor_copy(out=qtt[:, qs, :], in_=PB(5)),
                      waits=[(sp, sp.v)], incs=[(sv, 1)], pa=[5])
                qevac = sv.v
                em.dma("sync", sqw, lambda e, tg=tg, dc=dc, qs=qs: e.dma_start(
                    out=qT_scr[dc, :, tg * 512:(tg + 1) * 512], in_=qtt[:, qs, :]),
                    waits=[(sv, qevac)])
                qt_written += 1

    def barrier():
        if "nobar" in SKIP:
            return
        tags = ([(s, s.v) for s in (sdc, sgb, sdx, sqw, sgk, sdqt, srw, sgi, sgw,
                                    sdw, shw, sdr, sdd, sdh, sdx2, sout)]
                + [(sp, sp.v), (sv, sv.v), (ss, ss.v)])
        tags = [(s, v) for (s, v) in tags if v > 0]
        for e in ENGINES:
            if e == "tensor":
                continue
            em.op(e, lambda e2: None, waits=tags)

    barrier()
    p1.close()
    if upto == 1:
        return _finalize()

    p2 = contextlib.ExitStack()
    SB2 = lambda name, shape, dt: p2.enter_context(nc.sbuf_tensor("sb_" + name, shape, dt))
    wk_bf = SB2("wk_bf", [128, NV, D], BF16)
    wv_bf = SB2("wv_bf", [128, NV, D], BF16)
    S_sb = SB2("S_sb", [128, ND, D], F32)
    S_bf = SB2("S_bf", [128, ND, D], BF16)
    qtc = SB2("qtc", [128, 2, ND, 128], BF16)
    k_bf = SB2("k_bf", [128, D], BF16)
    kp_bf = SB2("kp_bf", [128, D], BF16)
    v_bf = SB2("v_bf", [128, D], BF16)
    kT_bf = SB2("kT_bf", [128, ND, 128], BF16)
    scT_bf = SB2("scT_bf", [128, 128], BF16)
    t1_bf = SB2("t1_bf", [128, D], BF16)
    t2_bf = SB2("t2_bf", [128, D], BF16)
    bt_bf = SB2("bt_bf", [128, 2, D], BF16)
    gd_c = SB2("gd_c", [128, 1], F32)

    pass_w_used = sp.v
    inter_hist = []
    supd = {}
    scast = {}
    gchunk = 0
    kv_evac = None
    kT_evac = sc_evac = ret_evac = a_done = 0
    for ch in range(3):
        em.dma("gpsimd", sgk, lambda e, ch=ch: e.dma_start(
            out=wk_bf[:], in_=wk_d[ch].rearrange("(c p) d -> p c d", p=128)),
            waits=[(sp, pass_w_used)])
        em.dma("gpsimd", sgk, lambda e, ch=ch: e.dma_start(
            out=wv_bf[:], in_=wv_d[ch].rearrange("(c p) d -> p c d", p=128)))
        wkv_tag = sgk.v
        em.op("vector", lambda e: e.memset(S_sb[:], 0.0), waits=[(ss, ss.v), (sp, pass_w_used)])
        em.op("vector", lambda e: e.memset(S_bf[:], 0.0), incs=[(sv, 1)])
        s_ready = sv.v
        sbf_ready = ss.v
        for I in range(NT - 1, -1, -1):
            qslot = gchunk % 2
            w = []
            if len(inter_hist) >= 2:
                w.append((sp, inter_hist[-2]))
            em.dma("sync", sdqt, lambda e, I=I, qslot=qslot: e.dma_start(
                out=qtc[:, qslot, :, :],
                in_=qT_scr[:, :, I * 128:(I + 1) * 128].transpose([1, 0, 2])), waits=w)
            q_tag = sdqt.v
            w = [(sgk, wkv_tag), (sv, s_ready)]
            if kv_evac:
                w.append((ss, kv_evac[1]))
            for vc in range(NV):
                em.op("tensor", lambda e, I=I, vc=vc: e.matmul(
                    PB(0), xnT[:, vc, I * 128:(I + 1) * 128], wk_bf[:, vc, :],
                    start=(vc == 0), stop=(vc == NV - 1)),
                    waits=(w if vc == 0 else []), incs=([(sp, 1)] if vc == NV - 1 else []),
                    pw=[0])
            k_tag = sp.v
            for vc in range(NV):
                em.op("tensor", lambda e, I=I, vc=vc: e.matmul(
                    PB(1), xnT[:, vc, I * 128:(I + 1) * 128], wv_bf[:, vc, :],
                    start=(vc == 0), stop=(vc == NV - 1)),
                    incs=([(sp, 1)] if vc == NV - 1 else []), pw=[1])
            v_tag = sp.v
            w = [(sp, k_tag)]
            if a_done:
                w.append((sp, a_done))
            em.op("scalar", lambda e: e.copy(out=k_bf[:], in_=PB(0)), waits=w, pa=[0])
            em.op("scalar", lambda e, ch=ch: e.activation(
                out=kp_bf[:], in_=PB(0), func=AF.Identity, scale=dpf[:, ch:ch + 1]), pa=[0])
            em.op("scalar", lambda e: e.copy(out=v_bf[:], in_=PB(1)),
                  waits=[(sp, v_tag)], incs=[(ss, 1)], pa=[1])
            kv_evac = ("ss", ss.v)
            w = [(ss, kv_evac[1])]
            for dc in range(ND):
                em.op("tensor", lambda e, dc=dc: e.transpose(
                    PBb(2, dc * 128, 128), k_bf[:, dc * 128:(dc + 1) * 128], ident[:]),
                    waits=(w if dc == 0 else []), incs=([(sp, 1)] if dc == ND - 1 else []), pw=[2])
            em.op("scalar", lambda e: e.copy(out=kT_bf[:], in_=PBb(2, 0, 512)),
                  waits=[(sp, sp.v)], incs=[(ss, 1)], pa=[2])
            kT_evac = ss.v
            w = [(ss, kT_evac), (sdqt, q_tag)]
            for dc in range(ND):
                em.op("tensor", lambda e, dc=dc, qslot=qslot: e.matmul(
                    PB(3, 128), kT_bf[:, dc, :], qtc[:, qslot, dc, :],
                    start=(dc == 0), stop=(dc == ND - 1)),
                    waits=(w if dc == 0 else []), incs=([(sp, 1)] if dc == ND - 1 else []), pw=[3])
            em.op("vector", lambda e, ch=ch: e.tensor_tensor(
                out=scT_bf[:], in0=PB(3, 128), in1=wmask[:, ch * 128:(ch + 1) * 128], op=ALU.mult),
                waits=[(sp, sp.v)], incs=[(sv, 1)], pa=[3])
            sc_evac = sv.v
            w = [(sv, sc_evac)]
            if ret_evac:
                w.append((sv, ret_evac))
            em.op("tensor", lambda e: e.matmul(PB(4), scT_bf[:], v_bf[:], start=True, stop=True),
                  waits=w, incs=[(sp, 1)], pw=[4])
            w = [(ss, sbf_ready), (sdqt, q_tag)]
            for dc in range(ND):
                em.op("tensor", lambda e, dc=dc, qslot=qslot: e.matmul(
                    PB(5), qtc[:, qslot, dc, :], S_bf[:, dc, :],
                    start=(dc == 0), stop=(dc == ND - 1)),
                    waits=(w if dc == 0 else []), incs=([(sp, 1)] if dc == ND - 1 else []), pw=[5])
            inter_tag = sp.v
            inter_hist.append(inter_tag)
            scast_prev = dict(scast)
            supd_prev = dict(supd)
            for dc in range(ND):
                bank = 6 + (dc % 2)
                w = []
                if dc >= 2:
                    w.append((sv, supd[dc - 2]))
                elif (dc + 2) in supd_prev:
                    w.append((sv, supd_prev[dc + 2]))
                em.op("tensor", lambda e, dc=dc, bank=bank: e.matmul(
                    PB(bank), kp_bf[:, dc * 128:(dc + 1) * 128], v_bf[:], start=True, stop=True),
                    waits=w, incs=[(sp, 1)], pw=[bank])
                a_tag = sp.v
                w = [(sp, a_tag)]
                if dc in scast_prev:
                    w.append((ss, scast_prev[dc]))
                em.op("vector", lambda e, dc=dc, bank=bank, ch=ch: e.scalar_tensor_tensor(
                    out=S_sb[:, dc, :], in0=S_sb[:, dc, :], scalar=dCc[:, ch:ch + 1],
                    in1=PB(bank), op0=ALU.mult, op1=ALU.add),
                    waits=w, incs=[(sv, 1)], pa=[bank])
                supd[dc] = sv.v
                em.op("scalar", lambda e, dc=dc: e.activation(
                    out=S_bf[:, dc, :], in_=S_sb[:, dc, :], func=AF.Identity),
                    waits=[(sv, supd[dc]), (sp, inter_tag)], incs=[(ss, 1)])
                scast[dc] = ss.v
            a_done = sp.v
            sbf_ready = ss.v
            em.op("vector", lambda e, ch=ch, I=I: e.tensor_mul(
                out=gd_c[:], in0=gate_sb[:, I * 3 + ch:I * 3 + ch + 1], in1=dpr[:, ch:ch + 1]),
                waits=[(sp, inter_tag)], incs=[(sv, 1)])
            em.op("scalar", lambda e, ch=ch, I=I: e.activation(
                out=t1_bf[:], in_=PB(4), func=AF.Identity,
                scale=gate_sb[:, I * 3 + ch:I * 3 + ch + 1]), waits=[(sp, inter_tag)], pa=[4])
            em.op("scalar", lambda e: e.activation(
                out=t2_bf[:], in_=PB(5), func=AF.Identity, scale=gd_c[:]),
                waits=[(sv, sv.v)], incs=[(ss, 1)], pa=[5])
            w = [(ss, ss.v)]
            if gchunk >= 2:
                w.append((srw, 16 * (gchunk - 1)))
            em.op("vector", lambda e, qslot=qslot: e.tensor_add(
                out=bt_bf[:, qslot, :], in0=t1_bf[:], in1=t2_bf[:]),
                waits=w, incs=[(sv, 1)])
            ret_evac = sv.v
            em.dma("sync", srw, lambda e, ch=ch, I=I, qslot=qslot: e.dma_start(
                out=ret_scr[ch, I], in_=bt_bf[:, qslot, :]), waits=[(sv, ret_evac)])
            gf.feed(22)
            gchunk += 1
        pass_w_used = sp.v

    barrier()
    p2.close()
    if upto == 2:
        return _finalize()

    p3a = contextlib.ExitStack()
    SA = lambda name, shape, dt: p3a.enter_context(nc.sbuf_tensor("sb_" + name, shape, dt))
    wg_s = SA("wg_s", [128, 2, NV, 256], BF16)
    wu_s = SA("wu_s", [128, 2, NV, 256], BF16)
    hrow = SA("hrow", [128, 2, T], BF16)
    sil = SA("sil", [128, 2, 512], F32)

    ogs = [(og * 256, min(256, NI - og * 256)) for og in range((NI + 255) // 256)]
    h_written = 0
    h_evac = {}
    og_pe_tag = {}
    for og, (obase, owid) in enumerate(ogs):
        osl = og % 2
        ou_need = (obase + owid - 1) // 1024
        assert ou_need in gf.ou_tag, f"gather feeder too slow: og={og} needs ou{ou_need}"
        w = [(sgw, gf.ou_tag[ou_need])]
        if og >= 2:
            w.append((sp, og_pe_tag[og - 2]))
        em.dma("sync", sdw, lambda e, obase=obase, owid=owid, osl=osl: e.dma_start(
            out=wg_s[:, osl, :, 0:owid], in_=wgt_scr[:, :, obase:obase + owid].transpose([1, 0, 2])),
            waits=w)
        em.dma("sync", sdw, lambda e, obase=obase, owid=owid, osl=osl: e.dma_start(
            out=wu_s[:, osl, :, 0:owid], in_=wup_scr[:, :, obase:obase + owid].transpose([1, 0, 2])))
        wt_tag = sdw.v
        for oci in range(max(1, owid // 128) if owid >= 128 else 1):
            hs = h_written % 2
            oc_g = obase // 128 + oci
            for tg in range(8):
                gf.feed(26)
                gb = tg % 2
                ub = 2 + gb
                w = [(sdw, wt_tag)]
                if h_written >= 2 and tg == 0:
                    w.append((shw, 16 * (h_written - 1)))
                if tg in h_evac and False:
                    pass
                key = tg - 2 if tg >= 2 else tg + 6
                if key in h_evac:
                    w.append((sv, h_evac[key]))
                for hc in range(NV):
                    em.op("tensor", lambda e, hc=hc, oci=oci, tg=tg, gb=gb, osl=osl: e.matmul(
                        PB(gb), wg_s[:, osl, hc, oci * 128:(oci + 1) * 128],
                        xnT[:, hc, tg * 512:(tg + 1) * 512],
                        start=(hc == 0), stop=(hc == NV - 1)),
                        waits=(w if hc == 0 else []), incs=([(sp, 1)] if hc == NV - 1 else []), pw=[gb])
                g_tag = sp.v
                for hc in range(NV):
                    em.op("tensor", lambda e, hc=hc, oci=oci, tg=tg, ub=ub, osl=osl: e.matmul(
                        PB(ub), wu_s[:, osl, hc, oci * 128:(oci + 1) * 128],
                        xnT[:, hc, tg * 512:(tg + 1) * 512],
                        start=(hc == 0), stop=(hc == NV - 1)),
                        incs=([(sp, 1)] if hc == NV - 1 else []), pw=[ub])

                u_tag = sp.v
                em.op("scalar", lambda e, gb=gb: e.activation(
                    out=sil[:, gb, :], in_=PB(gb), func=AF.Silu),
                    waits=[(sp, g_tag)], incs=[(ss, 1)], pa=[gb])
                em.op("vector", lambda e, gb=gb, ub=ub, hs=hs, tg=tg: e.tensor_tensor(
                    out=hrow[:, hs, tg * 512:(tg + 1) * 512], in0=sil[:, gb, :], in1=PB(ub),
                    op=ALU.mult),
                    waits=[(ss, ss.v), (sp, u_tag)], incs=[(sv, 1)], pa=[ub])
                h_evac[tg] = sv.v
            em.dma("sync", shw, lambda e, oc_g=oc_g, hs=hs: e.dma_start(
                out=hT_scr[oc_g], in_=hrow[:, hs, :]), waits=[(sv, sv.v)])
            h_written += 1
        og_pe_tag[og] = sp.v

    barrier()
    p3a.close()
    if upto == 4:
        return _finalize()
    mid.close()

    p3b = contextlib.ExitStack()
    SBB = lambda name, shape, dt: p3b.enter_context(nc.sbuf_tensor("sb_" + name, shape, dt))
    blT = SBB("blT", [128, ND, T], BF16)
    wo_bf = SBB("wo_bf", [128, ND, V], BF16)
    rbuf = SBB("rbuf", [128, 3, D], BF16)
    bsum = SBB("bsum", [128, 2, D], BF16)
    rsum_f = SBB("rsum_f", [128, D], F32)
    hT_t = SBB("hT_t", [128, 2, 6, 512], BF16)
    wdn_s = SBB("wdn_s", [128, 2, 6, 1024], BF16)
    g_f = SBB("g_f", [128, 4, V], BF16)
    x_t = SBB("x_t", [128, 4, V], F32)
    o_t = SBB("o_t", [128, 4, V], F32)
    sg_f = SBB("sg_f", [128, 512], F32)

    em.dma("gpsimd", sgb, lambda e: e.dma_start(
        out=wo_bf[:], in_=wo_d.rearrange("(c p) d -> p c d", p=128)))
    wo_tag = sgb.v

    bsum_done = btr_done = bl_evac = 0
    for I in range(NT):
        w = []
        if I >= 1:
            w.append((sv, bsum_done))
        for ch in range(3):
            em.dma("sync", sdr, lambda e, ch=ch, I=I: e.dma_start(
                out=rbuf[:, ch, :], in_=ret_scr[ch, I]), waits=(w if ch == 0 else []))
        r_tag = sdr.v
        bs = I % 2
        w = [(sdr, r_tag)]
        if I >= 2:
            w.append((sp, btr_done))
        em.op("vector", lambda e: e.tensor_add(out=rsum_f[:], in0=rbuf[:, 0, :], in1=rbuf[:, 1, :]),
              waits=w, incs=[(sv, 1)])
        em.op("vector", lambda e, bs=bs: e.tensor_add(out=bsum[:, bs, :], in0=rsum_f[:], in1=rbuf[:, 2, :]),
              waits=[(sv, sv.v)], incs=[(sv, 1)])
        bsum_done = sv.v
        w = [(sv, bsum_done)]
        if I >= 1:
            w.append((sv, bl_evac))
        for dc in range(ND):
            em.op("tensor", lambda e, dc=dc, bs=bs: e.transpose(
                PBb(6, dc * 128, 128), bsum[:, bs, dc * 128:(dc + 1) * 128], ident[:]),
                waits=(w if dc == 0 else []), incs=([(sp, 1)] if dc == ND - 1 else []), pw=[6])
        btr_done = sp.v
        em.op("vector", lambda e, I=I: e.tensor_copy(
            out=blT[:, :, I * 128:(I + 1) * 128], in_=PBb(6, 0, 512)),
            waits=[(sp, btr_done)], incs=[(sv, 1)], pa=[6])
        bl_evac = sv.v
        gf.feed(16)

    gf.finish()
    ngrp = (NO + 5) // 6
    wdn_used = {}
    comb_done = 0
    half_comb = 0
    for tg in range(8):
        tb = tg * 512
        for ts in range(4):
            em.dma("sync", sdx2, lambda e, tb=tb, ts=ts: e.dma_start(
                out=x_t[:, ts, :], in_=x_d[tb + ts * 128:tb + (ts + 1) * 128, :]),
                waits=([(sv, comb_done)] if (tg >= 1 and ts == 0) else []))
        x_tag = sdx2.v
        for half in range(2):
            for grp in range(ngrp):
                oc0 = grp * 6
                nsub = min(6, NO - oc0)
                dsl = grp % 2
                w = [(sgw, gf.wdn_tag[oc0 + nsub - 1])]
                if dsl in wdn_used:
                    w.append((sp, wdn_used[dsl]))
                em.dma("sync", sdd, lambda e, oc0=oc0, nsub=nsub, dsl=dsl, half=half: e.dma_start(
                    out=wdn_s[:, dsl, 0:nsub, :],
                    in_=wdn_scr[oc0:oc0 + nsub, :, half * 1024:(half + 1) * 1024].transpose([1, 0, 2])),
                    waits=w)
                d_tag = sdd.v
                w = [(sv, bl_evac)] if (tg == 0 and half == 0 and grp == 0) else []
                if dsl in wdn_used:
                    w.append((sp, wdn_used[dsl]))
                em.dma("gpsimd", sdh, lambda e, tb=tb, oc0=oc0, nsub=nsub, dsl=dsl: e.dma_start(
                    out=hT_t[:, dsl, 0:nsub, :],
                    in_=hT_scr[oc0:oc0 + nsub, :, tb:tb + 512].transpose([1, 0, 2])), waits=w)
                h_tag = sdh.v
                for sub in range(nsub):
                    oc = oc0 + sub
                    w0 = [(sdd, d_tag), (sdh, h_tag)]
                    if oc == 0 and half_comb:
                        w0.append((sv, half_comb))
                    first = (oc == 0)
                    last = (oc == NO - 1)
                    for ts in range(4):
                        for vgl in range(2):
                            em.op("tensor", lambda e, oc=oc, ts=ts, vgl=vgl, dsl=dsl, sub=sub,
                                  first=first, last=last: e.matmul(
                                PB(ts * 2 + vgl), hT_t[:, dsl, sub, ts * 128:(ts + 1) * 128],
                                wdn_s[:, dsl, sub, vgl * 512:(vgl + 1) * 512],
                                start=first, stop=last),
                                waits=(w0 if (ts == 0 and vgl == 0) else []),
                                incs=([(sp, 1)] if (ts == 3 and vgl == 1) else []),
                                pw=[ts * 2 + vgl])
                wdn_used[dsl] = sp.v
            dn_done = sp.v
            for ts in range(4):
                for vgl in range(2):
                    em.op("scalar", lambda e, ts=ts, vgl=vgl, half=half: e.activation(
                        out=g_f[:, ts, half * 1024 + vgl * 512:half * 1024 + (vgl + 1) * 512],
                        in_=PB(ts * 2 + vgl), func=AF.Silu),
                        waits=([(sp, dn_done)] if (ts == 0 and vgl == 0) else []),
                        incs=([(ss, 1)] if (ts == 3 and vgl == 1) else []), pa=[ts * 2 + vgl])
            g_done = ss.v
            for ts in range(4):
                for vgl in range(2):
                    bank = ts * 2 + vgl
                    vgg = half * 2 + vgl
                    for dc in range(ND):
                        em.op("tensor", lambda e, ts=ts, vgg=vgg, dc=dc, tb=tb, bank=bank: e.matmul(
                            PB(bank), blT[:, dc, tb + ts * 128:tb + (ts + 1) * 128],
                            wo_bf[:, dc, vgg * 512:(vgg + 1) * 512],
                            start=(dc == 0), stop=(dc == ND - 1)),
                            waits=([(ss, g_done), (sgb, wo_tag)] if (ts == 0 and vgl == 0 and dc == 0) else []),
                            incs=([(sp, 1)] if dc == ND - 1 else []), pw=[bank])
                    s_tag = sp.v
                    em.op("vector", lambda e, ts=ts, half=half, vgl=vgl, bank=bank: e.scalar_tensor_tensor(
                        out=sg_f[:], in0=PB(bank), scalar=oscc[:, 0:1],
                        in1=g_f[:, ts, half * 1024 + vgl * 512:half * 1024 + (vgl + 1) * 512],
                        op0=ALU.mult, op1=ALU.mult),
                        waits=[(sp, s_tag), (sdx2, x_tag), (sout, sout.v)], incs=[(sv, 1)], pa=[bank])
                    em.op("vector", lambda e, ts=ts, vgg=vgg: e.tensor_add(
                        out=o_t[:, ts, vgg * 512:(vgg + 1) * 512], in0=sg_f[:],
                        in1=x_t[:, ts, vgg * 512:(vgg + 1) * 512]),
                        waits=[(sv, sv.v)], incs=[(sv, 1)])
            half_comb = sv.v
        comb_done = sv.v
        for ts in range(4):
            em.dma("gpsimd", sout, lambda e, tb=tb, ts=ts: e.dma_start(
                out=out_d[tb + ts * 128:tb + (ts + 1) * 128, :], in_=o_t[:, ts, :]),
                waits=[(sv, comb_done)])

    em.op("sync", lambda e: None, waits=[(sout, sout.v)])

    em.check()
    em.psum_check()
    block = es.enter_context(nc.Block())
    em.replay(block)
    p3b.close()
    return nc, es


def kernel(L0_attn_x, L0_attn_Wq, L0_attn_Wk, L0_attn_Wv, L0_attn_Wo, L0_attn_Wg,
           L0_attn_decay_logits, L0_attn_out_scale, L0_ffn_lut, L0_ffn_gate_idx,
           L0_ffn_up_idx, L0_ffn_down_idx):
    B = L0_attn_x.shape[0]
    gidx = np.ascontiguousarray(np.asarray(L0_ffn_gate_idx, dtype=np.int32).T).reshape(NV, 128, NI)
    uidx = np.ascontiguousarray(np.asarray(L0_ffn_up_idx, dtype=np.int32).T).reshape(NV, 128, NI)
    didx = np.ascontiguousarray(np.asarray(L0_ffn_down_idx, dtype=np.int32).T).reshape(NO, 128, V)
    sl = np.arange(128, dtype=np.float32)
    expnT = (sl[:, None] - sl[None, :] - 1.0) * (sl[:, None] > sl[None, :])
    causT = (sl[:, None] > sl[None, :]).astype(np.float32)
    consts = dict(
        wq=np.ascontiguousarray(np.asarray(L0_attn_Wq, dtype=np.float32)),
        wk=np.ascontiguousarray(np.asarray(L0_attn_Wk, dtype=np.float32)),
        wv=np.ascontiguousarray(np.asarray(L0_attn_Wv, dtype=np.float32)),
        wo=np.ascontiguousarray(np.asarray(L0_attn_Wo, dtype=np.float32)),
        wg=np.ascontiguousarray(np.asarray(L0_attn_Wg, dtype=np.float32)),
        dlog=np.asarray(L0_attn_decay_logits, dtype=np.float32).reshape(1, 3),
        osc=np.asarray(L0_attn_out_scale, dtype=np.float32).reshape(1, 1),
        lut=np.asarray(L0_ffn_lut, dtype=np.float32).reshape(1, 16),
        gidx=gidx, uidx=uidx, didx=didx,
        ident=np.eye(128, dtype=bfdt),
        expnT=np.ascontiguousarray(expnT.astype(np.float32)),
        causT=causT,
        iota=np.ascontiguousarray(sl.reshape(128, 1)),
        riota=np.ascontiguousarray((127.0 - sl).reshape(128, 1)),
    )
    xs = np.asarray(L0_attn_x, dtype=np.float32)
    in_maps = [dict(consts, x=np.ascontiguousarray(xs[b])) for b in range(B)]
    nc, es = build_nc()
    try:
        res = run_bass_kernel_spmd(nc, in_maps, core_ids=list(range(B)))
    finally:
        es.close()
    return np.stack([np.asarray(res.results[b]["out"], dtype=np.float32) for b in range(B)], axis=0)


if __name__ == "__main__":
    nc, es = build_nc()
    print("build ok; instructions:", nc.n_instructions())


# version 16
# speedup vs baseline: 2.5000x; 2.5000x over previous
import contextlib
import numpy as np
import ml_dtypes
import concourse.bass as bass
import concourse.mybir as mybir
from concourse.bass_utils import run_bass_kernel_spmd

F32 = mybir.dt.float32
BF16 = mybir.dt.bfloat16
I32 = mybir.dt.int32
AF = mybir.ActivationFunctionType
ALU = mybir.AluOpType
AX = mybir.AxisListType.X
bfdt = ml_dtypes.bfloat16

T = 4096
V = 2048
D = 512
NI = 5504
NT = T // 128
NV = V // 128
ND = D // 128
NO = NI // 128
EPS = 1e-5

ENGINES = ("sync", "tensor", "vector", "scalar", "gpsimd")
SKIP = frozenset()


class Sem:
    __slots__ = ("h", "v", "name")

    def __init__(self, es, nc, name):
        self.h = es.enter_context(nc.semaphore(name))
        self.v = 0
        self.name = name


class EM:

    def __init__(self):
        self.q = {e: [] for e in ENGINES}

    def op(self, eng, fn, waits=(), incs=(), pw=(), pa=()):
        wl = [(s, int(v)) for (s, v) in waits]
        il = []
        for s, a in incs:
            s.v += int(a)
            il.append((s, int(a)))
        self.q[eng].append((fn, wl, il, tuple(pw), tuple(pa)))

    def dma(self, eng, sem, fn, waits=()):
        wl = [(sem, sem.v)] + list(waits)
        self.op(eng, fn, waits=wl, incs=[(sem, 16)])
        return sem.v

    def psum_check(self):
        eng_ix = {e: i for i, e in enumerate(ENGINES)}
        producers = {}
        for e in ENGINES:
            for i, (fn, wl, il, pw, pa) in enumerate(self.q[e]):
                for s, a in il:
                    producers.setdefault(s.name, []).append((e, i, a))
        import bisect
        cur = {}
        pos = {e: 0 for e in ENGINES}
        clocks = {e: [] for e in ENGINES}
        semhist = {}
        done = False
        while not done:
            done = True
            for e in ENGINES:
                while pos[e] < len(self.q[e]):
                    fn, wl, il, pw, pa = self.q[e][pos[e]]
                    if not all(cur.get(s.name, 0) >= v for s, v in wl):
                        break
                    vc = [0] * len(ENGINES)
                    if pos[e] > 0:
                        vc = list(clocks[e][pos[e] - 1])
                    vc[eng_ix[e]] = pos[e] + 1
                    for s, v in wl:
                        if v <= 0:
                            continue
                        hist = semhist.get(s.name, [])
                        lo, hi = 0, len(hist)
                        while lo < hi:
                            midi = (lo + hi) // 2
                            if hist[midi][0] >= v:
                                hi = midi
                            else:
                                lo = midi + 1
                        if lo < len(hist):
                            pvc = hist[lo][1]
                            vc = [max(a, b) for a, b in zip(vc, pvc)]
                        else:
                            raise RuntimeError(f"wait on {s.name}>={v} never produced")
                    clocks[e].append(vc)
                    for s, a in il:
                        cur[s.name] = cur.get(s.name, 0) + a
                        semhist.setdefault(s.name, []).append((cur[s.name], vc))
                    pos[e] += 1
                    done = False
        bank_pe = {}
        bank_va = {}
        for e in ENGINES:
            for i, (fn, wl, il, pw, pa) in enumerate(self.q[e]):
                for b in pw:
                    bank_pe.setdefault(b, []).append((e, i))
                for b in pa:
                    bank_va.setdefault(b, []).append((e, i))
        def hb(e1, i1, e2, i2):
            return clocks[e2][i2][eng_ix[e1]] >= i1 + 1
        bad = []
        for b in sorted(set(bank_pe) | set(bank_va)):
            for (e1, i1) in bank_pe.get(b, []):
                for (e2, i2) in bank_va.get(b, []):
                    if not (hb(e1, i1, e2, i2) or hb(e2, i2, e1, i1)):
                        bad.append((b, e1, i1, e2, i2))
        if bad:
            raise RuntimeError(f"PSUM collisions (bank, pe_op, other): {bad[:10]} (total {len(bad)})")

    def check(self):
        cur = {}
        pos = {e: 0 for e in ENGINES}
        progress = True
        while progress:
            progress = False
            for e in ENGINES:
                while pos[e] < len(self.q[e]):
                    fn, wl, il = self.q[e][pos[e]][:3]
                    if all(cur.get(s.name, 0) >= v for s, v in wl):
                        for s, a in il:
                            cur[s.name] = cur.get(s.name, 0) + a
                        pos[e] += 1
                        progress = True
                    else:
                        break
        stuck = {e: (len(self.q[e]) - pos[e]) for e in ENGINES if pos[e] < len(self.q[e])}
        if stuck:
            msgs = []
            for e in ENGINES:
                if pos[e] < len(self.q[e]):
                    fn, wl, il = self.q[e][pos[e]]
                    missing = [(s.name, v, cur.get(s.name, 0)) for s, v in wl
                               if cur.get(s.name, 0) < v]
                    msgs.append(f"{e}@{pos[e]}/{len(self.q[e])} needs {missing}")
            raise RuntimeError("deadlock: " + "; ".join(msgs))

    def replay(self, block):
        def mk(eng):
            ops = self.q[eng]

            def body(e):
                drain_incs = eng in ("vector", "scalar")
                for fn, wl, il, pw, pa in ops:
                    for s, v in wl:
                        if v > 0:
                            e.wait_ge(s.h, v)
                    inst = fn(e)
                    if il:
                        if inst is None:
                            inst = e.nop()
                        elif drain_incs:
                            inst = e.drain()
                        for s, a in il:
                            inst = inst.then_inc(s.h, a)
            return body
        block.sync(mk("sync"))
        block.tensor(mk("tensor"))
        block.vector(mk("vector"))
        block.scalar(mk("scalar"))
        block.gpsimd(mk("gpsimd"))




class GatherFeeder:

    def __init__(self, em, sems, bufs, lutc, jobs_gu, jobs_dn, NI):
        self.em = em
        self.sv, self.sgi, self.sgw, self.sgp = sems
        self.idxb, self.accb, self.tmpb = bufs
        self.lutc = lutc
        self.subs = []
        oun = (NI + 1023) // 1024
        for ou in range(oun):
            ob = ou * 1024
            ow = min(1024, NI - ob)
            for scr, idxd in jobs_gu:
                for hc in range(16):
                    self.subs.append((scr, idxd, hc, ob, ow, "ou", ou))
        for scr, idxd, nblk, FD in jobs_dn:
            for oc in range(nblk):
                for half in range(2):
                    self.subs.append((scr, idxd, oc, half * 1024, min(1024, FD - half * 1024),
                                      "dn", oc))
        self.n = len(self.subs)
        self.pos = 0
        self.opi = 0
        self.loaded = -1
        self.slot_tag = [None, None, None, None]
        self.acc_tag = [0, 0]
        self.ou_tag = {}
        self.wdn_tag = {}
        self._cnt = {}
        self.done = False

    def _load(self, k):
        if k >= self.n or k <= self.loaded:
            return
        scr, idxd, blk, ob, ow, kind, key = self.subs[k]
        sl = k % 4
        w = []
        if self.slot_tag[sl] is not None:
            w.append(self.slot_tag[sl])
        self.em.dma("gpsimd", self.sgi, lambda e, idxd=idxd, blk=blk, ob=ob, ow=ow, sl=sl:
                    e.dma_start(out=self.idxb[:, sl, 0:ow], in_=idxd[blk, :, ob:ob + ow]),
                    waits=w)
        self.loaded = k

    def feed(self, nops):
        em, sv, sgi, sgw = self.em, self.sv, self.sgi, self.sgw
        while nops > 0 and not self.done:
            p = self.pos
            a, b = 2 * p, 2 * p + 1
            eng = "vector"
            gcost = 4 if eng == "gpsimd" else 1
            if self.opi == 0:
                for k in (a, b, a + 2, b + 2):
                    self._load(k)
            two = b < self.n
            seq = []
            for j in range(16):
                seq.append((a, j, "ts"))
                if two:
                    seq.append((b, j, "ts"))
                seq.append((a, j, "tt"))
                if two:
                    seq.append((b, j, "tt"))
            while nops > 0 and self.opi < len(seq):
                k, j, op = seq[self.opi]
                scr, idxd, blk, ob, ow, kind, key = self.subs[k]
                sl = k % 4
                asl = k % 2
                gi_tag = 16 * (self.loaded + 1)
                if op == "ts" and j == 0:
                    w = [(sgi, 16 * (k + 1))]
                    if self.acc_tag[asl]:
                        w.append((sgw, self.acc_tag[asl]))
                    em.op(eng, lambda e, sl=sl, asl=asl, ow=ow: e.tensor_scalar(
                        out=self.accb[:, asl, 0:ow], in0=self.idxb[:, sl, 0:ow],
                        scalar1=0.0, scalar2=self.lutc[:, 0:1], op0=ALU.is_equal, op1=ALU.mult),
                        waits=w)
                elif op == "ts":
                    t = (k % 2) + (2 if eng == "gpsimd" else 0)
                    em.op(eng, lambda e, sl=sl, j=j, ow=ow, t=t: e.tensor_scalar(
                        out=self.tmpb[:, t, 0:ow], in0=self.idxb[:, sl, 0:ow],
                        scalar1=float(j), scalar2=self.lutc[:, j:j + 1], op0=ALU.is_equal, op1=ALU.mult))
                elif op == "tt" and j == 0:
                    pass
                else:
                    last = (j == 15)
                    t = (k % 2) + (2 if eng == "gpsimd" else 0)
                    em.op(eng, lambda e, asl=asl, ow=ow, t=t: e.tensor_add(
                        out=self.accb[:, asl, 0:ow], in0=self.accb[:, asl, 0:ow],
                        in1=self.tmpb[:, t, 0:ow]),
                        incs=([(sv, 1)] if (last and eng == "vector") else []))
                    if last:
                        if eng == "gpsimd":
                            em.op("gpsimd", lambda e: e.drain(), incs=[(self.sgp, 1)])
                            self.slot_tag[sl] = (self.sgp, self.sgp.v)
                            stw = [(self.sgp, self.sgp.v)]
                        else:
                            self.slot_tag[sl] = (sv, sv.v)
                            stw = [(sv, sv.v)]
                        em.dma("gpsimd", sgw, lambda e, scr=scr, blk=blk, ob=ob, ow=ow, asl=asl:
                               e.dma_start(out=scr[blk, :, ob:ob + ow], in_=self.accb[:, asl, 0:ow]),
                               waits=stw)
                        self.acc_tag[asl] = sgw.v
                        ck = (kind, key)
                        self._cnt[ck] = self._cnt.get(ck, 0) + 1
                        if kind == "ou":
                            if self._cnt[ck] == 32:
                                self.ou_tag[key] = sgw.v
                        else:
                            if self._cnt[ck] == 2:
                                self.wdn_tag[key] = sgw.v
                self.opi += 1
                if op == "tt" and j == 0:
                    continue
                nops -= gcost
            if self.opi >= len(seq):
                self.pos += 1
                self.opi = 0
                if 2 * self.pos >= self.n:
                    self.done = True

    def finish(self):
        while not self.done:
            self.feed(1 << 20)


def build_nc(upto=5, debug=False, p1_tiles=NT):
    nc = bass.Bass("TRN2", target_bir_lowering=False, detect_race_conditions=False)
    es = contextlib.ExitStack()

    x_d = nc.dram_tensor("x", [T, V], F32, kind="ExternalInput").ap()
    wq_d = nc.dram_tensor("wq", [V, D], F32, kind="ExternalInput").ap()
    wk_d = nc.dram_tensor("wk", [3, V, D], F32, kind="ExternalInput").ap()
    wv_d = nc.dram_tensor("wv", [3, V, D], F32, kind="ExternalInput").ap()
    wo_d = nc.dram_tensor("wo", [D, V], F32, kind="ExternalInput").ap()
    wg_d = nc.dram_tensor("wg", [V, 3], F32, kind="ExternalInput").ap()
    dlog_d = nc.dram_tensor("dlog", [1, 3], F32, kind="ExternalInput").ap()
    osc_d = nc.dram_tensor("osc", [1, 1], F32, kind="ExternalInput").ap()
    lut_d = nc.dram_tensor("lut", [1, 16], F32, kind="ExternalInput").ap()
    gidx_d = nc.dram_tensor("gidx", [NV, 128, NI], I32, kind="ExternalInput").ap()
    uidx_d = nc.dram_tensor("uidx", [NV, 128, NI], I32, kind="ExternalInput").ap()
    didx_d = nc.dram_tensor("didx", [NO, 128, V], I32, kind="ExternalInput").ap()
    ident_d = nc.dram_tensor("ident", [128, 128], BF16, kind="ExternalInput").ap()
    expnT_d = nc.dram_tensor("expnT", [128, 128], F32, kind="ExternalInput").ap()
    causT_d = nc.dram_tensor("causT", [128, 128], F32, kind="ExternalInput").ap()
    iota_d = nc.dram_tensor("iota", [128, 1], F32, kind="ExternalInput").ap()
    riota_d = nc.dram_tensor("riota", [128, 1], F32, kind="ExternalInput").ap()

    out_d = nc.dram_tensor("out", [T, V], F32, kind="ExternalOutput").ap()

    KSC = "ExternalOutput" if debug else "Internal"
    qT_scr = nc.dram_tensor("qT_scr", [ND, 128, T], BF16, kind=KSC).ap()
    ret_scr = nc.dram_tensor("ret_scr", [3, NT, 128, D], BF16, kind=KSC).ap()
    wgt_scr = nc.dram_tensor("wgt_scr", [NV, 128, NI], BF16, kind=KSC).ap()
    wup_scr = nc.dram_tensor("wup_scr", [NV, 128, NI], BF16, kind=KSC).ap()
    wdn_scr = nc.dram_tensor("wdn_scr", [NO, 128, V], BF16, kind=KSC).ap()
    hT_scr = nc.dram_tensor("hT_scr", [NO, 128, T], BF16, kind=KSC).ap()
    lnd_scr = nc.dram_tensor("lnd_scr", [1, 3], F32).ap()

    S = lambda n: Sem(es, nc, n)
    sp, sv, ss = S("sp"), S("sv"), S("ss")
    sy, sgp = S("sy"), S("sgp")
    sdc, sgb, sdx, sqw = S("sdc"), S("sgb"), S("sdx"), S("sqw")
    sgk, sdqt, srw = S("sgk"), S("sdqt"), S("srw")
    sgi, sgw = S("sgi"), S("sgw")
    sdw, shw = S("sdw"), S("shw")
    sdr, sdd, sdh, sdx2, sout = S("sdr"), S("sdd"), S("sdh"), S("sdx2"), S("sout")

    em = EM()

    def _finalize():
        if debug:
            xnT_dbg = nc.dram_tensor("xnT_dbg", [128, NV * T], BF16, kind="ExternalOutput").ap()
            gate_dbg = nc.dram_tensor("gate_dbg", [128, NT * 3], F32, kind="ExternalOutput").ap()
            if upto <= 4:
                for vc in range(NV):
                    em.dma("sync", sout, lambda e, vc=vc: e.dma_start(
                        out=xnT_dbg[:, vc * T:(vc + 1) * T], in_=xnT[:, vc, :]))
            em.dma("sync", sout, lambda e: e.dma_start(out=gate_dbg[:], in_=gate_sb[:]))
            em.op("sync", lambda e: None, waits=[(sout, sout.v)])
        em.check()
        em.psum_check()
        block = es.enter_context(nc.Block())
        em.replay(block)
        return nc, es

    ps_h = es.enter_context(nc.psum_tensor("ps", [128, 4096], F32))
    ps = ps_h[:]

    def PB(b, n=512):
        return ps[:, b * 512:b * 512 + n]

    psb = ps.bitcast(BF16)

    def PBb(b, off=0, n=1024):
        return psb[:, b * 1024 + off:b * 1024 + off + n]

    SBP = lambda name, shape, dt: es.enter_context(nc.sbuf_tensor("sb_" + name, shape, dt))
    ident = SBP("ident", [128, 128], BF16)
    expnT = SBP("expnT", [128, 128], F32)
    causT = SBP("causT", [128, 128], F32)
    iota = SBP("iotac", [128, 1], F32)
    riota = SBP("riotac", [128, 1], F32)
    lutc = SBP("lutc", [128, 16], F32)
    oscc = SBP("oscc", [128, 1], F32)
    lndc = SBP("lndc", [128, 3], F32)
    dpf = SBP("dpf", [128, 3], F32)
    dpr = SBP("dpr", [128, 3], F32)
    dCc = SBP("dCc", [128, 3], F32)
    wmask = SBP("wmask", [128, 3 * 128], F32)
    gate_sb = SBP("gate_sb", [128, NT * 3], F32)
    row3 = SBP("row3", [1, 3], F32)
    dec3 = SBP("dec3", [1, 3], F32)
    lnd3 = SBP("lnd3", [1, 3], F32)
    t128c = SBP("t128c", [128, 3], F32)
    dummy = SBP("dummy", [128, 4], F32)
    gf_idx = SBP("gf_idx", [128, 4, 1024], BF16)
    gf_acc = SBP("gf_acc", [128, 2, 1024], BF16)
    gf_tmp = SBP("gf_tmp", [128, 4, 1024], BF16)
    mid = contextlib.ExitStack()
    xnT = mid.enter_context(nc.sbuf_tensor("sb_xnT", [128, NV, T], BF16))

    em.dma("sync", sdc, lambda e: e.dma_start(out=ident[:], in_=ident_d[:]))
    em.dma("sync", sdc, lambda e: e.dma_start(out=expnT[:], in_=expnT_d[:]))
    em.dma("sync", sdc, lambda e: e.dma_start(out=causT[:], in_=causT_d[:]))
    em.dma("sync", sdc, lambda e: e.dma_start(out=iota[:], in_=iota_d[:]))
    em.dma("sync", sdc, lambda e: e.dma_start(out=riota[:], in_=riota_d[:]))
    em.dma("sync", sdc, lambda e: e.dma_start(out=row3[:], in_=dlog_d[:]))
    c_base = sdc.v
    if "nobcast" not in SKIP:
        em.dma("gpsimd", sgb, lambda e: e.dma_start(out=lutc[:], in_=lut_d[0:1, :].partition_broadcast(128)))
        em.dma("gpsimd", sgb, lambda e: e.dma_start(out=oscc[:], in_=osc_d[0:1, :].partition_broadcast(128)))

    em.op("vector", lambda e: e.memset(dummy[:], 0.0), incs=[(sv, 1)])
    if "nosetup" in SKIP:
        _skip_setup = True
    lnd_ready = sgb.v
    if "nosetup" not in SKIP:
     em.op("scalar", lambda e: e.activation(out=dec3[:], in_=row3[:], func=AF.Sigmoid),
          waits=[(sdc, c_base)], incs=[(ss, 1)])
     em.op("scalar", lambda e: e.activation(out=lnd3[:], in_=dec3[:], func=AF.Ln),
          waits=[(ss, ss.v)], incs=[(ss, 1)])
     em.dma("sync", sdc, lambda e: e.dma_start(out=lnd_scr[:], in_=lnd3[:]), waits=[(ss, ss.v)])
     em.dma("gpsimd", sgb, lambda e: e.dma_start(out=lndc[:], in_=lnd_scr[0:1, :].partition_broadcast(128)),
           waits=[(sdc, sdc.v)])
     lnd_ready = sgb.v
     for ch in range(3):
         em.op("scalar", lambda e, ch=ch: e.activation(out=dpf[:, ch:ch + 1], in_=iota[:], func=AF.Exp,
                                                       scale=lndc[:, ch:ch + 1]),
               waits=[(sgb, lnd_ready)], incs=[(ss, 1)])
         em.op("scalar", lambda e, ch=ch: e.activation(out=dpr[:, ch:ch + 1], in_=riota[:], func=AF.Exp,
                                                       scale=lndc[:, ch:ch + 1]),
               incs=[(ss, 1)])
         em.op("vector", lambda e, ch=ch: e.tensor_scalar(out=t128c[:, ch:ch + 1], in0=lndc[:, ch:ch + 1],
                                                          scalar1=128.0, scalar2=None, op0=ALU.mult),
               waits=[(sgb, lnd_ready)], incs=[(sv, 1)])
         em.op("scalar", lambda e, ch=ch: e.activation(out=dCc[:, ch:ch + 1], in_=t128c[:, ch:ch + 1], func=AF.Exp),
               waits=[(sv, sv.v)], incs=[(ss, 1)])
         em.op("vector", lambda e, ch=ch: e.tensor_scalar(out=PB(7, 128), in0=expnT[:],
                                                          scalar1=lndc[:, ch:ch + 1], scalar2=None, op0=ALU.mult),
               waits=[(sgb, lnd_ready), (ss, ss.v)], incs=[(sv, 1)], pa=[7])
         em.op("scalar", lambda e: e.activation(out=PB(6, 128), in_=PB(7, 128), func=AF.Exp),
               waits=[(sv, sv.v)], incs=[(ss, 1)], pa=[6, 7])
         em.op("vector", lambda e, ch=ch: e.tensor_mul(out=wmask[:, ch * 128:(ch + 1) * 128], in0=PB(6, 128),
                                                       in1=causT[:]),
               waits=[(ss, ss.v)], incs=[(sv, 1)], pa=[6])

     if upto == 0:
         em.op("sync", lambda e: None, waits=[(sdc, sdc.v)])
         em.op("gpsimd", lambda e: None, waits=[(sgb, sgb.v)])
         em.op("scalar", lambda e: None, waits=[(sv, sv.v)])
         em.op("vector", lambda e: None, waits=[(ss, ss.v)])
         em.check()
         em.psum_check()
         _blk = es.enter_context(nc.Block())
         em.replay(_blk)
         return nc, es

    gf = GatherFeeder(em, (sv, sgi, sgw, sgp), (gf_idx, gf_acc, gf_tmp), lutc,
                      [(wgt_scr, gidx_d), (wup_scr, uidx_d)], [(wdn_scr, didx_d, NO, V)], NI)

    p1 = contextlib.ExitStack()
    SB1 = lambda name, shape, dt: p1.enter_context(nc.sbuf_tensor("sb_" + name, shape, dt))
    x_f = SB1("x_f", [128, 2, V], F32)
    junk = SB1("junk", [128, V], F32)
    xn_nat = SB1("xn_nat", [128, 2, V], BF16)
    wq_bf = SB1("wq_bf", [128, NV, D], BF16)
    wg_bf = SB1("wg_bf", [128, NV, 3], BF16)
    qtt = SB1("qtt", [128, 2, D], BF16)
    exs = SB1("exs", [128, 3], F32)
    cols = SB1("cols", [128, 18], F32)
    sxa, sxb = cols[:, 0:1], cols[:, 1:2]
    sqa, sqb = cols[:, 2:3], cols[:, 3:4]
    m_c, var_c, a_c, b_c = cols[:, 4:5], cols[:, 5:6], cols[:, 6:7], cols[:, 7:8]
    lnv_c, t_c, t2_c, msq_c = cols[:, 8:9], cols[:, 9:10], cols[:, 10:11], cols[:, 11:12]
    mx_c, nmx_c, sm_c, iv_c = cols[:, 12:13], cols[:, 13:14], cols[:, 14:15], cols[:, 15:16]
    eps_c = cols[:, 16:17]
    em.op("vector", lambda e: e.memset(eps_c, EPS), incs=[(sv, 1)])

    if "wqg" not in SKIP:
        em.dma("gpsimd", sgb, lambda e: e.dma_start(
            out=wq_bf[:], in_=wq_d.rearrange("(c p) d -> p c d", p=128)))
        em.dma("gpsimd", sgb, lambda e: e.dma_start(
            out=wg_bf[:], in_=wg_d.rearrange("(c p) d -> p c d", p=128)))
    wqg_ready = sgb.v

    norm_done, stats_used, trevac, gate_ex = {}, {}, {}, {}
    if debug:
        cols_dbg = nc.dram_tensor("cols_dbg", [128, NT * 18], F32, kind="ExternalOutput").ap()
    qevac = 0
    qt_written = 0
    if "xact" in SKIP or "actonly" in SKIP:
        em.dma("sync", sdx, lambda e: e.dma_start(out=x_f[:, 0, :], in_=x_d[0:128, :]))
        em.op("scalar", lambda e: e.activation(out=xn_nat[:, 0, :], in_=x_f[:, 0, :],
                                               func=AF.Identity, scale=0.02),
              waits=[(sdx, sdx.v)], incs=[(ss, 1)])
        em.op("scalar", lambda e: e.activation(out=xn_nat[:, 1, :], in_=x_f[:, 0, :],
                                               func=AF.Identity, scale=0.02), incs=[(ss, 1)])
    for tI in range(p1_tiles):
        xs = tI % 2
        if "actonly" in SKIP:
            w = [(ss, ss.v)]
            if tI >= 2:
                w.append((sv, trevac[tI - 2]))
            em.op("scalar", lambda e, xs=xs: e.activation(out=xn_nat[:, xs, :], in_=x_f[:, 0, :],
                                                          func=AF.Identity, scale=0.02),
                  waits=w, incs=[(ss, 1)])
            norm_done[tI] = ss.v
            stats_used[tI] = sv.v
            tb = (tI % 2) * 2
            w = [(sdc, c_base), (ss, ss.v)]
            if tI >= 2:
                w.append((sv, trevac[tI - 2]))
            for vc in range(NV):
                em.op("tensor", lambda e, xs=xs, vc=vc, tb=tb: e.transpose(
                    PBb(tb + (vc // 8), (vc % 8) * 128, 128),
                    xn_nat[:, xs, vc * 128:(vc + 1) * 128], ident[:]),
                    waits=(w if vc == 0 else []), incs=([(sp, 1)] if vc == NV - 1 else []),
                    pw=[tb + (vc // 8)])
            em.op("vector", lambda e, tI=tI, tb=tb: e.tensor_copy(
                out=xnT[:, :, tI * 128:(tI + 1) * 128], in_=PBb(tb, 0, 2048)),
                waits=[(sp, sp.v)], incs=[(sv, 1)], pa=[tb, tb + 1])
            trevac[tI] = sv.v
            continue
        if "xact" in SKIP:
            norm_done[tI] = ss.v
            stats_used[tI] = sv.v
            tb = (tI % 2) * 2
            w = [(sdc, c_base), (ss, ss.v)]
            if tI >= 2:
                w.append((sv, trevac[tI - 2]))
            for vc in range(NV):
                em.op("tensor", lambda e, xs=xs, vc=vc, tb=tb: e.transpose(
                    PBb(tb + (vc // 8), (vc % 8) * 128, 128),
                    xn_nat[:, xs, vc * 128:(vc + 1) * 128], ident[:]),
                    waits=(w if vc == 0 else []), incs=([(sp, 1)] if vc == NV - 1 else []),
                    pw=[tb + (vc // 8)])
            em.op("vector", lambda e, tI=tI, tb=tb: e.tensor_copy(
                out=xnT[:, :, tI * 128:(tI + 1) * 128], in_=PBb(tb, 0, 2048)),
                waits=[(sp, sp.v)], incs=[(sv, 1)], pa=[tb, tb + 1])
            trevac[tI] = sv.v
            continue
        w = [(ss, norm_done[tI - 2])] if tI >= 2 else []
        em.dma("sync", sdx, lambda e, tI=tI, xs=xs: e.dma_start(
            out=x_f[:, xs, :], in_=x_d[tI * 128:(tI + 1) * 128, :]), waits=w)
        x_tag = sdx.v
        if "stats" in SKIP:
            em.op("scalar", lambda e, xs=xs: e.activation(out=xn_nat[:, xs, :], in_=x_f[:, xs, :],
                                                          func=AF.Identity, scale=0.02),
                  waits=[(sdx, x_tag)] + ([(sv, trevac[tI - 2])] if tI >= 2 else []), incs=[(ss, 1)])
            norm_done[tI] = ss.v
            stats_used[tI] = sv.v
        if "stats" not in SKIP:
            w = [(sdx, x_tag)] + ([(sv, stats_used[tI - 1])] if tI >= 1 else [])
            if tI >= 1:
                w.append((ss, stats_sq_done))
            em.op("vector", lambda e, xs=xs: e.reduce_sum(out=sxa, in_=x_f[:, xs, :], axis=AX),
                  waits=w)
            em.op("scalar", lambda e, xs=xs: e.activation(out=junk[:], in_=x_f[:, xs, :],
                                                          func=AF.Square),
                  waits=[(sdx, x_tag)] + ([(sv, stats_used[tI - 1])] if tI >= 1 else []),
                  incs=[(ss, 1)])
            stats_sq_done = ss.v
            em.op("vector", lambda e: e.reduce_sum(out=sqa, in_=junk[:], axis=AX),
                  waits=[(ss, stats_sq_done)])
            em.op("vector", lambda e: e.tensor_scalar(out=m_c, in0=sxa, scalar1=1.0 / V, scalar2=None, op0=ALU.mult),
                  incs=[(sv, 1)])
            em.op("vector", lambda e: e.tensor_scalar(out=t2_c, in0=sqa, scalar1=1.0 / V, scalar2=None, op0=ALU.mult))
            em.op("vector", lambda e: e.tensor_mul(out=msq_c, in0=m_c, in1=m_c),
                  waits=[(sv, sv.v)], incs=[(sv, 1)])
            em.op("vector", lambda e: e.tensor_sub(out=var_c, in0=t2_c, in1=msq_c),
                  waits=[(sv, sv.v)], incs=[(sv, 1)])
            stats_used[tI] = sv.v
            em.op("scalar", lambda e: e.activation(out=lnv_c, in_=var_c, func=AF.Ln, bias=eps_c),
                  waits=[(sv, sv.v)], incs=[(ss, 1)])
            em.op("scalar", lambda e: e.activation(out=a_c, in_=lnv_c, func=AF.Exp, scale=-0.5),
                  waits=[(ss, ss.v)], incs=[(ss, 1)])
            em.op("vector", lambda e: e.scalar_tensor_tensor(out=b_c, in0=m_c, scalar=-1.0,
                                                             in1=a_c, op0=ALU.mult, op1=ALU.mult),
                  waits=[(ss, ss.v)], incs=[(sv, 1)])
            w = [(sv, sv.v)] + ([(sv, trevac[tI - 2])] if tI >= 2 else [])
            em.op("scalar", lambda e, xs=xs: e.activation(out=xn_nat[:, xs, :], in_=x_f[:, xs, :],
                                                          func=AF.Identity, scale=a_c, bias=b_c),
                  waits=w, incs=[(ss, 1)])
            norm_done[tI] = ss.v
        if debug and "stats" not in SKIP:
            em.dma("sync", sqw, lambda e, tI=tI: e.dma_start(
                out=cols_dbg[:, tI * 18:(tI + 1) * 18], in_=cols[:]), waits=[(ss, norm_done[tI])])
        tb = (tI % 2) * 2
        w = [(ss, norm_done[tI]), (sdc, c_base)]
        if tI >= 2:
            w.append((sv, trevac[tI - 2]))
        if "tr" in SKIP:
            em.op("vector", lambda e: e.tensor_copy(out=dummy[:, 0:2], in_=dummy[:, 2:4]),
                  waits=w, incs=[(sv, 1)])
            trevac[tI] = sv.v
        else:
            for vc in range(NV):
                em.op("tensor", lambda e, xs=xs, vc=vc, tb=tb: e.transpose(
                    PBb(tb + (vc // 8), (vc % 8) * 128, 128),
                    xn_nat[:, xs, vc * 128:(vc + 1) * 128], ident[:]),
                    waits=(w if vc == 0 else []), incs=([(sp, 1)] if vc == NV - 1 else []),
                    pw=[tb + (vc // 8)])
            tr_tag = sp.v
            em.op("vector", lambda e, tI=tI, tb=tb: e.tensor_copy(
                out=xnT[:, :, tI * 128:(tI + 1) * 128], in_=PBb(tb, 0, 2048)),
                waits=[(sp, tr_tag)], incs=[(sv, 1)], pa=[tb, tb + 1])
            trevac[tI] = sv.v
        gf.feed(6)
        if "gate" in SKIP:
            em.op("vector", lambda e, tI=tI: e.memset(gate_sb[:, tI * 3:(tI + 1) * 3], 0.3), incs=[(sv, 1)])
        w = [(sv, trevac[tI]), (sgb, wqg_ready)]
        if tI >= 1 and (tI - 1) in gate_ex:
            w.append((ss, gate_ex[tI - 1]))
        if "gate" not in SKIP:
            for vc in range(NV):
                em.op("tensor", lambda e, tI=tI, vc=vc: e.matmul(
                    PB(4, 3), xnT[:, vc, tI * 128:(tI + 1) * 128], wg_bf[:, vc, :],
                    start=(vc == 0), stop=(vc == NV - 1)),
                    waits=(w if vc == 0 else []), incs=([(sp, 1)] if vc == NV - 1 else []),
                    pw=[4])
            em.op("vector", lambda e: e.reduce_max(out=mx_c, in_=PB(4, 3), axis=AX), waits=[(sp, sp.v)],
                  incs=[(sv, 1)], pa=[4])
            em.op("vector", lambda e: e.tensor_scalar(out=nmx_c, in0=mx_c, scalar1=-1.0, scalar2=None, op0=ALU.mult),
                  waits=[(sv, sv.v)], incs=[(sv, 1)])
            em.op("scalar", lambda e: e.activation(out=exs[:], in_=PB(4, 3), func=AF.Exp, bias=nmx_c),
                  waits=[(sv, sv.v)], incs=[(ss, 1)], pa=[4])
            gate_ex[tI] = ss.v
            em.op("vector", lambda e: e.reduce_sum(out=sm_c, in_=exs[:], axis=AX),
                  waits=[(ss, gate_ex[tI])], incs=[(sv, 1)])
            em.op("scalar", lambda e: e.activation(out=lnv_c, in_=sm_c, func=AF.Ln),
                  waits=[(sv, sv.v)], incs=[(ss, 1)])
            em.op("scalar", lambda e: e.activation(out=iv_c, in_=lnv_c, func=AF.Exp, scale=-1.0),
                  waits=[(ss, ss.v)], incs=[(ss, 1)])
            em.op("vector", lambda e, tI=tI: e.tensor_scalar(out=gate_sb[:, tI * 3:(tI + 1) * 3], in0=exs[:],
                                                             scalar1=iv_c, scalar2=None, op0=ALU.mult),
                  waits=[(ss, ss.v)], incs=[(sv, 1)])
        if tI % 4 == 3 and "qt" not in SKIP:
            tg = tI // 4
            for dc in range(ND):
                qs = qt_written % 2
                w = [(sv, trevac[tI]), (sgb, wqg_ready)]
                if qt_written >= 2:
                    w.append((sqw, 16 * (qt_written - 1)))
                if qt_written >= 1:
                    w.append((sv, qevac))
                for vc in range(NV):
                    em.op("tensor", lambda e, tg=tg, dc=dc, vc=vc: e.matmul(
                        PB(5), wq_bf[:, vc, dc * 128:(dc + 1) * 128],
                        xnT[:, vc, tg * 512:(tg + 1) * 512],
                        start=(vc == 0), stop=(vc == NV - 1)),
                        waits=(w if vc == 0 else []), incs=([(sp, 1)] if vc == NV - 1 else []),
                        pw=[5])
                em.op("vector", lambda e, qs=qs: e.tensor_copy(out=qtt[:, qs, :], in_=PB(5)),
                      waits=[(sp, sp.v)], incs=[(sv, 1)], pa=[5])
                qevac = sv.v
                em.dma("sync", sqw, lambda e, tg=tg, dc=dc, qs=qs: e.dma_start(
                    out=qT_scr[dc, :, tg * 512:(tg + 1) * 512], in_=qtt[:, qs, :]),
                    waits=[(sv, qevac)])
                qt_written += 1

    def barrier():
        if "nobar" in SKIP:
            return
        tags = ([(s, s.v) for s in (sdc, sgb, sdx, sqw, sgk, sdqt, srw, sgi, sgw,
                                    sdw, shw, sdr, sdd, sdh, sdx2, sout)]
                + [(sp, sp.v), (sv, sv.v), (ss, ss.v)])
        tags = [(s, v) for (s, v) in tags if v > 0]
        for e in ENGINES:
            if e == "tensor":
                continue
            em.op(e, lambda e2: None, waits=tags)

    barrier()
    p1.close()
    if upto == 1:
        return _finalize()

    p2 = contextlib.ExitStack()
    SB2 = lambda name, shape, dt: p2.enter_context(nc.sbuf_tensor("sb_" + name, shape, dt))
    wk_bf = SB2("wk_bf", [128, NV, D], BF16)
    wv_bf = SB2("wv_bf", [128, NV, D], BF16)
    S_sb = SB2("S_sb", [128, ND, D], F32)
    S_bf = SB2("S_bf", [128, ND, D], BF16)
    qtc = SB2("qtc", [128, 2, ND, 128], BF16)
    k_bf = SB2("k_bf", [128, D], BF16)
    kp_bf = SB2("kp_bf", [128, D], BF16)
    v_bf = SB2("v_bf", [128, D], BF16)
    kT_bf = SB2("kT_bf", [128, ND, 128], BF16)
    scT_bf = SB2("scT_bf", [128, 128], BF16)
    t1_bf = SB2("t1_bf", [128, D], BF16)
    t2_bf = SB2("t2_bf", [128, D], BF16)
    bt_bf = SB2("bt_bf", [128, 2, D], BF16)
    gd_c = SB2("gd_c", [128, 1], F32)

    pass_w_used = sp.v
    inter_hist = []
    supd = {}
    scast = {}
    gchunk = 0
    kv_evac = None
    kT_evac = sc_evac = ret_evac = a_done = 0
    for ch in range(3):
        em.dma("gpsimd", sgk, lambda e, ch=ch: e.dma_start(
            out=wk_bf[:], in_=wk_d[ch].rearrange("(c p) d -> p c d", p=128)),
            waits=[(sp, pass_w_used)])
        em.dma("gpsimd", sgk, lambda e, ch=ch: e.dma_start(
            out=wv_bf[:], in_=wv_d[ch].rearrange("(c p) d -> p c d", p=128)))
        wkv_tag = sgk.v
        em.op("vector", lambda e: e.memset(S_sb[:], 0.0), waits=[(ss, ss.v), (sp, pass_w_used)])
        em.op("vector", lambda e: e.memset(S_bf[:], 0.0), incs=[(sv, 1)])
        s_ready = sv.v
        sbf_ready = ss.v
        for I in range(NT - 1, -1, -1):
            qslot = gchunk % 2
            w = []
            if len(inter_hist) >= 2:
                w.append((sp, inter_hist[-2]))
            em.dma("sync", sdqt, lambda e, I=I, qslot=qslot: e.dma_start(
                out=qtc[:, qslot, :, :],
                in_=qT_scr[:, :, I * 128:(I + 1) * 128].transpose([1, 0, 2])), waits=w)
            q_tag = sdqt.v
            w = [(sgk, wkv_tag), (sv, s_ready)]
            if kv_evac:
                w.append((ss, kv_evac[1]))
            for vc in range(NV):
                em.op("tensor", lambda e, I=I, vc=vc: e.matmul(
                    PB(0), xnT[:, vc, I * 128:(I + 1) * 128], wk_bf[:, vc, :],
                    start=(vc == 0), stop=(vc == NV - 1)),
                    waits=(w if vc == 0 else []), incs=([(sp, 1)] if vc == NV - 1 else []),
                    pw=[0])
            k_tag = sp.v
            for vc in range(NV):
                em.op("tensor", lambda e, I=I, vc=vc: e.matmul(
                    PB(1), xnT[:, vc, I * 128:(I + 1) * 128], wv_bf[:, vc, :],
                    start=(vc == 0), stop=(vc == NV - 1)),
                    incs=([(sp, 1)] if vc == NV - 1 else []), pw=[1])
            v_tag = sp.v
            w = [(sp, k_tag)]
            if a_done:
                w.append((sp, a_done))
            em.op("scalar", lambda e: e.copy(out=k_bf[:], in_=PB(0)), waits=w, pa=[0])
            em.op("scalar", lambda e, ch=ch: e.activation(
                out=kp_bf[:], in_=PB(0), func=AF.Identity, scale=dpf[:, ch:ch + 1]), pa=[0])
            em.op("scalar", lambda e: e.copy(out=v_bf[:], in_=PB(1)),
                  waits=[(sp, v_tag)], incs=[(ss, 1)], pa=[1])
            kv_evac = ("ss", ss.v)
            w = [(ss, kv_evac[1])]
            for dc in range(ND):
                em.op("tensor", lambda e, dc=dc: e.transpose(
                    PBb(2, dc * 128, 128), k_bf[:, dc * 128:(dc + 1) * 128], ident[:]),
                    waits=(w if dc == 0 else []), incs=([(sp, 1)] if dc == ND - 1 else []), pw=[2])
            em.op("scalar", lambda e: e.copy(out=kT_bf[:], in_=PBb(2, 0, 512)),
                  waits=[(sp, sp.v)], incs=[(ss, 1)], pa=[2])
            kT_evac = ss.v
            w = [(ss, kT_evac), (sdqt, q_tag)]
            for dc in range(ND):
                em.op("tensor", lambda e, dc=dc, qslot=qslot: e.matmul(
                    PB(3, 128), kT_bf[:, dc, :], qtc[:, qslot, dc, :],
                    start=(dc == 0), stop=(dc == ND - 1)),
                    waits=(w if dc == 0 else []), incs=([(sp, 1)] if dc == ND - 1 else []), pw=[3])
            em.op("vector", lambda e, ch=ch: e.tensor_tensor(
                out=scT_bf[:], in0=PB(3, 128), in1=wmask[:, ch * 128:(ch + 1) * 128], op=ALU.mult),
                waits=[(sp, sp.v)], incs=[(sv, 1)], pa=[3])
            sc_evac = sv.v
            w = [(sv, sc_evac)]
            if ret_evac:
                w.append((sv, ret_evac))
            em.op("tensor", lambda e: e.matmul(PB(4), scT_bf[:], v_bf[:], start=True, stop=True),
                  waits=w, incs=[(sp, 1)], pw=[4])
            w = [(ss, sbf_ready), (sdqt, q_tag)]
            for dc in range(ND):
                em.op("tensor", lambda e, dc=dc, qslot=qslot: e.matmul(
                    PB(5), qtc[:, qslot, dc, :], S_bf[:, dc, :],
                    start=(dc == 0), stop=(dc == ND - 1)),
                    waits=(w if dc == 0 else []), incs=([(sp, 1)] if dc == ND - 1 else []), pw=[5])
            inter_tag = sp.v
            inter_hist.append(inter_tag)
            scast_prev = dict(scast)
            supd_prev = dict(supd)
            for dc in range(ND):
                bank = 6 + (dc % 2)
                w = []
                if dc >= 2:
                    w.append((sv, supd[dc - 2]))
                elif (dc + 2) in supd_prev:
                    w.append((sv, supd_prev[dc + 2]))
                em.op("tensor", lambda e, dc=dc, bank=bank: e.matmul(
                    PB(bank), kp_bf[:, dc * 128:(dc + 1) * 128], v_bf[:], start=True, stop=True),
                    waits=w, incs=[(sp, 1)], pw=[bank])
                a_tag = sp.v
                w = [(sp, a_tag)]
                if dc in scast_prev:
                    w.append((ss, scast_prev[dc]))
                em.op("vector", lambda e, dc=dc, bank=bank, ch=ch: e.scalar_tensor_tensor(
                    out=S_sb[:, dc, :], in0=S_sb[:, dc, :], scalar=dCc[:, ch:ch + 1],
                    in1=PB(bank), op0=ALU.mult, op1=ALU.add),
                    waits=w, incs=[(sv, 1)], pa=[bank])
                supd[dc] = sv.v
                em.op("scalar", lambda e, dc=dc: e.activation(
                    out=S_bf[:, dc, :], in_=S_sb[:, dc, :], func=AF.Identity),
                    waits=[(sv, supd[dc]), (sp, inter_tag)], incs=[(ss, 1)])
                scast[dc] = ss.v
            a_done = sp.v
            sbf_ready = ss.v
            em.op("vector", lambda e, ch=ch, I=I: e.tensor_mul(
                out=gd_c[:], in0=gate_sb[:, I * 3 + ch:I * 3 + ch + 1], in1=dpr[:, ch:ch + 1]),
                waits=[(sp, inter_tag)], incs=[(sv, 1)])
            em.op("scalar", lambda e, ch=ch, I=I: e.activation(
                out=t1_bf[:], in_=PB(4), func=AF.Identity,
                scale=gate_sb[:, I * 3 + ch:I * 3 + ch + 1]), waits=[(sp, inter_tag)], pa=[4])
            em.op("scalar", lambda e: e.activation(
                out=t2_bf[:], in_=PB(5), func=AF.Identity, scale=gd_c[:]),
                waits=[(sv, sv.v)], incs=[(ss, 1)], pa=[5])
            w = [(ss, ss.v)]
            if gchunk >= 2:
                w.append((srw, 16 * (gchunk - 1)))
            em.op("vector", lambda e, qslot=qslot: e.tensor_add(
                out=bt_bf[:, qslot, :], in0=t1_bf[:], in1=t2_bf[:]),
                waits=w, incs=[(sv, 1)])
            ret_evac = sv.v
            em.dma("sync", srw, lambda e, ch=ch, I=I, qslot=qslot: e.dma_start(
                out=ret_scr[ch, I], in_=bt_bf[:, qslot, :]), waits=[(sv, ret_evac)])
            gf.feed(22)
            gchunk += 1
        pass_w_used = sp.v

    barrier()
    p2.close()
    if upto == 2:
        return _finalize()

    p3a = contextlib.ExitStack()
    SA = lambda name, shape, dt: p3a.enter_context(nc.sbuf_tensor("sb_" + name, shape, dt))
    wg_s = SA("wg_s", [128, 2, NV, 256], BF16)
    wu_s = SA("wu_s", [128, 2, NV, 256], BF16)
    hrow = SA("hrow", [128, 2, T], BF16)
    sil = SA("sil", [128, 2, 512], F32)

    ogs = [(og * 256, min(256, NI - og * 256)) for og in range((NI + 255) // 256)]
    h_written = 0
    h_evac = {}
    og_pe_tag = {}
    for og, (obase, owid) in enumerate(ogs):
        osl = og % 2
        ou_need = (obase + owid - 1) // 1024
        assert ou_need in gf.ou_tag, f"gather feeder too slow: og={og} needs ou{ou_need}"
        w = [(sgw, gf.ou_tag[ou_need])]
        if og >= 2:
            w.append((sp, og_pe_tag[og - 2]))
        em.dma("sync", sdw, lambda e, obase=obase, owid=owid, osl=osl: e.dma_start(
            out=wg_s[:, osl, :, 0:owid], in_=wgt_scr[:, :, obase:obase + owid].transpose([1, 0, 2])),
            waits=w)
        em.dma("sync", sdw, lambda e, obase=obase, owid=owid, osl=osl: e.dma_start(
            out=wu_s[:, osl, :, 0:owid], in_=wup_scr[:, :, obase:obase + owid].transpose([1, 0, 2])))
        wt_tag = sdw.v
        for oci in range(max(1, owid // 128) if owid >= 128 else 1):
            hs = h_written % 2
            oc_g = obase // 128 + oci
            for tg in range(8):
                gf.feed(26)
                gb = tg % 2
                ub = 2 + gb
                w = [(sdw, wt_tag)]
                if h_written >= 2 and tg == 0:
                    w.append((shw, 16 * (h_written - 1)))
                if tg in h_evac and False:
                    pass
                key = tg - 2 if tg >= 2 else tg + 6
                if key in h_evac:
                    w.append((sv, h_evac[key]))
                for hc in range(NV):
                    em.op("tensor", lambda e, hc=hc, oci=oci, tg=tg, gb=gb, osl=osl: e.matmul(
                        PB(gb), wg_s[:, osl, hc, oci * 128:(oci + 1) * 128],
                        xnT[:, hc, tg * 512:(tg + 1) * 512],
                        start=(hc == 0), stop=(hc == NV - 1)),
                        waits=(w if hc == 0 else []), incs=([(sp, 1)] if hc == NV - 1 else []), pw=[gb])
                g_tag = sp.v
                for hc in range(NV):
                    em.op("tensor", lambda e, hc=hc, oci=oci, tg=tg, ub=ub, osl=osl: e.matmul(
                        PB(ub), wu_s[:, osl, hc, oci * 128:(oci + 1) * 128],
                        xnT[:, hc, tg * 512:(tg + 1) * 512],
                        start=(hc == 0), stop=(hc == NV - 1)),
                        incs=([(sp, 1)] if hc == NV - 1 else []), pw=[ub])

                u_tag = sp.v
                em.op("scalar", lambda e, gb=gb: e.activation(
                    out=sil[:, gb, :], in_=PB(gb), func=AF.Silu),
                    waits=[(sp, g_tag)], incs=[(ss, 1)], pa=[gb])
                em.op("vector", lambda e, gb=gb, ub=ub, hs=hs, tg=tg: e.tensor_tensor(
                    out=hrow[:, hs, tg * 512:(tg + 1) * 512], in0=sil[:, gb, :], in1=PB(ub),
                    op=ALU.mult),
                    waits=[(ss, ss.v), (sp, u_tag)], incs=[(sv, 1)], pa=[ub])
                h_evac[tg] = sv.v
            em.dma("sync", shw, lambda e, oc_g=oc_g, hs=hs: e.dma_start(
                out=hT_scr[oc_g], in_=hrow[:, hs, :]), waits=[(sv, sv.v)])
            h_written += 1
        og_pe_tag[og] = sp.v

    barrier()
    p3a.close()
    if upto == 4:
        return _finalize()
    mid.close()

    p3b = contextlib.ExitStack()
    SBB = lambda name, shape, dt: p3b.enter_context(nc.sbuf_tensor("sb_" + name, shape, dt))
    blT = SBB("blT", [128, ND, T], BF16)
    wo_bf = SBB("wo_bf", [128, ND, V], BF16)
    rbuf = SBB("rbuf", [128, 3, D], BF16)
    bsum = SBB("bsum", [128, 2, D], BF16)
    rsum_f = SBB("rsum_f", [128, D], F32)
    hT_t = SBB("hT_t", [128, 2, 6, 512], BF16)
    wdn_s = SBB("wdn_s", [128, 2, 6, 1024], BF16)
    g_f = SBB("g_f", [128, 4, V], BF16)
    x_t = SBB("x_t", [128, 4, V], F32)
    o_t = SBB("o_t", [128, 4, V], F32)
    sg_f = SBB("sg_f", [128, 512], F32)

    em.dma("gpsimd", sgb, lambda e: e.dma_start(
        out=wo_bf[:], in_=wo_d.rearrange("(c p) d -> p c d", p=128)))
    wo_tag = sgb.v

    bsum_done = btr_done = bl_evac = 0
    for I in range(NT):
        w = []
        if I >= 1:
            w.append((sv, bsum_done))
        for ch in range(3):
            em.dma("sync", sdr, lambda e, ch=ch, I=I: e.dma_start(
                out=rbuf[:, ch, :], in_=ret_scr[ch, I]), waits=(w if ch == 0 else []))
        r_tag = sdr.v
        bs = I % 2
        w = [(sdr, r_tag)]
        if I >= 2:
            w.append((sp, btr_done))
        em.op("vector", lambda e: e.tensor_add(out=rsum_f[:], in0=rbuf[:, 0, :], in1=rbuf[:, 1, :]),
              waits=w, incs=[(sv, 1)])
        em.op("vector", lambda e, bs=bs: e.tensor_add(out=bsum[:, bs, :], in0=rsum_f[:], in1=rbuf[:, 2, :]),
              waits=[(sv, sv.v)], incs=[(sv, 1)])
        bsum_done = sv.v
        w = [(sv, bsum_done)]
        if I >= 1:
            w.append((sv, bl_evac))
        for dc in range(ND):
            em.op("tensor", lambda e, dc=dc, bs=bs: e.transpose(
                PBb(6, dc * 128, 128), bsum[:, bs, dc * 128:(dc + 1) * 128], ident[:]),
                waits=(w if dc == 0 else []), incs=([(sp, 1)] if dc == ND - 1 else []), pw=[6])
        btr_done = sp.v
        em.op("vector", lambda e, I=I: e.tensor_copy(
            out=blT[:, :, I * 128:(I + 1) * 128], in_=PBb(6, 0, 512)),
            waits=[(sp, btr_done)], incs=[(sv, 1)], pa=[6])
        bl_evac = sv.v
        gf.feed(16)

    gf.finish()
    ngrp = (NO + 5) // 6
    wdn_used = {}
    comb_done = 0
    half_comb = 0
    for tg in range(8):
        tb = tg * 512
        for ts in range(4):
            em.dma("sync", sdx2, lambda e, tb=tb, ts=ts: e.dma_start(
                out=x_t[:, ts, :], in_=x_d[tb + ts * 128:tb + (ts + 1) * 128, :]),
                waits=([(sv, comb_done)] if (tg >= 1 and ts == 0) else []))
        x_tag = sdx2.v
        for half in range(2):
            for grp in range(ngrp):
                oc0 = grp * 6
                nsub = min(6, NO - oc0)
                dsl = grp % 2
                w = [(sgw, gf.wdn_tag[oc0 + nsub - 1])]
                if dsl in wdn_used:
                    w.append((sp, wdn_used[dsl]))
                em.dma("sync", sdd, lambda e, oc0=oc0, nsub=nsub, dsl=dsl, half=half: e.dma_start(
                    out=wdn_s[:, dsl, 0:nsub, :],
                    in_=wdn_scr[oc0:oc0 + nsub, :, half * 1024:(half + 1) * 1024].transpose([1, 0, 2])),
                    waits=w)
                d_tag = sdd.v
                w = [(sv, bl_evac)] if (tg == 0 and half == 0 and grp == 0) else []
                if dsl in wdn_used:
                    w.append((sp, wdn_used[dsl]))
                em.dma("gpsimd", sdh, lambda e, tb=tb, oc0=oc0, nsub=nsub, dsl=dsl: e.dma_start(
                    out=hT_t[:, dsl, 0:nsub, :],
                    in_=hT_scr[oc0:oc0 + nsub, :, tb:tb + 512].transpose([1, 0, 2])), waits=w)
                h_tag = sdh.v
                for sub in range(nsub):
                    oc = oc0 + sub
                    w0 = [(sdd, d_tag), (sdh, h_tag)]
                    if oc == 0 and half_comb:
                        w0.append((sv, half_comb))
                    first = (oc == 0)
                    last = (oc == NO - 1)
                    for ts in range(4):
                        for vgl in range(2):
                            em.op("tensor", lambda e, oc=oc, ts=ts, vgl=vgl, dsl=dsl, sub=sub,
                                  first=first, last=last: e.matmul(
                                PB(ts * 2 + vgl), hT_t[:, dsl, sub, ts * 128:(ts + 1) * 128],
                                wdn_s[:, dsl, sub, vgl * 512:(vgl + 1) * 512],
                                start=first, stop=last),
                                waits=(w0 if (ts == 0 and vgl == 0) else []),
                                incs=([(sp, 1)] if (ts == 3 and vgl == 1) else []),
                                pw=[ts * 2 + vgl])
                wdn_used[dsl] = sp.v
            dn_done = sp.v
            for ts in range(4):
                for vgl in range(2):
                    em.op("scalar", lambda e, ts=ts, vgl=vgl, half=half: e.activation(
                        out=g_f[:, ts, half * 1024 + vgl * 512:half * 1024 + (vgl + 1) * 512],
                        in_=PB(ts * 2 + vgl), func=AF.Silu),
                        waits=([(sp, dn_done)] if (ts == 0 and vgl == 0) else []),
                        incs=([(ss, 1)] if (ts == 3 and vgl == 1) else []), pa=[ts * 2 + vgl])
            g_done = ss.v
            for ts in range(4):
                for vgl in range(2):
                    bank = ts * 2 + vgl
                    vgg = half * 2 + vgl
                    for dc in range(ND):
                        em.op("tensor", lambda e, ts=ts, vgg=vgg, dc=dc, tb=tb, bank=bank: e.matmul(
                            PB(bank), blT[:, dc, tb + ts * 128:tb + (ts + 1) * 128],
                            wo_bf[:, dc, vgg * 512:(vgg + 1) * 512],
                            start=(dc == 0), stop=(dc == ND - 1)),
                            waits=([(ss, g_done), (sgb, wo_tag)] if (ts == 0 and vgl == 0 and dc == 0) else []),
                            incs=([(sp, 1)] if dc == ND - 1 else []), pw=[bank])
                    s_tag = sp.v
                    em.op("vector", lambda e, ts=ts, half=half, vgl=vgl, bank=bank: e.scalar_tensor_tensor(
                        out=sg_f[:], in0=PB(bank), scalar=oscc[:, 0:1],
                        in1=g_f[:, ts, half * 1024 + vgl * 512:half * 1024 + (vgl + 1) * 512],
                        op0=ALU.mult, op1=ALU.mult),
                        waits=[(sp, s_tag), (sdx2, x_tag), (sout, sout.v)], incs=[(sv, 1)], pa=[bank])
                    em.op("vector", lambda e, ts=ts, vgg=vgg: e.tensor_add(
                        out=o_t[:, ts, vgg * 512:(vgg + 1) * 512], in0=sg_f[:],
                        in1=x_t[:, ts, vgg * 512:(vgg + 1) * 512]),
                        waits=[(sv, sv.v)], incs=[(sv, 1)])
            half_comb = sv.v
        comb_done = sv.v
        for ts in range(4):
            em.dma("gpsimd", sout, lambda e, tb=tb, ts=ts: e.dma_start(
                out=out_d[tb + ts * 128:tb + (ts + 1) * 128, :], in_=o_t[:, ts, :]),
                waits=[(sv, comb_done)])

    em.op("sync", lambda e: None, waits=[(sout, sout.v)])

    em.check()
    em.psum_check()
    block = es.enter_context(nc.Block())
    em.replay(block)
    p3b.close()
    return nc, es


def kernel(L0_attn_x, L0_attn_Wq, L0_attn_Wk, L0_attn_Wv, L0_attn_Wo, L0_attn_Wg,
           L0_attn_decay_logits, L0_attn_out_scale, L0_ffn_lut, L0_ffn_gate_idx,
           L0_ffn_up_idx, L0_ffn_down_idx):
    B = L0_attn_x.shape[0]
    gidx = np.ascontiguousarray(np.asarray(L0_ffn_gate_idx, dtype=np.int32).T).reshape(NV, 128, NI)
    uidx = np.ascontiguousarray(np.asarray(L0_ffn_up_idx, dtype=np.int32).T).reshape(NV, 128, NI)
    didx = np.ascontiguousarray(np.asarray(L0_ffn_down_idx, dtype=np.int32).T).reshape(NO, 128, V)
    sl = np.arange(128, dtype=np.float32)
    expnT = (sl[:, None] - sl[None, :] - 1.0) * (sl[:, None] > sl[None, :])
    causT = (sl[:, None] > sl[None, :]).astype(np.float32)
    consts = dict(
        wq=np.ascontiguousarray(np.asarray(L0_attn_Wq, dtype=np.float32)),
        wk=np.ascontiguousarray(np.asarray(L0_attn_Wk, dtype=np.float32)),
        wv=np.ascontiguousarray(np.asarray(L0_attn_Wv, dtype=np.float32)),
        wo=np.ascontiguousarray(np.asarray(L0_attn_Wo, dtype=np.float32)),
        wg=np.ascontiguousarray(np.asarray(L0_attn_Wg, dtype=np.float32)),
        dlog=np.asarray(L0_attn_decay_logits, dtype=np.float32).reshape(1, 3),
        osc=np.asarray(L0_attn_out_scale, dtype=np.float32).reshape(1, 1),
        lut=np.asarray(L0_ffn_lut, dtype=np.float32).reshape(1, 16),
        gidx=gidx, uidx=uidx, didx=didx,
        ident=np.eye(128, dtype=bfdt),
        expnT=np.ascontiguousarray(expnT.astype(np.float32)),
        causT=causT,
        iota=np.ascontiguousarray(sl.reshape(128, 1)),
        riota=np.ascontiguousarray((127.0 - sl).reshape(128, 1)),
    )
    xs = np.asarray(L0_attn_x, dtype=np.float32)
    in_maps = [dict(consts, x=np.ascontiguousarray(xs[b])) for b in range(B)]
    nc, es = build_nc()
    try:
        res = run_bass_kernel_spmd(nc, in_maps, core_ids=list(range(B)))
    finally:
        es.close()
    return np.stack([np.asarray(res.results[b]["out"], dtype=np.float32) for b in range(B)], axis=0)


if __name__ == "__main__":
    nc, es = build_nc()
    print("build ok; instructions:", nc.n_instructions())


# version 17
# speedup vs baseline: 2.6567x; 1.0627x over previous
import contextlib
import numpy as np
import ml_dtypes
import concourse.bass as bass
import concourse.mybir as mybir
from concourse.bass_utils import run_bass_kernel_spmd

F32 = mybir.dt.float32
BF16 = mybir.dt.bfloat16
I32 = mybir.dt.int32
AF = mybir.ActivationFunctionType
ALU = mybir.AluOpType
AX = mybir.AxisListType.X
bfdt = ml_dtypes.bfloat16

T = 4096
V = 2048
D = 512
NI = 5504
NT = T // 128
NV = V // 128
ND = D // 128
NO = NI // 128
EPS = 1e-5

ENGINES = ("sync", "tensor", "vector", "scalar", "gpsimd")
SKIP = frozenset()


class Sem:
    __slots__ = ("h", "v", "name")

    def __init__(self, es, nc, name):
        self.h = es.enter_context(nc.semaphore(name))
        self.v = 0
        self.name = name


class EM:

    def __init__(self):
        self.q = {e: [] for e in ENGINES}

    def op(self, eng, fn, waits=(), incs=(), pw=(), pa=()):
        wl = [(s, int(v)) for (s, v) in waits]
        il = []
        for s, a in incs:
            s.v += int(a)
            il.append((s, int(a)))
        self.q[eng].append((fn, wl, il, tuple(pw), tuple(pa)))

    def dma(self, eng, sem, fn, waits=()):
        wl = [(sem, sem.v)] + list(waits)
        self.op(eng, fn, waits=wl, incs=[(sem, 16)])
        return sem.v

    def psum_check(self):
        eng_ix = {e: i for i, e in enumerate(ENGINES)}
        producers = {}
        for e in ENGINES:
            for i, (fn, wl, il, pw, pa) in enumerate(self.q[e]):
                for s, a in il:
                    producers.setdefault(s.name, []).append((e, i, a))
        import bisect
        cur = {}
        pos = {e: 0 for e in ENGINES}
        clocks = {e: [] for e in ENGINES}
        semhist = {}
        done = False
        while not done:
            done = True
            for e in ENGINES:
                while pos[e] < len(self.q[e]):
                    fn, wl, il, pw, pa = self.q[e][pos[e]]
                    if not all(cur.get(s.name, 0) >= v for s, v in wl):
                        break
                    vc = [0] * len(ENGINES)
                    if pos[e] > 0:
                        vc = list(clocks[e][pos[e] - 1])
                    vc[eng_ix[e]] = pos[e] + 1
                    for s, v in wl:
                        if v <= 0:
                            continue
                        hist = semhist.get(s.name, [])
                        lo, hi = 0, len(hist)
                        while lo < hi:
                            midi = (lo + hi) // 2
                            if hist[midi][0] >= v:
                                hi = midi
                            else:
                                lo = midi + 1
                        if lo < len(hist):
                            pvc = hist[lo][1]
                            vc = [max(a, b) for a, b in zip(vc, pvc)]
                        else:
                            raise RuntimeError(f"wait on {s.name}>={v} never produced")
                    clocks[e].append(vc)
                    for s, a in il:
                        cur[s.name] = cur.get(s.name, 0) + a
                        semhist.setdefault(s.name, []).append((cur[s.name], vc))
                    pos[e] += 1
                    done = False
        bank_pe = {}
        bank_va = {}
        for e in ENGINES:
            for i, (fn, wl, il, pw, pa) in enumerate(self.q[e]):
                for b in pw:
                    bank_pe.setdefault(b, []).append((e, i))
                for b in pa:
                    bank_va.setdefault(b, []).append((e, i))
        def hb(e1, i1, e2, i2):
            return clocks[e2][i2][eng_ix[e1]] >= i1 + 1
        bad = []
        for b in sorted(set(bank_pe) | set(bank_va)):
            for (e1, i1) in bank_pe.get(b, []):
                for (e2, i2) in bank_va.get(b, []):
                    if not (hb(e1, i1, e2, i2) or hb(e2, i2, e1, i1)):
                        bad.append((b, e1, i1, e2, i2))
        if bad:
            raise RuntimeError(f"PSUM collisions (bank, pe_op, other): {bad[:10]} (total {len(bad)})")

    def check(self):
        cur = {}
        pos = {e: 0 for e in ENGINES}
        progress = True
        while progress:
            progress = False
            for e in ENGINES:
                while pos[e] < len(self.q[e]):
                    fn, wl, il = self.q[e][pos[e]][:3]
                    if all(cur.get(s.name, 0) >= v for s, v in wl):
                        for s, a in il:
                            cur[s.name] = cur.get(s.name, 0) + a
                        pos[e] += 1
                        progress = True
                    else:
                        break
        stuck = {e: (len(self.q[e]) - pos[e]) for e in ENGINES if pos[e] < len(self.q[e])}
        if stuck:
            msgs = []
            for e in ENGINES:
                if pos[e] < len(self.q[e]):
                    fn, wl, il = self.q[e][pos[e]]
                    missing = [(s.name, v, cur.get(s.name, 0)) for s, v in wl
                               if cur.get(s.name, 0) < v]
                    msgs.append(f"{e}@{pos[e]}/{len(self.q[e])} needs {missing}")
            raise RuntimeError("deadlock: " + "; ".join(msgs))

    def replay(self, block):
        def mk(eng):
            ops = self.q[eng]

            def body(e):
                drain_incs = eng in ("vector", "scalar")
                for fn, wl, il, pw, pa in ops:
                    for s, v in wl:
                        if v > 0:
                            e.wait_ge(s.h, v)
                    inst = fn(e)
                    if il:
                        if inst is None:
                            inst = e.nop()
                        elif drain_incs:
                            inst = e.drain()
                        for s, a in il:
                            inst = inst.then_inc(s.h, a)
            return body
        block.sync(mk("sync"))
        block.tensor(mk("tensor"))
        block.vector(mk("vector"))
        block.scalar(mk("scalar"))
        block.gpsimd(mk("gpsimd"))




class GatherFeeder:

    def __init__(self, em, sems, bufs, lutc, jobs_gu, jobs_dn, NI):
        self.em = em
        self.sv, self.sgi, self.sgw, self.sgp = sems
        self.idxb, self.accb, self.tmpb = bufs
        self.lutc = lutc
        self.subs = []
        oun = (NI + 1023) // 1024
        for ou in range(oun):
            ob = ou * 1024
            ow = min(1024, NI - ob)
            for scr, idxd in jobs_gu:
                for hc in range(16):
                    self.subs.append((scr, idxd, hc, ob, ow, "ou", ou))
        for scr, idxd, nblk, FD in jobs_dn:
            for oc in range(nblk):
                for half in range(2):
                    self.subs.append((scr, idxd, oc, half * 1024, min(1024, FD - half * 1024),
                                      "dn", oc))
        self.n = len(self.subs)
        self.pos = 0
        self.opi = 0
        self.loaded = -1
        self.slot_tag = [None, None, None, None]
        self.acc_tag = [0, 0]
        self.ou_tag = {}
        self.wdn_tag = {}
        self._cnt = {}
        self.done = False

    def _load(self, k):
        if k >= self.n or k <= self.loaded:
            return
        scr, idxd, blk, ob, ow, kind, key = self.subs[k]
        sl = k % 4
        w = []
        if self.slot_tag[sl] is not None:
            w.append(self.slot_tag[sl])
        self.em.dma("gpsimd", self.sgi, lambda e, idxd=idxd, blk=blk, ob=ob, ow=ow, sl=sl:
                    e.dma_start(out=self.idxb[:, sl, 0:ow], in_=idxd[blk, :, ob:ob + ow]),
                    waits=w)
        self.loaded = k

    def feed(self, nops):
        em, sv, sgi, sgw = self.em, self.sv, self.sgi, self.sgw
        while nops > 0 and not self.done:
            p = self.pos
            a, b = 2 * p, 2 * p + 1
            eng = "vector"
            gcost = 4 if eng == "gpsimd" else 1
            if self.opi == 0:
                for k in (a, b, a + 2, b + 2):
                    self._load(k)
            two = b < self.n
            seq = []
            for j in range(16):
                seq.append((a, j, "ts"))
                if two:
                    seq.append((b, j, "ts"))
                seq.append((a, j, "tt"))
                if two:
                    seq.append((b, j, "tt"))
            while nops > 0 and self.opi < len(seq):
                k, j, op = seq[self.opi]
                scr, idxd, blk, ob, ow, kind, key = self.subs[k]
                sl = k % 4
                asl = k % 2
                gi_tag = 16 * (self.loaded + 1)
                if op == "ts" and j == 0:
                    w = [(sgi, 16 * (k + 1))]
                    if self.acc_tag[asl]:
                        w.append((sgw, self.acc_tag[asl]))
                    em.op(eng, lambda e, sl=sl, asl=asl, ow=ow: e.tensor_scalar(
                        out=self.accb[:, asl, 0:ow], in0=self.idxb[:, sl, 0:ow],
                        scalar1=0.0, scalar2=self.lutc[:, 0:1], op0=ALU.is_equal, op1=ALU.mult),
                        waits=w)
                elif op == "ts":
                    t = (k % 2) + (2 if eng == "gpsimd" else 0)
                    em.op(eng, lambda e, sl=sl, j=j, ow=ow, t=t: e.tensor_scalar(
                        out=self.tmpb[:, t, 0:ow], in0=self.idxb[:, sl, 0:ow],
                        scalar1=float(j), scalar2=self.lutc[:, j:j + 1], op0=ALU.is_equal, op1=ALU.mult))
                elif op == "tt" and j == 0:
                    pass
                else:
                    last = (j == 15)
                    t = (k % 2) + (2 if eng == "gpsimd" else 0)
                    em.op(eng, lambda e, asl=asl, ow=ow, t=t: e.tensor_add(
                        out=self.accb[:, asl, 0:ow], in0=self.accb[:, asl, 0:ow],
                        in1=self.tmpb[:, t, 0:ow]),
                        incs=([(sv, 1)] if (last and eng == "vector") else []))
                    if last:
                        if eng == "gpsimd":
                            em.op("gpsimd", lambda e: e.drain(), incs=[(self.sgp, 1)])
                            self.slot_tag[sl] = (self.sgp, self.sgp.v)
                            stw = [(self.sgp, self.sgp.v)]
                        else:
                            self.slot_tag[sl] = (sv, sv.v)
                            stw = [(sv, sv.v)]
                        em.dma("gpsimd", sgw, lambda e, scr=scr, blk=blk, ob=ob, ow=ow, asl=asl:
                               e.dma_start(out=scr[blk, :, ob:ob + ow], in_=self.accb[:, asl, 0:ow]),
                               waits=stw)
                        self.acc_tag[asl] = sgw.v
                        ck = (kind, key)
                        self._cnt[ck] = self._cnt.get(ck, 0) + 1
                        if kind == "ou":
                            if self._cnt[ck] == 32:
                                self.ou_tag[key] = sgw.v
                        else:
                            if self._cnt[ck] == 2:
                                self.wdn_tag[key] = sgw.v
                self.opi += 1
                if op == "tt" and j == 0:
                    continue
                nops -= gcost
            if self.opi >= len(seq):
                self.pos += 1
                self.opi = 0
                if 2 * self.pos >= self.n:
                    self.done = True

    def finish(self):
        while not self.done:
            self.feed(1 << 20)


def build_nc(upto=5, debug=False, p1_tiles=NT):
    nc = bass.Bass("TRN2", target_bir_lowering=False, detect_race_conditions=False)
    es = contextlib.ExitStack()

    x_d = nc.dram_tensor("x", [T, V], F32, kind="ExternalInput").ap()
    wq_d = nc.dram_tensor("wq", [V, D], F32, kind="ExternalInput").ap()
    wk_d = nc.dram_tensor("wk", [3, V, D], F32, kind="ExternalInput").ap()
    wv_d = nc.dram_tensor("wv", [3, V, D], F32, kind="ExternalInput").ap()
    wo_d = nc.dram_tensor("wo", [D, V], F32, kind="ExternalInput").ap()
    wg_d = nc.dram_tensor("wg", [V, 3], F32, kind="ExternalInput").ap()
    dlog_d = nc.dram_tensor("dlog", [1, 3], F32, kind="ExternalInput").ap()
    osc_d = nc.dram_tensor("osc", [1, 1], F32, kind="ExternalInput").ap()
    lut_d = nc.dram_tensor("lut", [1, 16], F32, kind="ExternalInput").ap()
    gidx_d = nc.dram_tensor("gidx", [NV, 128, NI], I32, kind="ExternalInput").ap()
    uidx_d = nc.dram_tensor("uidx", [NV, 128, NI], I32, kind="ExternalInput").ap()
    didx_d = nc.dram_tensor("didx", [NO, 128, V], I32, kind="ExternalInput").ap()
    ident_d = nc.dram_tensor("ident", [128, 128], BF16, kind="ExternalInput").ap()
    expnT_d = nc.dram_tensor("expnT", [128, 128], F32, kind="ExternalInput").ap()
    causT_d = nc.dram_tensor("causT", [128, 128], F32, kind="ExternalInput").ap()
    iota_d = nc.dram_tensor("iota", [128, 1], F32, kind="ExternalInput").ap()
    riota_d = nc.dram_tensor("riota", [128, 1], F32, kind="ExternalInput").ap()

    out_d = nc.dram_tensor("out", [T, V], F32, kind="ExternalOutput").ap()

    KSC = "ExternalOutput" if debug else "Internal"
    qT_scr = nc.dram_tensor("qT_scr", [ND, 128, T], BF16, kind=KSC).ap()
    ret_scr = nc.dram_tensor("ret_scr", [3, NT, 128, D], BF16, kind=KSC).ap()
    wgt_scr = nc.dram_tensor("wgt_scr", [NV, 128, NI], BF16, kind=KSC).ap()
    wup_scr = nc.dram_tensor("wup_scr", [NV, 128, NI], BF16, kind=KSC).ap()
    wdn_scr = nc.dram_tensor("wdn_scr", [NO, 128, V], BF16, kind=KSC).ap()
    hT_scr = nc.dram_tensor("hT_scr", [NO, 128, T], BF16, kind=KSC).ap()
    lnd_scr = nc.dram_tensor("lnd_scr", [1, 3], F32).ap()

    S = lambda n: Sem(es, nc, n)
    sp, sv, ss = S("sp"), S("sv"), S("ss")
    sy, sgp = S("sy"), S("sgp")
    sdc, sgb, sdx, sqw = S("sdc"), S("sgb"), S("sdx"), S("sqw")
    sgk, sdqt, srw = S("sgk"), S("sdqt"), S("srw")
    sgi, sgw = S("sgi"), S("sgw")
    sdw, shw = S("sdw"), S("shw")
    sdr, sdd, sdh, sdx2, sout = S("sdr"), S("sdd"), S("sdh"), S("sdx2"), S("sout")

    em = EM()

    def _finalize():
        if debug:
            xnT_dbg = nc.dram_tensor("xnT_dbg", [128, NV * T], BF16, kind="ExternalOutput").ap()
            gate_dbg = nc.dram_tensor("gate_dbg", [128, NT * 3], F32, kind="ExternalOutput").ap()
            if upto <= 4:
                for vc in range(NV):
                    em.dma("sync", sout, lambda e, vc=vc: e.dma_start(
                        out=xnT_dbg[:, vc * T:(vc + 1) * T], in_=xnT[:, vc, :]))
            em.dma("sync", sout, lambda e: e.dma_start(out=gate_dbg[:], in_=gate_sb[:]))
            em.op("sync", lambda e: None, waits=[(sout, sout.v)])
        em.check()
        em.psum_check()
        block = es.enter_context(nc.Block())
        em.replay(block)
        return nc, es

    ps_h = es.enter_context(nc.psum_tensor("ps", [128, 4096], F32))
    ps = ps_h[:]

    def PB(b, n=512):
        return ps[:, b * 512:b * 512 + n]

    psb = ps.bitcast(BF16)

    def PBb(b, off=0, n=1024):
        return psb[:, b * 1024 + off:b * 1024 + off + n]

    SBP = lambda name, shape, dt: es.enter_context(nc.sbuf_tensor("sb_" + name, shape, dt))
    ident = SBP("ident", [128, 128], BF16)
    expnT = SBP("expnT", [128, 128], F32)
    causT = SBP("causT", [128, 128], F32)
    iota = SBP("iotac", [128, 1], F32)
    riota = SBP("riotac", [128, 1], F32)
    lutc = SBP("lutc", [128, 16], F32)
    oscc = SBP("oscc", [128, 1], F32)
    lndc = SBP("lndc", [128, 3], F32)
    dpf = SBP("dpf", [128, 3], F32)
    dpr = SBP("dpr", [128, 3], F32)
    dCc = SBP("dCc", [128, 3], F32)
    wmask = SBP("wmask", [128, 3 * 128], F32)
    gate_sb = SBP("gate_sb", [128, NT * 3], F32)
    row3 = SBP("row3", [1, 3], F32)
    dec3 = SBP("dec3", [1, 3], F32)
    lnd3 = SBP("lnd3", [1, 3], F32)
    t128c = SBP("t128c", [128, 3], F32)
    dummy = SBP("dummy", [128, 4], F32)
    gf_idx = SBP("gf_idx", [128, 4, 1024], BF16)
    gf_acc = SBP("gf_acc", [128, 2, 1024], BF16)
    gf_tmp = SBP("gf_tmp", [128, 4, 1024], BF16)
    mid = contextlib.ExitStack()
    xnT = mid.enter_context(nc.sbuf_tensor("sb_xnT", [128, NV, T], BF16))

    em.dma("sync", sdc, lambda e: e.dma_start(out=ident[:], in_=ident_d[:]))
    em.dma("sync", sdc, lambda e: e.dma_start(out=expnT[:], in_=expnT_d[:]))
    em.dma("sync", sdc, lambda e: e.dma_start(out=causT[:], in_=causT_d[:]))
    em.dma("sync", sdc, lambda e: e.dma_start(out=iota[:], in_=iota_d[:]))
    em.dma("sync", sdc, lambda e: e.dma_start(out=riota[:], in_=riota_d[:]))
    em.dma("sync", sdc, lambda e: e.dma_start(out=row3[:], in_=dlog_d[:]))
    c_base = sdc.v
    if "nobcast" not in SKIP:
        em.dma("gpsimd", sgb, lambda e: e.dma_start(out=lutc[:], in_=lut_d[0:1, :].partition_broadcast(128)))
        em.dma("gpsimd", sgb, lambda e: e.dma_start(out=oscc[:], in_=osc_d[0:1, :].partition_broadcast(128)))

    em.op("vector", lambda e: e.memset(dummy[:], 0.0), incs=[(sv, 1)])
    if "nosetup" in SKIP:
        _skip_setup = True
    lnd_ready = sgb.v
    if "nosetup" not in SKIP:
     em.op("scalar", lambda e: e.activation(out=dec3[:], in_=row3[:], func=AF.Sigmoid),
          waits=[(sdc, c_base)], incs=[(ss, 1)])
     em.op("scalar", lambda e: e.activation(out=lnd3[:], in_=dec3[:], func=AF.Ln),
          waits=[(ss, ss.v)], incs=[(ss, 1)])
     em.dma("sync", sdc, lambda e: e.dma_start(out=lnd_scr[:], in_=lnd3[:]), waits=[(ss, ss.v)])
     em.dma("gpsimd", sgb, lambda e: e.dma_start(out=lndc[:], in_=lnd_scr[0:1, :].partition_broadcast(128)),
           waits=[(sdc, sdc.v)])
     lnd_ready = sgb.v
     for ch in range(3):
         em.op("scalar", lambda e, ch=ch: e.activation(out=dpf[:, ch:ch + 1], in_=iota[:], func=AF.Exp,
                                                       scale=lndc[:, ch:ch + 1]),
               waits=[(sgb, lnd_ready)], incs=[(ss, 1)])
         em.op("scalar", lambda e, ch=ch: e.activation(out=dpr[:, ch:ch + 1], in_=riota[:], func=AF.Exp,
                                                       scale=lndc[:, ch:ch + 1]),
               incs=[(ss, 1)])
         em.op("vector", lambda e, ch=ch: e.tensor_scalar(out=t128c[:, ch:ch + 1], in0=lndc[:, ch:ch + 1],
                                                          scalar1=128.0, scalar2=None, op0=ALU.mult),
               waits=[(sgb, lnd_ready)], incs=[(sv, 1)])
         em.op("scalar", lambda e, ch=ch: e.activation(out=dCc[:, ch:ch + 1], in_=t128c[:, ch:ch + 1], func=AF.Exp),
               waits=[(sv, sv.v)], incs=[(ss, 1)])
         em.op("vector", lambda e, ch=ch: e.tensor_scalar(out=PB(7, 128), in0=expnT[:],
                                                          scalar1=lndc[:, ch:ch + 1], scalar2=None, op0=ALU.mult),
               waits=[(sgb, lnd_ready), (ss, ss.v)], incs=[(sv, 1)], pa=[7])
         em.op("scalar", lambda e: e.activation(out=PB(6, 128), in_=PB(7, 128), func=AF.Exp),
               waits=[(sv, sv.v)], incs=[(ss, 1)], pa=[6, 7])
         em.op("vector", lambda e, ch=ch: e.tensor_mul(out=wmask[:, ch * 128:(ch + 1) * 128], in0=PB(6, 128),
                                                       in1=causT[:]),
               waits=[(ss, ss.v)], incs=[(sv, 1)], pa=[6])

     if upto == 0:
         em.op("sync", lambda e: None, waits=[(sdc, sdc.v)])
         em.op("gpsimd", lambda e: None, waits=[(sgb, sgb.v)])
         em.op("scalar", lambda e: None, waits=[(sv, sv.v)])
         em.op("vector", lambda e: None, waits=[(ss, ss.v)])
         em.check()
         em.psum_check()
         _blk = es.enter_context(nc.Block())
         em.replay(_blk)
         return nc, es

    gf = GatherFeeder(em, (sv, sgi, sgw, sgp), (gf_idx, gf_acc, gf_tmp), lutc,
                      [(wgt_scr, gidx_d), (wup_scr, uidx_d)], [(wdn_scr, didx_d, NO, V)], NI)

    p1 = contextlib.ExitStack()
    SB1 = lambda name, shape, dt: p1.enter_context(nc.sbuf_tensor("sb_" + name, shape, dt))
    x_f = SB1("x_f", [128, 2, V], F32)
    junk = SB1("junk", [128, V], F32)
    xn_nat = SB1("xn_nat", [128, 2, V], BF16)
    wq_bf = SB1("wq_bf", [128, NV, D], BF16)
    wg_bf = SB1("wg_bf", [128, NV, 3], BF16)
    qtt = SB1("qtt", [128, 2, D], BF16)
    exs = SB1("exs", [128, 3], F32)
    cols = SB1("cols", [128, 18], F32)
    sxa, sxb = cols[:, 0:1], cols[:, 1:2]
    sqa, sqb = cols[:, 2:3], cols[:, 3:4]
    m_c, var_c, a_c, b_c = cols[:, 4:5], cols[:, 5:6], cols[:, 6:7], cols[:, 7:8]
    lnv_c, t_c, t2_c, msq_c = cols[:, 8:9], cols[:, 9:10], cols[:, 10:11], cols[:, 11:12]
    mx_c, nmx_c, sm_c, iv_c = cols[:, 12:13], cols[:, 13:14], cols[:, 14:15], cols[:, 15:16]
    eps_c = cols[:, 16:17]
    em.op("vector", lambda e: e.memset(eps_c, EPS), incs=[(sv, 1)])

    if "wqg" not in SKIP:
        em.dma("gpsimd", sgb, lambda e: e.dma_start(
            out=wq_bf[:], in_=wq_d.rearrange("(c p) d -> p c d", p=128)))
        em.dma("gpsimd", sgb, lambda e: e.dma_start(
            out=wg_bf[:], in_=wg_d.rearrange("(c p) d -> p c d", p=128)))
    wqg_ready = sgb.v

    norm_done, stats_used, trevac, gate_ex = {}, {}, {}, {}
    if debug:
        cols_dbg = nc.dram_tensor("cols_dbg", [128, NT * 18], F32, kind="ExternalOutput").ap()
    qevac = 0
    qt_written = 0
    if "xact" in SKIP or "actonly" in SKIP:
        em.dma("sync", sdx, lambda e: e.dma_start(out=x_f[:, 0, :], in_=x_d[0:128, :]))
        em.op("scalar", lambda e: e.activation(out=xn_nat[:, 0, :], in_=x_f[:, 0, :],
                                               func=AF.Identity, scale=0.02),
              waits=[(sdx, sdx.v)], incs=[(ss, 1)])
        em.op("scalar", lambda e: e.activation(out=xn_nat[:, 1, :], in_=x_f[:, 0, :],
                                               func=AF.Identity, scale=0.02), incs=[(ss, 1)])
    for tI in range(p1_tiles):
        xs = tI % 2
        if "actonly" in SKIP:
            w = [(ss, ss.v)]
            if tI >= 2:
                w.append((sv, trevac[tI - 2]))
            em.op("scalar", lambda e, xs=xs: e.activation(out=xn_nat[:, xs, :], in_=x_f[:, 0, :],
                                                          func=AF.Identity, scale=0.02),
                  waits=w, incs=[(ss, 1)])
            norm_done[tI] = ss.v
            stats_used[tI] = sv.v
            tb = (tI % 2) * 2
            w = [(sdc, c_base), (ss, ss.v)]
            if tI >= 2:
                w.append((sv, trevac[tI - 2]))
            for vc in range(NV):
                em.op("tensor", lambda e, xs=xs, vc=vc, tb=tb: e.transpose(
                    PBb(tb + (vc // 8), (vc % 8) * 128, 128),
                    xn_nat[:, xs, vc * 128:(vc + 1) * 128], ident[:]),
                    waits=(w if vc == 0 else []), incs=([(sp, 1)] if vc == NV - 1 else []),
                    pw=[tb + (vc // 8)])
            em.op("vector", lambda e, tI=tI, tb=tb: e.tensor_copy(
                out=xnT[:, :, tI * 128:(tI + 1) * 128], in_=PBb(tb, 0, 2048)),
                waits=[(sp, sp.v)], incs=[(sv, 1)], pa=[tb, tb + 1])
            trevac[tI] = sv.v
            continue
        if "xact" in SKIP:
            norm_done[tI] = ss.v
            stats_used[tI] = sv.v
            tb = (tI % 2) * 2
            w = [(sdc, c_base), (ss, ss.v)]
            if tI >= 2:
                w.append((sv, trevac[tI - 2]))
            for vc in range(NV):
                em.op("tensor", lambda e, xs=xs, vc=vc, tb=tb: e.transpose(
                    PBb(tb + (vc // 8), (vc % 8) * 128, 128),
                    xn_nat[:, xs, vc * 128:(vc + 1) * 128], ident[:]),
                    waits=(w if vc == 0 else []), incs=([(sp, 1)] if vc == NV - 1 else []),
                    pw=[tb + (vc // 8)])
            em.op("vector", lambda e, tI=tI, tb=tb: e.tensor_copy(
                out=xnT[:, :, tI * 128:(tI + 1) * 128], in_=PBb(tb, 0, 2048)),
                waits=[(sp, sp.v)], incs=[(sv, 1)], pa=[tb, tb + 1])
            trevac[tI] = sv.v
            continue
        w = [(ss, norm_done[tI - 2])] if tI >= 2 else []
        em.dma("sync", sdx, lambda e, tI=tI, xs=xs: e.dma_start(
            out=x_f[:, xs, :], in_=x_d[tI * 128:(tI + 1) * 128, :]), waits=w)
        x_tag = sdx.v
        if "stats" in SKIP:
            em.op("scalar", lambda e, xs=xs: e.activation(out=xn_nat[:, xs, :], in_=x_f[:, xs, :],
                                                          func=AF.Identity, scale=0.02),
                  waits=[(sdx, x_tag)] + ([(sv, trevac[tI - 2])] if tI >= 2 else []), incs=[(ss, 1)])
            norm_done[tI] = ss.v
            stats_used[tI] = sv.v
        if "stats" not in SKIP:
            w = [(sdx, x_tag)] + ([(sv, stats_used[tI - 1])] if tI >= 1 else [])
            if tI >= 1:
                w.append((ss, stats_sq_done))
            em.op("vector", lambda e, xs=xs: e.reduce_sum(out=sxa, in_=x_f[:, xs, :], axis=AX),
                  waits=w)
            em.op("scalar", lambda e, xs=xs: e.activation(out=junk[:], in_=x_f[:, xs, :],
                                                          func=AF.Square),
                  waits=[(sdx, x_tag)] + ([(sv, stats_used[tI - 1])] if tI >= 1 else []),
                  incs=[(ss, 1)])
            stats_sq_done = ss.v
            em.op("vector", lambda e: e.reduce_sum(out=sqa, in_=junk[:], axis=AX),
                  waits=[(ss, stats_sq_done)])
            em.op("vector", lambda e: e.tensor_scalar(out=m_c, in0=sxa, scalar1=1.0 / V, scalar2=None, op0=ALU.mult),
                  incs=[(sv, 1)])
            em.op("vector", lambda e: e.tensor_scalar(out=t2_c, in0=sqa, scalar1=1.0 / V, scalar2=None, op0=ALU.mult))
            em.op("vector", lambda e: e.tensor_mul(out=msq_c, in0=m_c, in1=m_c),
                  waits=[(sv, sv.v)], incs=[(sv, 1)])
            em.op("vector", lambda e: e.tensor_sub(out=var_c, in0=t2_c, in1=msq_c),
                  waits=[(sv, sv.v)], incs=[(sv, 1)])
            stats_used[tI] = sv.v
            em.op("scalar", lambda e: e.activation(out=lnv_c, in_=var_c, func=AF.Ln, bias=eps_c),
                  waits=[(sv, sv.v)], incs=[(ss, 1)])
            em.op("scalar", lambda e: e.activation(out=a_c, in_=lnv_c, func=AF.Exp, scale=-0.5),
                  waits=[(ss, ss.v)], incs=[(ss, 1)])
            em.op("vector", lambda e: e.scalar_tensor_tensor(out=b_c, in0=m_c, scalar=-1.0,
                                                             in1=a_c, op0=ALU.mult, op1=ALU.mult),
                  waits=[(ss, ss.v)], incs=[(sv, 1)])
            w = [(sv, sv.v)] + ([(sv, trevac[tI - 2])] if tI >= 2 else [])
            em.op("scalar", lambda e, xs=xs: e.activation(out=xn_nat[:, xs, :], in_=x_f[:, xs, :],
                                                          func=AF.Identity, scale=a_c, bias=b_c),
                  waits=w, incs=[(ss, 1)])
            norm_done[tI] = ss.v
        if debug and "stats" not in SKIP:
            em.dma("sync", sqw, lambda e, tI=tI: e.dma_start(
                out=cols_dbg[:, tI * 18:(tI + 1) * 18], in_=cols[:]), waits=[(ss, norm_done[tI])])
        tb = (tI % 2) * 2
        w = [(ss, norm_done[tI]), (sdc, c_base)]
        if tI >= 2:
            w.append((sv, trevac[tI - 2]))
        if "tr" in SKIP:
            em.op("vector", lambda e: e.tensor_copy(out=dummy[:, 0:2], in_=dummy[:, 2:4]),
                  waits=w, incs=[(sv, 1)])
            trevac[tI] = sv.v
        else:
            for vc in range(NV):
                em.op("tensor", lambda e, xs=xs, vc=vc, tb=tb: e.transpose(
                    PBb(tb + (vc // 8), (vc % 8) * 128, 128),
                    xn_nat[:, xs, vc * 128:(vc + 1) * 128], ident[:]),
                    waits=(w if vc == 0 else []), incs=([(sp, 1)] if vc == NV - 1 else []),
                    pw=[tb + (vc // 8)])
            tr_tag = sp.v
            em.op("vector", lambda e, tI=tI, tb=tb: e.tensor_copy(
                out=xnT[:, :, tI * 128:(tI + 1) * 128], in_=PBb(tb, 0, 2048)),
                waits=[(sp, tr_tag)], incs=[(sv, 1)], pa=[tb, tb + 1])
            trevac[tI] = sv.v
        gf.feed(6)
        if "gate" in SKIP:
            em.op("vector", lambda e, tI=tI: e.memset(gate_sb[:, tI * 3:(tI + 1) * 3], 0.3), incs=[(sv, 1)])
        w = [(sv, trevac[tI]), (sgb, wqg_ready)]
        if tI >= 1 and (tI - 1) in gate_ex:
            w.append((ss, gate_ex[tI - 1]))
        if "gate" not in SKIP:
            for vc in range(NV):
                em.op("tensor", lambda e, tI=tI, vc=vc: e.matmul(
                    PB(4, 3), xnT[:, vc, tI * 128:(tI + 1) * 128], wg_bf[:, vc, :],
                    start=(vc == 0), stop=(vc == NV - 1)),
                    waits=(w if vc == 0 else []), incs=([(sp, 1)] if vc == NV - 1 else []),
                    pw=[4])
            em.op("vector", lambda e: e.reduce_max(out=mx_c, in_=PB(4, 3), axis=AX), waits=[(sp, sp.v)],
                  incs=[(sv, 1)], pa=[4])
            em.op("vector", lambda e: e.tensor_scalar(out=nmx_c, in0=mx_c, scalar1=-1.0, scalar2=None, op0=ALU.mult),
                  waits=[(sv, sv.v)], incs=[(sv, 1)])
            em.op("scalar", lambda e: e.activation(out=exs[:], in_=PB(4, 3), func=AF.Exp, bias=nmx_c),
                  waits=[(sv, sv.v)], incs=[(ss, 1)], pa=[4])
            gate_ex[tI] = ss.v
            em.op("vector", lambda e: e.reduce_sum(out=sm_c, in_=exs[:], axis=AX),
                  waits=[(ss, gate_ex[tI])], incs=[(sv, 1)])
            em.op("scalar", lambda e: e.activation(out=lnv_c, in_=sm_c, func=AF.Ln),
                  waits=[(sv, sv.v)], incs=[(ss, 1)])
            em.op("scalar", lambda e: e.activation(out=iv_c, in_=lnv_c, func=AF.Exp, scale=-1.0),
                  waits=[(ss, ss.v)], incs=[(ss, 1)])
            em.op("vector", lambda e, tI=tI: e.tensor_scalar(out=gate_sb[:, tI * 3:(tI + 1) * 3], in0=exs[:],
                                                             scalar1=iv_c, scalar2=None, op0=ALU.mult),
                  waits=[(ss, ss.v)], incs=[(sv, 1)])
        if tI % 4 == 3 and "qt" not in SKIP:
            tg = tI // 4
            for dc in range(ND):
                qs = qt_written % 2
                w = [(sv, trevac[tI]), (sgb, wqg_ready)]
                if qt_written >= 2:
                    w.append((sqw, 16 * (qt_written - 1)))
                if qt_written >= 1:
                    w.append((sv, qevac))
                for vc in range(NV):
                    em.op("tensor", lambda e, tg=tg, dc=dc, vc=vc: e.matmul(
                        PB(5), wq_bf[:, vc, dc * 128:(dc + 1) * 128],
                        xnT[:, vc, tg * 512:(tg + 1) * 512],
                        start=(vc == 0), stop=(vc == NV - 1)),
                        waits=(w if vc == 0 else []), incs=([(sp, 1)] if vc == NV - 1 else []),
                        pw=[5])
                em.op("vector", lambda e, qs=qs: e.tensor_copy(out=qtt[:, qs, :], in_=PB(5)),
                      waits=[(sp, sp.v)], incs=[(sv, 1)], pa=[5])
                qevac = sv.v
                em.dma("sync", sqw, lambda e, tg=tg, dc=dc, qs=qs: e.dma_start(
                    out=qT_scr[dc, :, tg * 512:(tg + 1) * 512], in_=qtt[:, qs, :]),
                    waits=[(sv, qevac)])
                qt_written += 1

    def barrier():
        if "nobar" in SKIP:
            return
        tags = ([(s, s.v) for s in (sdc, sgb, sdx, sqw, sgk, sdqt, srw, sgi, sgw,
                                    sdw, shw, sdr, sdd, sdh, sdx2, sout)]
                + [(sp, sp.v), (sv, sv.v), (ss, ss.v)])
        tags = [(s, v) for (s, v) in tags if v > 0]
        for e in ENGINES:
            if e == "tensor":
                continue
            em.op(e, lambda e2: None, waits=tags)

    barrier()
    p1.close()
    if upto == 1:
        return _finalize()

    p2 = contextlib.ExitStack()
    SB2 = lambda name, shape, dt: p2.enter_context(nc.sbuf_tensor("sb_" + name, shape, dt))
    wk_bf = SB2("wk_bf", [128, NV, D], BF16)
    wv_bf = SB2("wv_bf", [128, NV, D], BF16)
    S_sb = SB2("S_sb", [128, ND, D], F32)
    S_bf = SB2("S_bf", [128, ND, D], BF16)
    qtc = SB2("qtc", [128, 2, ND, 128], BF16)
    k_bf = SB2("k_bf", [128, D], BF16)
    kp_bf = SB2("kp_bf", [128, D], BF16)
    v_bf = SB2("v_bf", [128, D], BF16)
    kT_bf = SB2("kT_bf", [128, ND, 128], BF16)
    scT_bf = SB2("scT_bf", [128, 128], BF16)
    t1_bf = SB2("t1_bf", [128, D], BF16)
    t2_bf = SB2("t2_bf", [128, D], BF16)
    bt_bf = SB2("bt_bf", [128, 2, D], BF16)
    gd_c = SB2("gd_c", [128, 1], F32)

    pass_w_used = sp.v
    inter_hist = []
    supd = {}
    scast = {}
    gchunk = 0
    kv_evac = None
    kT_evac = sc_evac = ret_evac = a_done = 0
    for ch in range(3):
        em.dma("gpsimd", sgk, lambda e, ch=ch: e.dma_start(
            out=wk_bf[:], in_=wk_d[ch].rearrange("(c p) d -> p c d", p=128)),
            waits=[(sp, pass_w_used)])
        em.dma("gpsimd", sgk, lambda e, ch=ch: e.dma_start(
            out=wv_bf[:], in_=wv_d[ch].rearrange("(c p) d -> p c d", p=128)))
        wkv_tag = sgk.v
        em.op("vector", lambda e: e.memset(S_sb[:], 0.0), waits=[(ss, ss.v), (sp, pass_w_used)])
        em.op("vector", lambda e: e.memset(S_bf[:], 0.0), incs=[(sv, 1)])
        s_ready = sv.v
        sbf_ready = ss.v
        for I in range(NT - 1, -1, -1):
            qslot = gchunk % 2
            w = []
            if len(inter_hist) >= 2:
                w.append((sp, inter_hist[-2]))
            em.dma("sync", sdqt, lambda e, I=I, qslot=qslot: e.dma_start(
                out=qtc[:, qslot, :, :],
                in_=qT_scr[:, :, I * 128:(I + 1) * 128].transpose([1, 0, 2])), waits=w)
            q_tag = sdqt.v
            w = [(sgk, wkv_tag), (sv, s_ready)]
            if kv_evac:
                w.append((ss, kv_evac[1]))
            for vc in range(NV):
                em.op("tensor", lambda e, I=I, vc=vc: e.matmul(
                    PB(0), xnT[:, vc, I * 128:(I + 1) * 128], wk_bf[:, vc, :],
                    start=(vc == 0), stop=(vc == NV - 1)),
                    waits=(w if vc == 0 else []), incs=([(sp, 1)] if vc == NV - 1 else []),
                    pw=[0])
            k_tag = sp.v
            for vc in range(NV):
                em.op("tensor", lambda e, I=I, vc=vc: e.matmul(
                    PB(1), xnT[:, vc, I * 128:(I + 1) * 128], wv_bf[:, vc, :],
                    start=(vc == 0), stop=(vc == NV - 1)),
                    incs=([(sp, 1)] if vc == NV - 1 else []), pw=[1])
            v_tag = sp.v
            w = [(sp, k_tag)]
            if a_done:
                w.append((sp, a_done))
            em.op("scalar", lambda e: e.copy(out=k_bf[:], in_=PB(0)), waits=w, pa=[0])
            em.op("scalar", lambda e, ch=ch: e.activation(
                out=kp_bf[:], in_=PB(0), func=AF.Identity, scale=dpf[:, ch:ch + 1]), pa=[0])
            em.op("scalar", lambda e: e.copy(out=v_bf[:], in_=PB(1)),
                  waits=[(sp, v_tag)], incs=[(ss, 1)], pa=[1])
            kv_evac = ("ss", ss.v)
            w = [(ss, kv_evac[1])]
            for dc in range(ND):
                em.op("tensor", lambda e, dc=dc: e.transpose(
                    PBb(2, dc * 128, 128), k_bf[:, dc * 128:(dc + 1) * 128], ident[:]),
                    waits=(w if dc == 0 else []), incs=([(sp, 1)] if dc == ND - 1 else []), pw=[2])
            em.op("scalar", lambda e: e.copy(out=kT_bf[:], in_=PBb(2, 0, 512)),
                  waits=[(sp, sp.v)], incs=[(ss, 1)], pa=[2])
            kT_evac = ss.v
            w = [(ss, kT_evac), (sdqt, q_tag)]
            for dc in range(ND):
                em.op("tensor", lambda e, dc=dc, qslot=qslot: e.matmul(
                    PB(3, 128), kT_bf[:, dc, :], qtc[:, qslot, dc, :],
                    start=(dc == 0), stop=(dc == ND - 1)),
                    waits=(w if dc == 0 else []), incs=([(sp, 1)] if dc == ND - 1 else []), pw=[3])
            em.op("vector", lambda e, ch=ch: e.tensor_tensor(
                out=scT_bf[:], in0=PB(3, 128), in1=wmask[:, ch * 128:(ch + 1) * 128], op=ALU.mult),
                waits=[(sp, sp.v)], incs=[(sv, 1)], pa=[3])
            sc_evac = sv.v
            w = [(sv, sc_evac)]
            if ret_evac:
                w.append((sv, ret_evac))
            em.op("tensor", lambda e: e.matmul(PB(4), scT_bf[:], v_bf[:], start=True, stop=True),
                  waits=w, incs=[(sp, 1)], pw=[4])
            w = [(ss, sbf_ready), (sdqt, q_tag)]
            for dc in range(ND):
                em.op("tensor", lambda e, dc=dc, qslot=qslot: e.matmul(
                    PB(5), qtc[:, qslot, dc, :], S_bf[:, dc, :],
                    start=(dc == 0), stop=(dc == ND - 1)),
                    waits=(w if dc == 0 else []), incs=([(sp, 1)] if dc == ND - 1 else []), pw=[5])
            inter_tag = sp.v
            inter_hist.append(inter_tag)
            scast_prev = dict(scast)
            supd_prev = dict(supd)
            for dc in range(ND):
                bank = 6 + (dc % 2)
                w = []
                if dc >= 2:
                    w.append((sv, supd[dc - 2]))
                elif (dc + 2) in supd_prev:
                    w.append((sv, supd_prev[dc + 2]))
                em.op("tensor", lambda e, dc=dc, bank=bank: e.matmul(
                    PB(bank), kp_bf[:, dc * 128:(dc + 1) * 128], v_bf[:], start=True, stop=True),
                    waits=w, incs=[(sp, 1)], pw=[bank])
                a_tag = sp.v
                w = [(sp, a_tag)]
                if dc in scast_prev:
                    w.append((ss, scast_prev[dc]))
                em.op("vector", lambda e, dc=dc, bank=bank, ch=ch: e.scalar_tensor_tensor(
                    out=S_sb[:, dc, :], in0=S_sb[:, dc, :], scalar=dCc[:, ch:ch + 1],
                    in1=PB(bank), op0=ALU.mult, op1=ALU.add),
                    waits=w, incs=[(sv, 1)], pa=[bank])
                supd[dc] = sv.v
                em.op("scalar", lambda e, dc=dc: e.activation(
                    out=S_bf[:, dc, :], in_=S_sb[:, dc, :], func=AF.Identity),
                    waits=[(sv, supd[dc]), (sp, inter_tag)], incs=[(ss, 1)])
                scast[dc] = ss.v
            a_done = sp.v
            sbf_ready = ss.v
            em.op("vector", lambda e, ch=ch, I=I: e.tensor_mul(
                out=gd_c[:], in0=gate_sb[:, I * 3 + ch:I * 3 + ch + 1], in1=dpr[:, ch:ch + 1]),
                waits=[(sp, inter_tag)], incs=[(sv, 1)])
            em.op("scalar", lambda e, ch=ch, I=I: e.activation(
                out=t1_bf[:], in_=PB(4), func=AF.Identity,
                scale=gate_sb[:, I * 3 + ch:I * 3 + ch + 1]), waits=[(sp, inter_tag)], pa=[4])
            em.op("scalar", lambda e: e.activation(
                out=t2_bf[:], in_=PB(5), func=AF.Identity, scale=gd_c[:]),
                waits=[(sv, sv.v)], incs=[(ss, 1)], pa=[5])
            w = [(ss, ss.v)]
            if gchunk >= 2:
                w.append((srw, 16 * (gchunk - 1)))
            em.op("vector", lambda e, qslot=qslot: e.tensor_add(
                out=bt_bf[:, qslot, :], in0=t1_bf[:], in1=t2_bf[:]),
                waits=w, incs=[(sv, 1)])
            ret_evac = sv.v
            em.dma("sync", srw, lambda e, ch=ch, I=I, qslot=qslot: e.dma_start(
                out=ret_scr[ch, I], in_=bt_bf[:, qslot, :]), waits=[(sv, ret_evac)])
            gf.feed(10)
            gchunk += 1
        pass_w_used = sp.v

    barrier()
    p2.close()
    if upto == 2:
        return _finalize()

    p3a = contextlib.ExitStack()
    SA = lambda name, shape, dt: p3a.enter_context(nc.sbuf_tensor("sb_" + name, shape, dt))
    wg_s = SA("wg_s", [128, 2, NV, 256], BF16)
    wu_s = SA("wu_s", [128, 2, NV, 256], BF16)
    hrow = SA("hrow", [128, 2, T], BF16)
    sil = SA("sil", [128, 2, 512], F32)

    ogs = [(og * 256, min(256, NI - og * 256)) for og in range((NI + 255) // 256)]
    h_written = 0
    h_evac = {}
    og_pe_tag = {}
    for og, (obase, owid) in enumerate(ogs):
        osl = og % 2
        ou_need = (obase + owid - 1) // 1024
        assert ou_need in gf.ou_tag, f"gather feeder too slow: og={og} needs ou{ou_need}"
        w = [(sgw, gf.ou_tag[ou_need])]
        if og >= 2:
            w.append((sp, og_pe_tag[og - 2]))
        em.dma("sync", sdw, lambda e, obase=obase, owid=owid, osl=osl: e.dma_start(
            out=wg_s[:, osl, :, 0:owid], in_=wgt_scr[:, :, obase:obase + owid].transpose([1, 0, 2])),
            waits=w)
        em.dma("sync", sdw, lambda e, obase=obase, owid=owid, osl=osl: e.dma_start(
            out=wu_s[:, osl, :, 0:owid], in_=wup_scr[:, :, obase:obase + owid].transpose([1, 0, 2])))
        wt_tag = sdw.v
        for oci in range(max(1, owid // 128) if owid >= 128 else 1):
            hs = h_written % 2
            oc_g = obase // 128 + oci
            for tg in range(8):
                gf.feed(17)
                gb = tg % 2
                ub = 2 + gb
                w = [(sdw, wt_tag)]
                if h_written >= 2 and tg == 0:
                    w.append((shw, 16 * (h_written - 1)))
                if tg in h_evac and False:
                    pass
                key = tg - 2 if tg >= 2 else tg + 6
                if key in h_evac:
                    w.append((sv, h_evac[key]))
                for hc in range(NV):
                    em.op("tensor", lambda e, hc=hc, oci=oci, tg=tg, gb=gb, osl=osl: e.matmul(
                        PB(gb), wg_s[:, osl, hc, oci * 128:(oci + 1) * 128],
                        xnT[:, hc, tg * 512:(tg + 1) * 512],
                        start=(hc == 0), stop=(hc == NV - 1)),
                        waits=(w if hc == 0 else []), incs=([(sp, 1)] if hc == NV - 1 else []), pw=[gb])
                g_tag = sp.v
                for hc in range(NV):
                    em.op("tensor", lambda e, hc=hc, oci=oci, tg=tg, ub=ub, osl=osl: e.matmul(
                        PB(ub), wu_s[:, osl, hc, oci * 128:(oci + 1) * 128],
                        xnT[:, hc, tg * 512:(tg + 1) * 512],
                        start=(hc == 0), stop=(hc == NV - 1)),
                        incs=([(sp, 1)] if hc == NV - 1 else []), pw=[ub])

                u_tag = sp.v
                em.op("scalar", lambda e, gb=gb: e.activation(
                    out=sil[:, gb, :], in_=PB(gb), func=AF.Silu),
                    waits=[(sp, g_tag)], incs=[(ss, 1)], pa=[gb])
                em.op("vector", lambda e, gb=gb, ub=ub, hs=hs, tg=tg: e.tensor_tensor(
                    out=hrow[:, hs, tg * 512:(tg + 1) * 512], in0=sil[:, gb, :], in1=PB(ub),
                    op=ALU.mult),
                    waits=[(ss, ss.v), (sp, u_tag)], incs=[(sv, 1)], pa=[ub])
                h_evac[tg] = sv.v
            em.dma("sync", shw, lambda e, oc_g=oc_g, hs=hs: e.dma_start(
                out=hT_scr[oc_g], in_=hrow[:, hs, :]), waits=[(sv, sv.v)])
            h_written += 1
        og_pe_tag[og] = sp.v

    barrier()
    p3a.close()
    if upto == 4:
        return _finalize()
    mid.close()

    p3b = contextlib.ExitStack()
    SBB = lambda name, shape, dt: p3b.enter_context(nc.sbuf_tensor("sb_" + name, shape, dt))
    blT = SBB("blT", [128, ND, T], BF16)
    wo_bf = SBB("wo_bf", [128, ND, V], BF16)
    rbuf = SBB("rbuf", [128, 3, D], BF16)
    bsum = SBB("bsum", [128, 2, D], BF16)
    rsum_f = SBB("rsum_f", [128, D], F32)
    hT_t = SBB("hT_t", [128, 2, 6, 512], BF16)
    wdn_s = SBB("wdn_s", [128, 2, 6, 1024], BF16)
    g_f = SBB("g_f", [128, 4, V], BF16)
    x_t = SBB("x_t", [128, 4, V], F32)
    o_t = SBB("o_t", [128, 4, V], F32)
    sg_f = SBB("sg_f", [128, 512], F32)

    em.dma("gpsimd", sgb, lambda e: e.dma_start(
        out=wo_bf[:], in_=wo_d.rearrange("(c p) d -> p c d", p=128)))
    wo_tag = sgb.v

    bsum_done = btr_done = bl_evac = 0
    for I in range(NT):
        w = []
        if I >= 1:
            w.append((sv, bsum_done))
        for ch in range(3):
            em.dma("sync", sdr, lambda e, ch=ch, I=I: e.dma_start(
                out=rbuf[:, ch, :], in_=ret_scr[ch, I]), waits=(w if ch == 0 else []))
        r_tag = sdr.v
        bs = I % 2
        w = [(sdr, r_tag)]
        if I >= 2:
            w.append((sp, btr_done))
        em.op("vector", lambda e: e.tensor_add(out=rsum_f[:], in0=rbuf[:, 0, :], in1=rbuf[:, 1, :]),
              waits=w, incs=[(sv, 1)])
        em.op("vector", lambda e, bs=bs: e.tensor_add(out=bsum[:, bs, :], in0=rsum_f[:], in1=rbuf[:, 2, :]),
              waits=[(sv, sv.v)], incs=[(sv, 1)])
        bsum_done = sv.v
        w = [(sv, bsum_done)]
        if I >= 1:
            w.append((sv, bl_evac))
        for dc in range(ND):
            em.op("tensor", lambda e, dc=dc, bs=bs: e.transpose(
                PBb(6, dc * 128, 128), bsum[:, bs, dc * 128:(dc + 1) * 128], ident[:]),
                waits=(w if dc == 0 else []), incs=([(sp, 1)] if dc == ND - 1 else []), pw=[6])
        btr_done = sp.v
        em.op("vector", lambda e, I=I: e.tensor_copy(
            out=blT[:, :, I * 128:(I + 1) * 128], in_=PBb(6, 0, 512)),
            waits=[(sp, btr_done)], incs=[(sv, 1)], pa=[6])
        bl_evac = sv.v
        gf.feed(16)

    gf.finish()
    ngrp = (NO + 5) // 6
    wdn_used = {}
    comb_done = 0
    half_comb = 0
    for tg in range(8):
        tb = tg * 512
        for ts in range(4):
            em.dma("sync", sdx2, lambda e, tb=tb, ts=ts: e.dma_start(
                out=x_t[:, ts, :], in_=x_d[tb + ts * 128:tb + (ts + 1) * 128, :]),
                waits=([(sv, comb_done)] if (tg >= 1 and ts == 0) else []))
        x_tag = sdx2.v
        for half in range(2):
            for grp in range(ngrp):
                oc0 = grp * 6
                nsub = min(6, NO - oc0)
                dsl = grp % 2
                w = [(sgw, gf.wdn_tag[oc0 + nsub - 1])]
                if dsl in wdn_used:
                    w.append((sp, wdn_used[dsl]))
                em.dma("sync", sdd, lambda e, oc0=oc0, nsub=nsub, dsl=dsl, half=half: e.dma_start(
                    out=wdn_s[:, dsl, 0:nsub, :],
                    in_=wdn_scr[oc0:oc0 + nsub, :, half * 1024:(half + 1) * 1024].transpose([1, 0, 2])),
                    waits=w)
                d_tag = sdd.v
                w = [(sv, bl_evac)] if (tg == 0 and half == 0 and grp == 0) else []
                if dsl in wdn_used:
                    w.append((sp, wdn_used[dsl]))
                em.dma("gpsimd", sdh, lambda e, tb=tb, oc0=oc0, nsub=nsub, dsl=dsl: e.dma_start(
                    out=hT_t[:, dsl, 0:nsub, :],
                    in_=hT_scr[oc0:oc0 + nsub, :, tb:tb + 512].transpose([1, 0, 2])), waits=w)
                h_tag = sdh.v
                for sub in range(nsub):
                    oc = oc0 + sub
                    w0 = [(sdd, d_tag), (sdh, h_tag)]
                    if oc == 0 and half_comb:
                        w0.append((sv, half_comb))
                    first = (oc == 0)
                    last = (oc == NO - 1)
                    for ts in range(4):
                        for vgl in range(2):
                            em.op("tensor", lambda e, oc=oc, ts=ts, vgl=vgl, dsl=dsl, sub=sub,
                                  first=first, last=last: e.matmul(
                                PB(ts * 2 + vgl), hT_t[:, dsl, sub, ts * 128:(ts + 1) * 128],
                                wdn_s[:, dsl, sub, vgl * 512:(vgl + 1) * 512],
                                start=first, stop=last),
                                waits=(w0 if (ts == 0 and vgl == 0) else []),
                                incs=([(sp, 1)] if (ts == 3 and vgl == 1) else []),
                                pw=[ts * 2 + vgl])
                wdn_used[dsl] = sp.v
            dn_done = sp.v
            for ts in range(4):
                for vgl in range(2):
                    em.op("scalar", lambda e, ts=ts, vgl=vgl, half=half: e.activation(
                        out=g_f[:, ts, half * 1024 + vgl * 512:half * 1024 + (vgl + 1) * 512],
                        in_=PB(ts * 2 + vgl), func=AF.Silu),
                        waits=([(sp, dn_done)] if (ts == 0 and vgl == 0) else []),
                        incs=([(ss, 1)] if (ts == 3 and vgl == 1) else []), pa=[ts * 2 + vgl])
            g_done = ss.v
            for ts in range(4):
                for vgl in range(2):
                    bank = ts * 2 + vgl
                    vgg = half * 2 + vgl
                    for dc in range(ND):
                        em.op("tensor", lambda e, ts=ts, vgg=vgg, dc=dc, tb=tb, bank=bank: e.matmul(
                            PB(bank), blT[:, dc, tb + ts * 128:tb + (ts + 1) * 128],
                            wo_bf[:, dc, vgg * 512:(vgg + 1) * 512],
                            start=(dc == 0), stop=(dc == ND - 1)),
                            waits=([(ss, g_done), (sgb, wo_tag)] if (ts == 0 and vgl == 0 and dc == 0) else []),
                            incs=([(sp, 1)] if dc == ND - 1 else []), pw=[bank])
                    s_tag = sp.v
                    em.op("vector", lambda e, ts=ts, half=half, vgl=vgl, bank=bank: e.scalar_tensor_tensor(
                        out=sg_f[:], in0=PB(bank), scalar=oscc[:, 0:1],
                        in1=g_f[:, ts, half * 1024 + vgl * 512:half * 1024 + (vgl + 1) * 512],
                        op0=ALU.mult, op1=ALU.mult),
                        waits=[(sp, s_tag), (sdx2, x_tag), (sout, sout.v)], incs=[(sv, 1)], pa=[bank])
                    em.op("vector", lambda e, ts=ts, vgg=vgg: e.tensor_add(
                        out=o_t[:, ts, vgg * 512:(vgg + 1) * 512], in0=sg_f[:],
                        in1=x_t[:, ts, vgg * 512:(vgg + 1) * 512]),
                        waits=[(sv, sv.v)], incs=[(sv, 1)])
            half_comb = sv.v
        comb_done = sv.v
        for ts in range(4):
            em.dma("gpsimd", sout, lambda e, tb=tb, ts=ts: e.dma_start(
                out=out_d[tb + ts * 128:tb + (ts + 1) * 128, :], in_=o_t[:, ts, :]),
                waits=[(sv, comb_done)])

    em.op("sync", lambda e: None, waits=[(sout, sout.v)])

    em.check()
    em.psum_check()
    block = es.enter_context(nc.Block())
    em.replay(block)
    p3b.close()
    return nc, es


def kernel(L0_attn_x, L0_attn_Wq, L0_attn_Wk, L0_attn_Wv, L0_attn_Wo, L0_attn_Wg,
           L0_attn_decay_logits, L0_attn_out_scale, L0_ffn_lut, L0_ffn_gate_idx,
           L0_ffn_up_idx, L0_ffn_down_idx):
    B = L0_attn_x.shape[0]
    gidx = np.ascontiguousarray(np.asarray(L0_ffn_gate_idx, dtype=np.int32).T).reshape(NV, 128, NI)
    uidx = np.ascontiguousarray(np.asarray(L0_ffn_up_idx, dtype=np.int32).T).reshape(NV, 128, NI)
    didx = np.ascontiguousarray(np.asarray(L0_ffn_down_idx, dtype=np.int32).T).reshape(NO, 128, V)
    sl = np.arange(128, dtype=np.float32)
    expnT = (sl[:, None] - sl[None, :] - 1.0) * (sl[:, None] > sl[None, :])
    causT = (sl[:, None] > sl[None, :]).astype(np.float32)
    consts = dict(
        wq=np.ascontiguousarray(np.asarray(L0_attn_Wq, dtype=np.float32)),
        wk=np.ascontiguousarray(np.asarray(L0_attn_Wk, dtype=np.float32)),
        wv=np.ascontiguousarray(np.asarray(L0_attn_Wv, dtype=np.float32)),
        wo=np.ascontiguousarray(np.asarray(L0_attn_Wo, dtype=np.float32)),
        wg=np.ascontiguousarray(np.asarray(L0_attn_Wg, dtype=np.float32)),
        dlog=np.asarray(L0_attn_decay_logits, dtype=np.float32).reshape(1, 3),
        osc=np.asarray(L0_attn_out_scale, dtype=np.float32).reshape(1, 1),
        lut=np.asarray(L0_ffn_lut, dtype=np.float32).reshape(1, 16),
        gidx=gidx, uidx=uidx, didx=didx,
        ident=np.eye(128, dtype=bfdt),
        expnT=np.ascontiguousarray(expnT.astype(np.float32)),
        causT=causT,
        iota=np.ascontiguousarray(sl.reshape(128, 1)),
        riota=np.ascontiguousarray((127.0 - sl).reshape(128, 1)),
    )
    xs = np.asarray(L0_attn_x, dtype=np.float32)
    in_maps = [dict(consts, x=np.ascontiguousarray(xs[b])) for b in range(B)]
    nc, es = build_nc()
    try:
        res = run_bass_kernel_spmd(nc, in_maps, core_ids=list(range(B)))
    finally:
        es.close()
    return np.stack([np.asarray(res.results[b]["out"], dtype=np.float32) for b in range(B)], axis=0)


if __name__ == "__main__":
    nc, es = build_nc()
    print("build ok; instructions:", nc.n_instructions())


# version 18
# speedup vs baseline: 2.7058x; 1.0185x over previous
import contextlib
import numpy as np
import ml_dtypes
import concourse.bass as bass
import concourse.mybir as mybir
from concourse.bass_utils import run_bass_kernel_spmd

F32 = mybir.dt.float32
BF16 = mybir.dt.bfloat16
I32 = mybir.dt.int32
AF = mybir.ActivationFunctionType
ALU = mybir.AluOpType
AX = mybir.AxisListType.X
bfdt = ml_dtypes.bfloat16

T = 4096
V = 2048
D = 512
NI = 5504
NT = T // 128
NV = V // 128
ND = D // 128
NO = NI // 128
EPS = 1e-5

ENGINES = ("sync", "tensor", "vector", "scalar", "gpsimd")
SKIP = frozenset()


class Sem:
    __slots__ = ("h", "v", "name")

    def __init__(self, es, nc, name):
        self.h = es.enter_context(nc.semaphore(name))
        self.v = 0
        self.name = name


class EM:

    def __init__(self):
        self.q = {e: [] for e in ENGINES}

    def op(self, eng, fn, waits=(), incs=(), pw=(), pa=()):
        wl = [(s, int(v)) for (s, v) in waits]
        il = []
        for s, a in incs:
            s.v += int(a)
            il.append((s, int(a)))
        self.q[eng].append((fn, wl, il, tuple(pw), tuple(pa)))

    def dma(self, eng, sem, fn, waits=()):
        wl = [(sem, sem.v)] + list(waits)
        self.op(eng, fn, waits=wl, incs=[(sem, 16)])
        return sem.v

    def psum_check(self):
        eng_ix = {e: i for i, e in enumerate(ENGINES)}
        producers = {}
        for e in ENGINES:
            for i, (fn, wl, il, pw, pa) in enumerate(self.q[e]):
                for s, a in il:
                    producers.setdefault(s.name, []).append((e, i, a))
        import bisect
        cur = {}
        pos = {e: 0 for e in ENGINES}
        clocks = {e: [] for e in ENGINES}
        semhist = {}
        done = False
        while not done:
            done = True
            for e in ENGINES:
                while pos[e] < len(self.q[e]):
                    fn, wl, il, pw, pa = self.q[e][pos[e]]
                    if not all(cur.get(s.name, 0) >= v for s, v in wl):
                        break
                    vc = [0] * len(ENGINES)
                    if pos[e] > 0:
                        vc = list(clocks[e][pos[e] - 1])
                    vc[eng_ix[e]] = pos[e] + 1
                    for s, v in wl:
                        if v <= 0:
                            continue
                        hist = semhist.get(s.name, [])
                        lo, hi = 0, len(hist)
                        while lo < hi:
                            midi = (lo + hi) // 2
                            if hist[midi][0] >= v:
                                hi = midi
                            else:
                                lo = midi + 1
                        if lo < len(hist):
                            pvc = hist[lo][1]
                            vc = [max(a, b) for a, b in zip(vc, pvc)]
                        else:
                            raise RuntimeError(f"wait on {s.name}>={v} never produced")
                    clocks[e].append(vc)
                    for s, a in il:
                        cur[s.name] = cur.get(s.name, 0) + a
                        semhist.setdefault(s.name, []).append((cur[s.name], vc))
                    pos[e] += 1
                    done = False
        bank_pe = {}
        bank_va = {}
        for e in ENGINES:
            for i, (fn, wl, il, pw, pa) in enumerate(self.q[e]):
                for b in pw:
                    bank_pe.setdefault(b, []).append((e, i))
                for b in pa:
                    bank_va.setdefault(b, []).append((e, i))
        def hb(e1, i1, e2, i2):
            return clocks[e2][i2][eng_ix[e1]] >= i1 + 1
        bad = []
        for b in sorted(set(bank_pe) | set(bank_va)):
            for (e1, i1) in bank_pe.get(b, []):
                for (e2, i2) in bank_va.get(b, []):
                    if not (hb(e1, i1, e2, i2) or hb(e2, i2, e1, i1)):
                        bad.append((b, e1, i1, e2, i2))
        if bad:
            raise RuntimeError(f"PSUM collisions (bank, pe_op, other): {bad[:10]} (total {len(bad)})")

    def check(self):
        cur = {}
        pos = {e: 0 for e in ENGINES}
        progress = True
        while progress:
            progress = False
            for e in ENGINES:
                while pos[e] < len(self.q[e]):
                    fn, wl, il = self.q[e][pos[e]][:3]
                    if all(cur.get(s.name, 0) >= v for s, v in wl):
                        for s, a in il:
                            cur[s.name] = cur.get(s.name, 0) + a
                        pos[e] += 1
                        progress = True
                    else:
                        break
        stuck = {e: (len(self.q[e]) - pos[e]) for e in ENGINES if pos[e] < len(self.q[e])}
        if stuck:
            msgs = []
            for e in ENGINES:
                if pos[e] < len(self.q[e]):
                    fn, wl, il = self.q[e][pos[e]]
                    missing = [(s.name, v, cur.get(s.name, 0)) for s, v in wl
                               if cur.get(s.name, 0) < v]
                    msgs.append(f"{e}@{pos[e]}/{len(self.q[e])} needs {missing}")
            raise RuntimeError("deadlock: " + "; ".join(msgs))

    def replay(self, block):
        def mk(eng):
            ops = self.q[eng]

            def body(e):
                drain_incs = eng in ("vector", "scalar")
                for fn, wl, il, pw, pa in ops:
                    for s, v in wl:
                        if v > 0:
                            e.wait_ge(s.h, v)
                    inst = fn(e)
                    if il:
                        if inst is None:
                            inst = e.nop()
                        elif drain_incs:
                            inst = e.drain()
                        for s, a in il:
                            inst = inst.then_inc(s.h, a)
            return body
        block.sync(mk("sync"))
        block.tensor(mk("tensor"))
        block.vector(mk("vector"))
        block.scalar(mk("scalar"))
        block.gpsimd(mk("gpsimd"))




class GatherFeeder:

    def __init__(self, em, sems, bufs, lutc, jobs_gu, jobs_dn, NI):
        self.em = em
        self.sv, self.sgi, self.sgw, self.sgp = sems
        self.idxb, self.accb, self.tmpb = bufs
        self.lutc = lutc
        self.subs = []
        oun = (NI + 1023) // 1024
        dn_units = []
        for scr, idxd, nblk, FD in jobs_dn:
            for oc in range(nblk):
                for half in range(2):
                    dn_units.append((scr, idxd, oc, half * 1024, min(1024, FD - half * 1024),
                                     "dn", oc))
        dpos = 0
        for ou in range(oun):
            ob = ou * 1024
            ow = min(1024, NI - ob)
            for scr, idxd in jobs_gu:
                for hc in range(16):
                    self.subs.append((scr, idxd, hc, ob, ow, "ou", ou))
            take = (len(dn_units) * (ou + 1)) // oun - dpos
            self.subs.extend(dn_units[dpos:dpos + take])
            dpos += take
        self.n = len(self.subs)
        self.pos = 0
        self.opi = 0
        self.loaded = -1
        self.slot_tag = [None, None, None, None]
        self.acc_tag = [0, 0]
        self.ou_tag = {}
        self.wdn_tag = {}
        self._cnt = {}
        self.done = False

    def _load(self, k):
        if k >= self.n or k <= self.loaded:
            return
        scr, idxd, blk, ob, ow, kind, key = self.subs[k]
        sl = k % 4
        w = []
        if self.slot_tag[sl] is not None:
            w.append(self.slot_tag[sl])
        self.em.dma("gpsimd", self.sgi, lambda e, idxd=idxd, blk=blk, ob=ob, ow=ow, sl=sl:
                    e.dma_start(out=self.idxb[:, sl, 0:ow], in_=idxd[blk, :, ob:ob + ow]),
                    waits=w)
        self.loaded = k

    def feed(self, nops):
        em, sv, sgi, sgw = self.em, self.sv, self.sgi, self.sgw
        while nops > 0 and not self.done:
            p = self.pos
            a, b = 2 * p, 2 * p + 1
            eng = "vector"
            gcost = 4 if eng == "gpsimd" else 1
            if self.opi == 0:
                for k in (a, b, a + 2, b + 2):
                    self._load(k)
            two = b < self.n
            seq = []
            for j in range(16):
                seq.append((a, j, "ts"))
                if two:
                    seq.append((b, j, "ts"))
                seq.append((a, j, "tt"))
                if two:
                    seq.append((b, j, "tt"))
            while nops > 0 and self.opi < len(seq):
                k, j, op = seq[self.opi]
                scr, idxd, blk, ob, ow, kind, key = self.subs[k]
                sl = k % 4
                asl = k % 2
                gi_tag = 16 * (self.loaded + 1)
                if op == "ts" and j == 0:
                    w = [(sgi, 16 * (k + 1))]
                    if self.acc_tag[asl]:
                        w.append((sgw, self.acc_tag[asl]))
                    em.op(eng, lambda e, sl=sl, asl=asl, ow=ow: e.tensor_scalar(
                        out=self.accb[:, asl, 0:ow], in0=self.idxb[:, sl, 0:ow],
                        scalar1=0.0, scalar2=self.lutc[:, 0:1], op0=ALU.is_equal, op1=ALU.mult),
                        waits=w)
                elif op == "ts":
                    t = (k % 2) + (2 if eng == "gpsimd" else 0)
                    em.op(eng, lambda e, sl=sl, j=j, ow=ow, t=t: e.tensor_scalar(
                        out=self.tmpb[:, t, 0:ow], in0=self.idxb[:, sl, 0:ow],
                        scalar1=float(j), scalar2=self.lutc[:, j:j + 1], op0=ALU.is_equal, op1=ALU.mult))
                elif op == "tt" and j == 0:
                    pass
                else:
                    last = (j == 15)
                    t = (k % 2) + (2 if eng == "gpsimd" else 0)
                    em.op(eng, lambda e, asl=asl, ow=ow, t=t: e.tensor_add(
                        out=self.accb[:, asl, 0:ow], in0=self.accb[:, asl, 0:ow],
                        in1=self.tmpb[:, t, 0:ow]),
                        incs=([(sv, 1)] if (last and eng == "vector") else []))
                    if last:
                        if eng == "gpsimd":
                            em.op("gpsimd", lambda e: e.drain(), incs=[(self.sgp, 1)])
                            self.slot_tag[sl] = (self.sgp, self.sgp.v)
                            stw = [(self.sgp, self.sgp.v)]
                        else:
                            self.slot_tag[sl] = (sv, sv.v)
                            stw = [(sv, sv.v)]
                        em.dma("gpsimd", sgw, lambda e, scr=scr, blk=blk, ob=ob, ow=ow, asl=asl:
                               e.dma_start(out=scr[blk, :, ob:ob + ow], in_=self.accb[:, asl, 0:ow]),
                               waits=stw)
                        self.acc_tag[asl] = sgw.v
                        ck = (kind, key)
                        self._cnt[ck] = self._cnt.get(ck, 0) + 1
                        if kind == "ou":
                            if self._cnt[ck] == 32:
                                self.ou_tag[key] = sgw.v
                        else:
                            if self._cnt[ck] == 2:
                                self.wdn_tag[key] = sgw.v
                self.opi += 1
                if op == "tt" and j == 0:
                    continue
                nops -= gcost
            if self.opi >= len(seq):
                self.pos += 1
                self.opi = 0
                if 2 * self.pos >= self.n:
                    self.done = True

    def finish(self):
        while not self.done:
            self.feed(1 << 20)


def build_nc(upto=5, debug=False, p1_tiles=NT):
    nc = bass.Bass("TRN2", target_bir_lowering=False, detect_race_conditions=False)
    es = contextlib.ExitStack()

    x_d = nc.dram_tensor("x", [T, V], F32, kind="ExternalInput").ap()
    wq_d = nc.dram_tensor("wq", [V, D], F32, kind="ExternalInput").ap()
    wk_d = nc.dram_tensor("wk", [3, V, D], F32, kind="ExternalInput").ap()
    wv_d = nc.dram_tensor("wv", [3, V, D], F32, kind="ExternalInput").ap()
    wo_d = nc.dram_tensor("wo", [D, V], F32, kind="ExternalInput").ap()
    wg_d = nc.dram_tensor("wg", [V, 3], F32, kind="ExternalInput").ap()
    dlog_d = nc.dram_tensor("dlog", [1, 3], F32, kind="ExternalInput").ap()
    osc_d = nc.dram_tensor("osc", [1, 1], F32, kind="ExternalInput").ap()
    lut_d = nc.dram_tensor("lut", [1, 16], F32, kind="ExternalInput").ap()
    gidx_d = nc.dram_tensor("gidx", [NV, 128, NI], I32, kind="ExternalInput").ap()
    uidx_d = nc.dram_tensor("uidx", [NV, 128, NI], I32, kind="ExternalInput").ap()
    didx_d = nc.dram_tensor("didx", [NO, 128, V], I32, kind="ExternalInput").ap()
    ident_d = nc.dram_tensor("ident", [128, 128], BF16, kind="ExternalInput").ap()
    expnT_d = nc.dram_tensor("expnT", [128, 128], F32, kind="ExternalInput").ap()
    causT_d = nc.dram_tensor("causT", [128, 128], F32, kind="ExternalInput").ap()
    iota_d = nc.dram_tensor("iota", [128, 1], F32, kind="ExternalInput").ap()
    riota_d = nc.dram_tensor("riota", [128, 1], F32, kind="ExternalInput").ap()

    out_d = nc.dram_tensor("out", [T, V], F32, kind="ExternalOutput").ap()

    KSC = "ExternalOutput" if debug else "Internal"
    qT_scr = nc.dram_tensor("qT_scr", [ND, 128, T], BF16, kind=KSC).ap()
    ret_scr = nc.dram_tensor("ret_scr", [3, NT, 128, D], BF16, kind=KSC).ap()
    wgt_scr = nc.dram_tensor("wgt_scr", [NV, 128, NI], BF16, kind=KSC).ap()
    wup_scr = nc.dram_tensor("wup_scr", [NV, 128, NI], BF16, kind=KSC).ap()
    wdn_scr = nc.dram_tensor("wdn_scr", [NO, 128, V], BF16, kind=KSC).ap()
    hT_scr = nc.dram_tensor("hT_scr", [NO, 128, T], BF16, kind=KSC).ap()
    lnd_scr = nc.dram_tensor("lnd_scr", [1, 3], F32).ap()

    S = lambda n: Sem(es, nc, n)
    sp, sv, ss = S("sp"), S("sv"), S("ss")
    sy, sgp = S("sy"), S("sgp")
    sdc, sgb, sdx, sqw = S("sdc"), S("sgb"), S("sdx"), S("sqw")
    sgk, sdqt, srw = S("sgk"), S("sdqt"), S("srw")
    sgi, sgw = S("sgi"), S("sgw")
    sdw, shw = S("sdw"), S("shw")
    sdr, sdd, sdh, sdx2, sout = S("sdr"), S("sdd"), S("sdh"), S("sdx2"), S("sout")

    em = EM()

    def _finalize():
        if debug:
            xnT_dbg = nc.dram_tensor("xnT_dbg", [128, NV * T], BF16, kind="ExternalOutput").ap()
            gate_dbg = nc.dram_tensor("gate_dbg", [128, NT * 3], F32, kind="ExternalOutput").ap()
            if upto <= 4:
                for vc in range(NV):
                    em.dma("sync", sout, lambda e, vc=vc: e.dma_start(
                        out=xnT_dbg[:, vc * T:(vc + 1) * T], in_=xnT[:, vc, :]))
            em.dma("sync", sout, lambda e: e.dma_start(out=gate_dbg[:], in_=gate_sb[:]))
            em.op("sync", lambda e: None, waits=[(sout, sout.v)])
        em.check()
        em.psum_check()
        block = es.enter_context(nc.Block())
        em.replay(block)
        return nc, es

    ps_h = es.enter_context(nc.psum_tensor("ps", [128, 4096], F32))
    ps = ps_h[:]

    def PB(b, n=512):
        return ps[:, b * 512:b * 512 + n]

    psb = ps.bitcast(BF16)

    def PBb(b, off=0, n=1024):
        return psb[:, b * 1024 + off:b * 1024 + off + n]

    SBP = lambda name, shape, dt: es.enter_context(nc.sbuf_tensor("sb_" + name, shape, dt))
    ident = SBP("ident", [128, 128], BF16)
    expnT = SBP("expnT", [128, 128], F32)
    causT = SBP("causT", [128, 128], F32)
    iota = SBP("iotac", [128, 1], F32)
    riota = SBP("riotac", [128, 1], F32)
    lutc = SBP("lutc", [128, 16], F32)
    oscc = SBP("oscc", [128, 1], F32)
    lndc = SBP("lndc", [128, 3], F32)
    dpf = SBP("dpf", [128, 3], F32)
    dpr = SBP("dpr", [128, 3], F32)
    dCc = SBP("dCc", [128, 3], F32)
    wmask = SBP("wmask", [128, 3 * 128], F32)
    gate_sb = SBP("gate_sb", [128, NT * 3], F32)
    row3 = SBP("row3", [1, 3], F32)
    dec3 = SBP("dec3", [1, 3], F32)
    lnd3 = SBP("lnd3", [1, 3], F32)
    t128c = SBP("t128c", [128, 3], F32)
    dummy = SBP("dummy", [128, 4], F32)
    gf_idx = SBP("gf_idx", [128, 4, 1024], BF16)
    gf_acc = SBP("gf_acc", [128, 2, 1024], BF16)
    gf_tmp = SBP("gf_tmp", [128, 4, 1024], BF16)
    mid = contextlib.ExitStack()
    xnT = mid.enter_context(nc.sbuf_tensor("sb_xnT", [128, NV, T], BF16))

    em.dma("sync", sdc, lambda e: e.dma_start(out=ident[:], in_=ident_d[:]))
    em.dma("sync", sdc, lambda e: e.dma_start(out=expnT[:], in_=expnT_d[:]))
    em.dma("sync", sdc, lambda e: e.dma_start(out=causT[:], in_=causT_d[:]))
    em.dma("sync", sdc, lambda e: e.dma_start(out=iota[:], in_=iota_d[:]))
    em.dma("sync", sdc, lambda e: e.dma_start(out=riota[:], in_=riota_d[:]))
    em.dma("sync", sdc, lambda e: e.dma_start(out=row3[:], in_=dlog_d[:]))
    c_base = sdc.v
    if "nobcast" not in SKIP:
        em.dma("gpsimd", sgb, lambda e: e.dma_start(out=lutc[:], in_=lut_d[0:1, :].partition_broadcast(128)))
        em.dma("gpsimd", sgb, lambda e: e.dma_start(out=oscc[:], in_=osc_d[0:1, :].partition_broadcast(128)))

    em.op("vector", lambda e: e.memset(dummy[:], 0.0), incs=[(sv, 1)])
    if "nosetup" in SKIP:
        _skip_setup = True
    lnd_ready = sgb.v
    if "nosetup" not in SKIP:
     em.op("scalar", lambda e: e.activation(out=dec3[:], in_=row3[:], func=AF.Sigmoid),
          waits=[(sdc, c_base)], incs=[(ss, 1)])
     em.op("scalar", lambda e: e.activation(out=lnd3[:], in_=dec3[:], func=AF.Ln),
          waits=[(ss, ss.v)], incs=[(ss, 1)])
     em.dma("sync", sdc, lambda e: e.dma_start(out=lnd_scr[:], in_=lnd3[:]), waits=[(ss, ss.v)])
     em.dma("gpsimd", sgb, lambda e: e.dma_start(out=lndc[:], in_=lnd_scr[0:1, :].partition_broadcast(128)),
           waits=[(sdc, sdc.v)])
     lnd_ready = sgb.v
     for ch in range(3):
         em.op("scalar", lambda e, ch=ch: e.activation(out=dpf[:, ch:ch + 1], in_=iota[:], func=AF.Exp,
                                                       scale=lndc[:, ch:ch + 1]),
               waits=[(sgb, lnd_ready)], incs=[(ss, 1)])
         em.op("scalar", lambda e, ch=ch: e.activation(out=dpr[:, ch:ch + 1], in_=riota[:], func=AF.Exp,
                                                       scale=lndc[:, ch:ch + 1]),
               incs=[(ss, 1)])
         em.op("vector", lambda e, ch=ch: e.tensor_scalar(out=t128c[:, ch:ch + 1], in0=lndc[:, ch:ch + 1],
                                                          scalar1=128.0, scalar2=None, op0=ALU.mult),
               waits=[(sgb, lnd_ready)], incs=[(sv, 1)])
         em.op("scalar", lambda e, ch=ch: e.activation(out=dCc[:, ch:ch + 1], in_=t128c[:, ch:ch + 1], func=AF.Exp),
               waits=[(sv, sv.v)], incs=[(ss, 1)])
         em.op("vector", lambda e, ch=ch: e.tensor_scalar(out=PB(7, 128), in0=expnT[:],
                                                          scalar1=lndc[:, ch:ch + 1], scalar2=None, op0=ALU.mult),
               waits=[(sgb, lnd_ready), (ss, ss.v)], incs=[(sv, 1)], pa=[7])
         em.op("scalar", lambda e: e.activation(out=PB(6, 128), in_=PB(7, 128), func=AF.Exp),
               waits=[(sv, sv.v)], incs=[(ss, 1)], pa=[6, 7])
         em.op("vector", lambda e, ch=ch: e.tensor_mul(out=wmask[:, ch * 128:(ch + 1) * 128], in0=PB(6, 128),
                                                       in1=causT[:]),
               waits=[(ss, ss.v)], incs=[(sv, 1)], pa=[6])

     if upto == 0:
         em.op("sync", lambda e: None, waits=[(sdc, sdc.v)])
         em.op("gpsimd", lambda e: None, waits=[(sgb, sgb.v)])
         em.op("scalar", lambda e: None, waits=[(sv, sv.v)])
         em.op("vector", lambda e: None, waits=[(ss, ss.v)])
         em.check()
         em.psum_check()
         _blk = es.enter_context(nc.Block())
         em.replay(_blk)
         return nc, es

    gf = GatherFeeder(em, (sv, sgi, sgw, sgp), (gf_idx, gf_acc, gf_tmp), lutc,
                      [(wgt_scr, gidx_d), (wup_scr, uidx_d)], [(wdn_scr, didx_d, NO, V)], NI)

    p1 = contextlib.ExitStack()
    SB1 = lambda name, shape, dt: p1.enter_context(nc.sbuf_tensor("sb_" + name, shape, dt))
    x_f = SB1("x_f", [128, 2, V], F32)
    junk = SB1("junk", [128, V], F32)
    xn_nat = SB1("xn_nat", [128, 2, V], BF16)
    wq_bf = SB1("wq_bf", [128, NV, D], BF16)
    wg_bf = SB1("wg_bf", [128, NV, 3], BF16)
    qtt = SB1("qtt", [128, 2, D], BF16)
    exs = SB1("exs", [128, 3], F32)
    cols = SB1("cols", [128, 18], F32)
    sxa, sxb = cols[:, 0:1], cols[:, 1:2]
    sqa, sqb = cols[:, 2:3], cols[:, 3:4]
    m_c, var_c, a_c, b_c = cols[:, 4:5], cols[:, 5:6], cols[:, 6:7], cols[:, 7:8]
    lnv_c, t_c, t2_c, msq_c = cols[:, 8:9], cols[:, 9:10], cols[:, 10:11], cols[:, 11:12]
    mx_c, nmx_c, sm_c, iv_c = cols[:, 12:13], cols[:, 13:14], cols[:, 14:15], cols[:, 15:16]
    eps_c = cols[:, 16:17]
    em.op("vector", lambda e: e.memset(eps_c, EPS), incs=[(sv, 1)])

    if "wqg" not in SKIP:
        em.dma("gpsimd", sgb, lambda e: e.dma_start(
            out=wq_bf[:], in_=wq_d.rearrange("(c p) d -> p c d", p=128)))
        em.dma("gpsimd", sgb, lambda e: e.dma_start(
            out=wg_bf[:], in_=wg_d.rearrange("(c p) d -> p c d", p=128)))
    wqg_ready = sgb.v

    norm_done, stats_used, trevac, gate_ex = {}, {}, {}, {}
    if debug:
        cols_dbg = nc.dram_tensor("cols_dbg", [128, NT * 18], F32, kind="ExternalOutput").ap()
    qevac = 0
    qt_written = 0
    if "xact" in SKIP or "actonly" in SKIP:
        em.dma("sync", sdx, lambda e: e.dma_start(out=x_f[:, 0, :], in_=x_d[0:128, :]))
        em.op("scalar", lambda e: e.activation(out=xn_nat[:, 0, :], in_=x_f[:, 0, :],
                                               func=AF.Identity, scale=0.02),
              waits=[(sdx, sdx.v)], incs=[(ss, 1)])
        em.op("scalar", lambda e: e.activation(out=xn_nat[:, 1, :], in_=x_f[:, 0, :],
                                               func=AF.Identity, scale=0.02), incs=[(ss, 1)])
    for tI in range(p1_tiles):
        xs = tI % 2
        if "actonly" in SKIP:
            w = [(ss, ss.v)]
            if tI >= 2:
                w.append((sv, trevac[tI - 2]))
            em.op("scalar", lambda e, xs=xs: e.activation(out=xn_nat[:, xs, :], in_=x_f[:, 0, :],
                                                          func=AF.Identity, scale=0.02),
                  waits=w, incs=[(ss, 1)])
            norm_done[tI] = ss.v
            stats_used[tI] = sv.v
            tb = (tI % 2) * 2
            w = [(sdc, c_base), (ss, ss.v)]
            if tI >= 2:
                w.append((sv, trevac[tI - 2]))
            for vc in range(NV):
                em.op("tensor", lambda e, xs=xs, vc=vc, tb=tb: e.transpose(
                    PBb(tb + (vc // 8), (vc % 8) * 128, 128),
                    xn_nat[:, xs, vc * 128:(vc + 1) * 128], ident[:]),
                    waits=(w if vc == 0 else []), incs=([(sp, 1)] if vc == NV - 1 else []),
                    pw=[tb + (vc // 8)])
            em.op("vector", lambda e, tI=tI, tb=tb: e.tensor_copy(
                out=xnT[:, :, tI * 128:(tI + 1) * 128], in_=PBb(tb, 0, 2048)),
                waits=[(sp, sp.v)], incs=[(sv, 1)], pa=[tb, tb + 1])
            trevac[tI] = sv.v
            continue
        if "xact" in SKIP:
            norm_done[tI] = ss.v
            stats_used[tI] = sv.v
            tb = (tI % 2) * 2
            w = [(sdc, c_base), (ss, ss.v)]
            if tI >= 2:
                w.append((sv, trevac[tI - 2]))
            for vc in range(NV):
                em.op("tensor", lambda e, xs=xs, vc=vc, tb=tb: e.transpose(
                    PBb(tb + (vc // 8), (vc % 8) * 128, 128),
                    xn_nat[:, xs, vc * 128:(vc + 1) * 128], ident[:]),
                    waits=(w if vc == 0 else []), incs=([(sp, 1)] if vc == NV - 1 else []),
                    pw=[tb + (vc // 8)])
            em.op("vector", lambda e, tI=tI, tb=tb: e.tensor_copy(
                out=xnT[:, :, tI * 128:(tI + 1) * 128], in_=PBb(tb, 0, 2048)),
                waits=[(sp, sp.v)], incs=[(sv, 1)], pa=[tb, tb + 1])
            trevac[tI] = sv.v
            continue
        w = [(ss, norm_done[tI - 2])] if tI >= 2 else []
        em.dma("sync", sdx, lambda e, tI=tI, xs=xs: e.dma_start(
            out=x_f[:, xs, :], in_=x_d[tI * 128:(tI + 1) * 128, :]), waits=w)
        x_tag = sdx.v
        if "stats" in SKIP:
            em.op("scalar", lambda e, xs=xs: e.activation(out=xn_nat[:, xs, :], in_=x_f[:, xs, :],
                                                          func=AF.Identity, scale=0.02),
                  waits=[(sdx, x_tag)] + ([(sv, trevac[tI - 2])] if tI >= 2 else []), incs=[(ss, 1)])
            norm_done[tI] = ss.v
            stats_used[tI] = sv.v
        if "stats" not in SKIP:
            w = [(sdx, x_tag)] + ([(sv, stats_used[tI - 1])] if tI >= 1 else [])
            if tI >= 1:
                w.append((ss, stats_sq_done))
            em.op("vector", lambda e, xs=xs: e.reduce_sum(out=sxa, in_=x_f[:, xs, :], axis=AX),
                  waits=w)
            em.op("scalar", lambda e, xs=xs: e.activation(out=junk[:], in_=x_f[:, xs, :],
                                                          func=AF.Square),
                  waits=[(sdx, x_tag)] + ([(sv, stats_used[tI - 1])] if tI >= 1 else []),
                  incs=[(ss, 1)])
            stats_sq_done = ss.v
            em.op("vector", lambda e: e.reduce_sum(out=sqa, in_=junk[:], axis=AX),
                  waits=[(ss, stats_sq_done)])
            em.op("vector", lambda e: e.tensor_scalar(out=m_c, in0=sxa, scalar1=1.0 / V, scalar2=None, op0=ALU.mult),
                  incs=[(sv, 1)])
            em.op("vector", lambda e: e.tensor_scalar(out=t2_c, in0=sqa, scalar1=1.0 / V, scalar2=None, op0=ALU.mult))
            em.op("vector", lambda e: e.tensor_mul(out=msq_c, in0=m_c, in1=m_c),
                  waits=[(sv, sv.v)], incs=[(sv, 1)])
            em.op("vector", lambda e: e.tensor_sub(out=var_c, in0=t2_c, in1=msq_c),
                  waits=[(sv, sv.v)], incs=[(sv, 1)])
            stats_used[tI] = sv.v
            em.op("scalar", lambda e: e.activation(out=lnv_c, in_=var_c, func=AF.Ln, bias=eps_c),
                  waits=[(sv, sv.v)], incs=[(ss, 1)])
            em.op("scalar", lambda e: e.activation(out=a_c, in_=lnv_c, func=AF.Exp, scale=-0.5),
                  waits=[(ss, ss.v)], incs=[(ss, 1)])
            em.op("vector", lambda e: e.scalar_tensor_tensor(out=b_c, in0=m_c, scalar=-1.0,
                                                             in1=a_c, op0=ALU.mult, op1=ALU.mult),
                  waits=[(ss, ss.v)], incs=[(sv, 1)])
            w = [(sv, sv.v)] + ([(sv, trevac[tI - 2])] if tI >= 2 else [])
            em.op("scalar", lambda e, xs=xs: e.activation(out=xn_nat[:, xs, :], in_=x_f[:, xs, :],
                                                          func=AF.Identity, scale=a_c, bias=b_c),
                  waits=w, incs=[(ss, 1)])
            norm_done[tI] = ss.v
        if debug and "stats" not in SKIP:
            em.dma("sync", sqw, lambda e, tI=tI: e.dma_start(
                out=cols_dbg[:, tI * 18:(tI + 1) * 18], in_=cols[:]), waits=[(ss, norm_done[tI])])
        tb = (tI % 2) * 2
        w = [(ss, norm_done[tI]), (sdc, c_base)]
        if tI >= 2:
            w.append((sv, trevac[tI - 2]))
        if "tr" in SKIP:
            em.op("vector", lambda e: e.tensor_copy(out=dummy[:, 0:2], in_=dummy[:, 2:4]),
                  waits=w, incs=[(sv, 1)])
            trevac[tI] = sv.v
        else:
            for vc in range(NV):
                em.op("tensor", lambda e, xs=xs, vc=vc, tb=tb: e.transpose(
                    PBb(tb + (vc // 8), (vc % 8) * 128, 128),
                    xn_nat[:, xs, vc * 128:(vc + 1) * 128], ident[:]),
                    waits=(w if vc == 0 else []), incs=([(sp, 1)] if vc == NV - 1 else []),
                    pw=[tb + (vc // 8)])
            tr_tag = sp.v
            em.op("vector", lambda e, tI=tI, tb=tb: e.tensor_copy(
                out=xnT[:, :, tI * 128:(tI + 1) * 128], in_=PBb(tb, 0, 2048)),
                waits=[(sp, tr_tag)], incs=[(sv, 1)], pa=[tb, tb + 1])
            trevac[tI] = sv.v
        gf.feed(6)
        if "gate" in SKIP:
            em.op("vector", lambda e, tI=tI: e.memset(gate_sb[:, tI * 3:(tI + 1) * 3], 0.3), incs=[(sv, 1)])
        w = [(sv, trevac[tI]), (sgb, wqg_ready)]
        if tI >= 1 and (tI - 1) in gate_ex:
            w.append((ss, gate_ex[tI - 1]))
        if "gate" not in SKIP:
            for vc in range(NV):
                em.op("tensor", lambda e, tI=tI, vc=vc: e.matmul(
                    PB(4, 3), xnT[:, vc, tI * 128:(tI + 1) * 128], wg_bf[:, vc, :],
                    start=(vc == 0), stop=(vc == NV - 1)),
                    waits=(w if vc == 0 else []), incs=([(sp, 1)] if vc == NV - 1 else []),
                    pw=[4])
            em.op("vector", lambda e: e.reduce_max(out=mx_c, in_=PB(4, 3), axis=AX), waits=[(sp, sp.v)],
                  incs=[(sv, 1)], pa=[4])
            em.op("vector", lambda e: e.tensor_scalar(out=nmx_c, in0=mx_c, scalar1=-1.0, scalar2=None, op0=ALU.mult),
                  waits=[(sv, sv.v)], incs=[(sv, 1)])
            em.op("scalar", lambda e: e.activation(out=exs[:], in_=PB(4, 3), func=AF.Exp, bias=nmx_c),
                  waits=[(sv, sv.v)], incs=[(ss, 1)], pa=[4])
            gate_ex[tI] = ss.v
            em.op("vector", lambda e: e.reduce_sum(out=sm_c, in_=exs[:], axis=AX),
                  waits=[(ss, gate_ex[tI])], incs=[(sv, 1)])
            em.op("scalar", lambda e: e.activation(out=lnv_c, in_=sm_c, func=AF.Ln),
                  waits=[(sv, sv.v)], incs=[(ss, 1)])
            em.op("scalar", lambda e: e.activation(out=iv_c, in_=lnv_c, func=AF.Exp, scale=-1.0),
                  waits=[(ss, ss.v)], incs=[(ss, 1)])
            em.op("vector", lambda e, tI=tI: e.tensor_scalar(out=gate_sb[:, tI * 3:(tI + 1) * 3], in0=exs[:],
                                                             scalar1=iv_c, scalar2=None, op0=ALU.mult),
                  waits=[(ss, ss.v)], incs=[(sv, 1)])
        if tI % 4 == 3 and "qt" not in SKIP:
            tg = tI // 4
            for dc in range(ND):
                qs = qt_written % 2
                w = [(sv, trevac[tI]), (sgb, wqg_ready)]
                if qt_written >= 2:
                    w.append((sqw, 16 * (qt_written - 1)))
                if qt_written >= 1:
                    w.append((sv, qevac))
                for vc in range(NV):
                    em.op("tensor", lambda e, tg=tg, dc=dc, vc=vc: e.matmul(
                        PB(5), wq_bf[:, vc, dc * 128:(dc + 1) * 128],
                        xnT[:, vc, tg * 512:(tg + 1) * 512],
                        start=(vc == 0), stop=(vc == NV - 1)),
                        waits=(w if vc == 0 else []), incs=([(sp, 1)] if vc == NV - 1 else []),
                        pw=[5])
                em.op("vector", lambda e, qs=qs: e.tensor_copy(out=qtt[:, qs, :], in_=PB(5)),
                      waits=[(sp, sp.v)], incs=[(sv, 1)], pa=[5])
                qevac = sv.v
                em.dma("sync", sqw, lambda e, tg=tg, dc=dc, qs=qs: e.dma_start(
                    out=qT_scr[dc, :, tg * 512:(tg + 1) * 512], in_=qtt[:, qs, :]),
                    waits=[(sv, qevac)])
                qt_written += 1

    def barrier():
        if "nobar" in SKIP:
            return
        tags = ([(s, s.v) for s in (sdc, sgb, sdx, sqw, sgk, sdqt, srw, sgi, sgw,
                                    sdw, shw, sdr, sdd, sdh, sdx2, sout)]
                + [(sp, sp.v), (sv, sv.v), (ss, ss.v)])
        tags = [(s, v) for (s, v) in tags if v > 0]
        for e in ENGINES:
            if e == "tensor":
                continue
            em.op(e, lambda e2: None, waits=tags)

    barrier()
    p1.close()
    if upto == 1:
        return _finalize()

    p2 = contextlib.ExitStack()
    SB2 = lambda name, shape, dt: p2.enter_context(nc.sbuf_tensor("sb_" + name, shape, dt))
    wk_bf = SB2("wk_bf", [128, NV, D], BF16)
    wv_bf = SB2("wv_bf", [128, NV, D], BF16)
    S_sb = SB2("S_sb", [128, ND, D], F32)
    S_bf = SB2("S_bf", [128, ND, D], BF16)
    qtc = SB2("qtc", [128, 2, ND, 128], BF16)
    k_bf = SB2("k_bf", [128, D], BF16)
    kp_bf = SB2("kp_bf", [128, D], BF16)
    v_bf = SB2("v_bf", [128, D], BF16)
    kT_bf = SB2("kT_bf", [128, ND, 128], BF16)
    scT_bf = SB2("scT_bf", [128, 128], BF16)
    t1_bf = SB2("t1_bf", [128, D], BF16)
    t2_bf = SB2("t2_bf", [128, D], BF16)
    bt_bf = SB2("bt_bf", [128, 2, D], BF16)
    gd_c = SB2("gd_c", [128, 1], F32)

    pass_w_used = sp.v
    inter_hist = []
    supd = {}
    scast = {}
    gchunk = 0
    kv_evac = None
    kT_evac = sc_evac = ret_evac = a_done = 0
    for ch in range(3):
        em.dma("gpsimd", sgk, lambda e, ch=ch: e.dma_start(
            out=wk_bf[:], in_=wk_d[ch].rearrange("(c p) d -> p c d", p=128)),
            waits=[(sp, pass_w_used)])
        em.dma("gpsimd", sgk, lambda e, ch=ch: e.dma_start(
            out=wv_bf[:], in_=wv_d[ch].rearrange("(c p) d -> p c d", p=128)))
        wkv_tag = sgk.v
        em.op("vector", lambda e: e.memset(S_sb[:], 0.0), waits=[(ss, ss.v), (sp, pass_w_used)])
        em.op("vector", lambda e: e.memset(S_bf[:], 0.0), incs=[(sv, 1)])
        s_ready = sv.v
        sbf_ready = ss.v
        for I in range(NT - 1, -1, -1):
            qslot = gchunk % 2
            w = []
            if len(inter_hist) >= 2:
                w.append((sp, inter_hist[-2]))
            em.dma("sync", sdqt, lambda e, I=I, qslot=qslot: e.dma_start(
                out=qtc[:, qslot, :, :],
                in_=qT_scr[:, :, I * 128:(I + 1) * 128].transpose([1, 0, 2])), waits=w)
            q_tag = sdqt.v
            w = [(sgk, wkv_tag), (sv, s_ready)]
            if kv_evac:
                w.append((ss, kv_evac[1]))
            for vc in range(NV):
                em.op("tensor", lambda e, I=I, vc=vc: e.matmul(
                    PB(0), xnT[:, vc, I * 128:(I + 1) * 128], wk_bf[:, vc, :],
                    start=(vc == 0), stop=(vc == NV - 1)),
                    waits=(w if vc == 0 else []), incs=([(sp, 1)] if vc == NV - 1 else []),
                    pw=[0])
            k_tag = sp.v
            for vc in range(NV):
                em.op("tensor", lambda e, I=I, vc=vc: e.matmul(
                    PB(1), xnT[:, vc, I * 128:(I + 1) * 128], wv_bf[:, vc, :],
                    start=(vc == 0), stop=(vc == NV - 1)),
                    incs=([(sp, 1)] if vc == NV - 1 else []), pw=[1])
            v_tag = sp.v
            w = [(sp, k_tag)]
            if a_done:
                w.append((sp, a_done))
            em.op("scalar", lambda e: e.copy(out=k_bf[:], in_=PB(0)), waits=w, pa=[0])
            em.op("scalar", lambda e, ch=ch: e.activation(
                out=kp_bf[:], in_=PB(0), func=AF.Identity, scale=dpf[:, ch:ch + 1]), pa=[0])
            em.op("scalar", lambda e: e.copy(out=v_bf[:], in_=PB(1)),
                  waits=[(sp, v_tag)], incs=[(ss, 1)], pa=[1])
            kv_evac = ("ss", ss.v)
            gf.feed(3)
            w = [(ss, kv_evac[1])]
            for dc in range(ND):
                em.op("tensor", lambda e, dc=dc: e.transpose(
                    PBb(2, dc * 128, 128), k_bf[:, dc * 128:(dc + 1) * 128], ident[:]),
                    waits=(w if dc == 0 else []), incs=([(sp, 1)] if dc == ND - 1 else []), pw=[2])
            em.op("scalar", lambda e: e.copy(out=kT_bf[:], in_=PBb(2, 0, 512)),
                  waits=[(sp, sp.v)], incs=[(ss, 1)], pa=[2])
            kT_evac = ss.v
            gf.feed(3)
            w = [(ss, kT_evac), (sdqt, q_tag)]
            for dc in range(ND):
                em.op("tensor", lambda e, dc=dc, qslot=qslot: e.matmul(
                    PB(3, 128), kT_bf[:, dc, :], qtc[:, qslot, dc, :],
                    start=(dc == 0), stop=(dc == ND - 1)),
                    waits=(w if dc == 0 else []), incs=([(sp, 1)] if dc == ND - 1 else []), pw=[3])
            em.op("vector", lambda e, ch=ch: e.tensor_tensor(
                out=scT_bf[:], in0=PB(3, 128), in1=wmask[:, ch * 128:(ch + 1) * 128], op=ALU.mult),
                waits=[(sp, sp.v)], incs=[(sv, 1)], pa=[3])
            sc_evac = sv.v
            gf.feed(3)
            w = [(sv, sc_evac)]
            if ret_evac:
                w.append((sv, ret_evac))
            em.op("tensor", lambda e: e.matmul(PB(4), scT_bf[:], v_bf[:], start=True, stop=True),
                  waits=w, incs=[(sp, 1)], pw=[4])
            w = [(ss, sbf_ready), (sdqt, q_tag)]
            for dc in range(ND):
                em.op("tensor", lambda e, dc=dc, qslot=qslot: e.matmul(
                    PB(5), qtc[:, qslot, dc, :], S_bf[:, dc, :],
                    start=(dc == 0), stop=(dc == ND - 1)),
                    waits=(w if dc == 0 else []), incs=([(sp, 1)] if dc == ND - 1 else []), pw=[5])
            inter_tag = sp.v
            inter_hist.append(inter_tag)
            scast_prev = dict(scast)
            supd_prev = dict(supd)
            for dc in range(ND):
                bank = 6 + (dc % 2)
                w = []
                if dc >= 2:
                    w.append((sv, supd[dc - 2]))
                elif (dc + 2) in supd_prev:
                    w.append((sv, supd_prev[dc + 2]))
                em.op("tensor", lambda e, dc=dc, bank=bank: e.matmul(
                    PB(bank), kp_bf[:, dc * 128:(dc + 1) * 128], v_bf[:], start=True, stop=True),
                    waits=w, incs=[(sp, 1)], pw=[bank])
                a_tag = sp.v
                w = [(sp, a_tag)]
                if dc in scast_prev:
                    w.append((ss, scast_prev[dc]))
                em.op("vector", lambda e, dc=dc, bank=bank, ch=ch: e.scalar_tensor_tensor(
                    out=S_sb[:, dc, :], in0=S_sb[:, dc, :], scalar=dCc[:, ch:ch + 1],
                    in1=PB(bank), op0=ALU.mult, op1=ALU.add),
                    waits=w, incs=[(sv, 1)], pa=[bank])
                supd[dc] = sv.v
                em.op("scalar", lambda e, dc=dc: e.activation(
                    out=S_bf[:, dc, :], in_=S_sb[:, dc, :], func=AF.Identity),
                    waits=[(sv, supd[dc]), (sp, inter_tag)], incs=[(ss, 1)])
                scast[dc] = ss.v
            a_done = sp.v
            sbf_ready = ss.v
            gf.feed(3)
            em.op("vector", lambda e, ch=ch, I=I: e.tensor_mul(
                out=gd_c[:], in0=gate_sb[:, I * 3 + ch:I * 3 + ch + 1], in1=dpr[:, ch:ch + 1]),
                waits=[(sp, inter_tag)], incs=[(sv, 1)])
            em.op("scalar", lambda e, ch=ch, I=I: e.activation(
                out=t1_bf[:], in_=PB(4), func=AF.Identity,
                scale=gate_sb[:, I * 3 + ch:I * 3 + ch + 1]), waits=[(sp, inter_tag)], pa=[4])
            em.op("scalar", lambda e: e.activation(
                out=t2_bf[:], in_=PB(5), func=AF.Identity, scale=gd_c[:]),
                waits=[(sv, sv.v)], incs=[(ss, 1)], pa=[5])
            w = [(ss, ss.v)]
            if gchunk >= 2:
                w.append((srw, 16 * (gchunk - 1)))
            em.op("vector", lambda e, qslot=qslot: e.tensor_add(
                out=bt_bf[:, qslot, :], in0=t1_bf[:], in1=t2_bf[:]),
                waits=w, incs=[(sv, 1)])
            ret_evac = sv.v
            em.dma("sync", srw, lambda e, ch=ch, I=I, qslot=qslot: e.dma_start(
                out=ret_scr[ch, I], in_=bt_bf[:, qslot, :]), waits=[(sv, ret_evac)])
            gf.feed(4)
            gchunk += 1
        pass_w_used = sp.v

    barrier()
    p2.close()
    if upto == 2:
        return _finalize()

    p3a = contextlib.ExitStack()
    SA = lambda name, shape, dt: p3a.enter_context(nc.sbuf_tensor("sb_" + name, shape, dt))
    wg_s = SA("wg_s", [128, 2, NV, 256], BF16)
    wu_s = SA("wu_s", [128, 2, NV, 256], BF16)
    hrow = SA("hrow", [128, 2, T], BF16)
    sil = SA("sil", [128, 2, 512], F32)

    ogs = [(og * 256, min(256, NI - og * 256)) for og in range((NI + 255) // 256)]
    h_written = 0
    h_evac = {}
    og_pe_tag = {}
    for og, (obase, owid) in enumerate(ogs):
        osl = og % 2
        ou_need = (obase + owid - 1) // 1024
        assert ou_need in gf.ou_tag, f"gather feeder too slow: og={og} needs ou{ou_need}"
        w = [(sgw, gf.ou_tag[ou_need])]
        if og >= 2:
            w.append((sp, og_pe_tag[og - 2]))
        em.dma("sync", sdw, lambda e, obase=obase, owid=owid, osl=osl: e.dma_start(
            out=wg_s[:, osl, :, 0:owid], in_=wgt_scr[:, :, obase:obase + owid].transpose([1, 0, 2])),
            waits=w)
        em.dma("sync", sdw, lambda e, obase=obase, owid=owid, osl=osl: e.dma_start(
            out=wu_s[:, osl, :, 0:owid], in_=wup_scr[:, :, obase:obase + owid].transpose([1, 0, 2])))
        wt_tag = sdw.v
        for oci in range(max(1, owid // 128) if owid >= 128 else 1):
            hs = h_written % 2
            oc_g = obase // 128 + oci
            for tg in range(8):
                gf.feed(21)
                gb = tg % 2
                ub = 2 + gb
                w = [(sdw, wt_tag)]
                if h_written >= 2 and tg == 0:
                    w.append((shw, 16 * (h_written - 1)))
                if tg in h_evac and False:
                    pass
                key = tg - 2 if tg >= 2 else tg + 6
                if key in h_evac:
                    w.append((sv, h_evac[key]))
                for hc in range(NV):
                    em.op("tensor", lambda e, hc=hc, oci=oci, tg=tg, gb=gb, osl=osl: e.matmul(
                        PB(gb), wg_s[:, osl, hc, oci * 128:(oci + 1) * 128],
                        xnT[:, hc, tg * 512:(tg + 1) * 512],
                        start=(hc == 0), stop=(hc == NV - 1)),
                        waits=(w if hc == 0 else []), incs=([(sp, 1)] if hc == NV - 1 else []), pw=[gb])
                g_tag = sp.v
                for hc in range(NV):
                    em.op("tensor", lambda e, hc=hc, oci=oci, tg=tg, ub=ub, osl=osl: e.matmul(
                        PB(ub), wu_s[:, osl, hc, oci * 128:(oci + 1) * 128],
                        xnT[:, hc, tg * 512:(tg + 1) * 512],
                        start=(hc == 0), stop=(hc == NV - 1)),
                        incs=([(sp, 1)] if hc == NV - 1 else []), pw=[ub])

                u_tag = sp.v
                em.op("scalar", lambda e, gb=gb: e.activation(
                    out=sil[:, gb, :], in_=PB(gb), func=AF.Silu),
                    waits=[(sp, g_tag)], incs=[(ss, 1)], pa=[gb])
                em.op("vector", lambda e, gb=gb, ub=ub, hs=hs, tg=tg: e.tensor_tensor(
                    out=hrow[:, hs, tg * 512:(tg + 1) * 512], in0=sil[:, gb, :], in1=PB(ub),
                    op=ALU.mult),
                    waits=[(ss, ss.v), (sp, u_tag)], incs=[(sv, 1)], pa=[ub])
                h_evac[tg] = sv.v
            em.dma("sync", shw, lambda e, oc_g=oc_g, hs=hs: e.dma_start(
                out=hT_scr[oc_g], in_=hrow[:, hs, :]), waits=[(sv, sv.v)])
            h_written += 1
        og_pe_tag[og] = sp.v

    barrier()
    p3a.close()
    if upto == 4:
        return _finalize()
    mid.close()

    p3b = contextlib.ExitStack()
    SBB = lambda name, shape, dt: p3b.enter_context(nc.sbuf_tensor("sb_" + name, shape, dt))
    blT = SBB("blT", [128, ND, T], BF16)
    wo_bf = SBB("wo_bf", [128, ND, V], BF16)
    rbuf = SBB("rbuf", [128, 3, D], BF16)
    bsum = SBB("bsum", [128, 2, D], BF16)
    rsum_f = SBB("rsum_f", [128, D], F32)
    hT_t = SBB("hT_t", [128, 2, 6, 512], BF16)
    wdn_s = SBB("wdn_s", [128, 2, 6, 1024], BF16)
    g_f = SBB("g_f", [128, 4, V], BF16)
    x_t = SBB("x_t", [128, 4, V], F32)
    o_t = SBB("o_t", [128, 4, V], F32)
    sg_f = SBB("sg_f", [128, 512], F32)

    em.dma("gpsimd", sgb, lambda e: e.dma_start(
        out=wo_bf[:], in_=wo_d.rearrange("(c p) d -> p c d", p=128)))
    wo_tag = sgb.v

    bsum_done = btr_done = bl_evac = 0
    for I in range(NT):
        w = []
        if I >= 1:
            w.append((sv, bsum_done))
        for ch in range(3):
            em.dma("sync", sdr, lambda e, ch=ch, I=I: e.dma_start(
                out=rbuf[:, ch, :], in_=ret_scr[ch, I]), waits=(w if ch == 0 else []))
        r_tag = sdr.v
        bs = I % 2
        w = [(sdr, r_tag)]
        if I >= 2:
            w.append((sp, btr_done))
        em.op("vector", lambda e: e.tensor_add(out=rsum_f[:], in0=rbuf[:, 0, :], in1=rbuf[:, 1, :]),
              waits=w, incs=[(sv, 1)])
        em.op("vector", lambda e, bs=bs: e.tensor_add(out=bsum[:, bs, :], in0=rsum_f[:], in1=rbuf[:, 2, :]),
              waits=[(sv, sv.v)], incs=[(sv, 1)])
        bsum_done = sv.v
        w = [(sv, bsum_done)]
        if I >= 1:
            w.append((sv, bl_evac))
        for dc in range(ND):
            em.op("tensor", lambda e, dc=dc, bs=bs: e.transpose(
                PBb(6, dc * 128, 128), bsum[:, bs, dc * 128:(dc + 1) * 128], ident[:]),
                waits=(w if dc == 0 else []), incs=([(sp, 1)] if dc == ND - 1 else []), pw=[6])
        btr_done = sp.v
        em.op("vector", lambda e, I=I: e.tensor_copy(
            out=blT[:, :, I * 128:(I + 1) * 128], in_=PBb(6, 0, 512)),
            waits=[(sp, btr_done)], incs=[(sv, 1)], pa=[6])
        bl_evac = sv.v
        gf.feed(16)

    gf.finish()
    ngrp = (NO + 5) // 6
    wdn_used = {}
    comb_done = 0
    half_comb = 0
    for tg in range(8):
        tb = tg * 512
        for ts in range(4):
            em.dma("sync", sdx2, lambda e, tb=tb, ts=ts: e.dma_start(
                out=x_t[:, ts, :], in_=x_d[tb + ts * 128:tb + (ts + 1) * 128, :]),
                waits=([(sv, comb_done)] if (tg >= 1 and ts == 0) else []))
        x_tag = sdx2.v
        for half in range(2):
            for grp in range(ngrp):
                oc0 = grp * 6
                nsub = min(6, NO - oc0)
                dsl = grp % 2
                w = [(sgw, gf.wdn_tag[oc0 + nsub - 1])]
                if dsl in wdn_used:
                    w.append((sp, wdn_used[dsl]))
                em.dma("sync", sdd, lambda e, oc0=oc0, nsub=nsub, dsl=dsl, half=half: e.dma_start(
                    out=wdn_s[:, dsl, 0:nsub, :],
                    in_=wdn_scr[oc0:oc0 + nsub, :, half * 1024:(half + 1) * 1024].transpose([1, 0, 2])),
                    waits=w)
                d_tag = sdd.v
                w = [(sv, bl_evac)] if (tg == 0 and half == 0 and grp == 0) else []
                if dsl in wdn_used:
                    w.append((sp, wdn_used[dsl]))
                em.dma("gpsimd", sdh, lambda e, tb=tb, oc0=oc0, nsub=nsub, dsl=dsl: e.dma_start(
                    out=hT_t[:, dsl, 0:nsub, :],
                    in_=hT_scr[oc0:oc0 + nsub, :, tb:tb + 512].transpose([1, 0, 2])), waits=w)
                h_tag = sdh.v
                for sub in range(nsub):
                    oc = oc0 + sub
                    w0 = [(sdd, d_tag), (sdh, h_tag)]
                    if oc == 0 and half_comb:
                        w0.append((sv, half_comb))
                    first = (oc == 0)
                    last = (oc == NO - 1)
                    for ts in range(4):
                        for vgl in range(2):
                            em.op("tensor", lambda e, oc=oc, ts=ts, vgl=vgl, dsl=dsl, sub=sub,
                                  first=first, last=last: e.matmul(
                                PB(ts * 2 + vgl), hT_t[:, dsl, sub, ts * 128:(ts + 1) * 128],
                                wdn_s[:, dsl, sub, vgl * 512:(vgl + 1) * 512],
                                start=first, stop=last),
                                waits=(w0 if (ts == 0 and vgl == 0) else []),
                                incs=([(sp, 1)] if (ts == 3 and vgl == 1) else []),
                                pw=[ts * 2 + vgl])
                wdn_used[dsl] = sp.v
            dn_done = sp.v
            for ts in range(4):
                for vgl in range(2):
                    em.op("scalar", lambda e, ts=ts, vgl=vgl, half=half: e.activation(
                        out=g_f[:, ts, half * 1024 + vgl * 512:half * 1024 + (vgl + 1) * 512],
                        in_=PB(ts * 2 + vgl), func=AF.Silu),
                        waits=([(sp, dn_done)] if (ts == 0 and vgl == 0) else []),
                        incs=([(ss, 1)] if (ts == 3 and vgl == 1) else []), pa=[ts * 2 + vgl])
            g_done = ss.v
            for ts in range(4):
                for vgl in range(2):
                    bank = ts * 2 + vgl
                    vgg = half * 2 + vgl
                    for dc in range(ND):
                        em.op("tensor", lambda e, ts=ts, vgg=vgg, dc=dc, tb=tb, bank=bank: e.matmul(
                            PB(bank), blT[:, dc, tb + ts * 128:tb + (ts + 1) * 128],
                            wo_bf[:, dc, vgg * 512:(vgg + 1) * 512],
                            start=(dc == 0), stop=(dc == ND - 1)),
                            waits=([(ss, g_done), (sgb, wo_tag)] if (ts == 0 and vgl == 0 and dc == 0) else []),
                            incs=([(sp, 1)] if dc == ND - 1 else []), pw=[bank])
                    s_tag = sp.v
                    em.op("vector", lambda e, ts=ts, half=half, vgl=vgl, bank=bank: e.scalar_tensor_tensor(
                        out=sg_f[:], in0=PB(bank), scalar=oscc[:, 0:1],
                        in1=g_f[:, ts, half * 1024 + vgl * 512:half * 1024 + (vgl + 1) * 512],
                        op0=ALU.mult, op1=ALU.mult),
                        waits=[(sp, s_tag), (sdx2, x_tag), (sout, sout.v)], incs=[(sv, 1)], pa=[bank])
                    em.op("vector", lambda e, ts=ts, vgg=vgg: e.tensor_add(
                        out=o_t[:, ts, vgg * 512:(vgg + 1) * 512], in0=sg_f[:],
                        in1=x_t[:, ts, vgg * 512:(vgg + 1) * 512]),
                        waits=[(sv, sv.v)], incs=[(sv, 1)])
            half_comb = sv.v
        comb_done = sv.v
        for ts in range(4):
            em.dma("gpsimd", sout, lambda e, tb=tb, ts=ts: e.dma_start(
                out=out_d[tb + ts * 128:tb + (ts + 1) * 128, :], in_=o_t[:, ts, :]),
                waits=[(sv, comb_done)])

    em.op("sync", lambda e: None, waits=[(sout, sout.v)])

    em.check()
    em.psum_check()
    block = es.enter_context(nc.Block())
    em.replay(block)
    p3b.close()
    return nc, es


def kernel(L0_attn_x, L0_attn_Wq, L0_attn_Wk, L0_attn_Wv, L0_attn_Wo, L0_attn_Wg,
           L0_attn_decay_logits, L0_attn_out_scale, L0_ffn_lut, L0_ffn_gate_idx,
           L0_ffn_up_idx, L0_ffn_down_idx):
    B = L0_attn_x.shape[0]
    gidx = np.ascontiguousarray(np.asarray(L0_ffn_gate_idx, dtype=np.int32).T).reshape(NV, 128, NI)
    uidx = np.ascontiguousarray(np.asarray(L0_ffn_up_idx, dtype=np.int32).T).reshape(NV, 128, NI)
    didx = np.ascontiguousarray(np.asarray(L0_ffn_down_idx, dtype=np.int32).T).reshape(NO, 128, V)
    sl = np.arange(128, dtype=np.float32)
    expnT = (sl[:, None] - sl[None, :] - 1.0) * (sl[:, None] > sl[None, :])
    causT = (sl[:, None] > sl[None, :]).astype(np.float32)
    consts = dict(
        wq=np.ascontiguousarray(np.asarray(L0_attn_Wq, dtype=np.float32)),
        wk=np.ascontiguousarray(np.asarray(L0_attn_Wk, dtype=np.float32)),
        wv=np.ascontiguousarray(np.asarray(L0_attn_Wv, dtype=np.float32)),
        wo=np.ascontiguousarray(np.asarray(L0_attn_Wo, dtype=np.float32)),
        wg=np.ascontiguousarray(np.asarray(L0_attn_Wg, dtype=np.float32)),
        dlog=np.asarray(L0_attn_decay_logits, dtype=np.float32).reshape(1, 3),
        osc=np.asarray(L0_attn_out_scale, dtype=np.float32).reshape(1, 1),
        lut=np.asarray(L0_ffn_lut, dtype=np.float32).reshape(1, 16),
        gidx=gidx, uidx=uidx, didx=didx,
        ident=np.eye(128, dtype=bfdt),
        expnT=np.ascontiguousarray(expnT.astype(np.float32)),
        causT=causT,
        iota=np.ascontiguousarray(sl.reshape(128, 1)),
        riota=np.ascontiguousarray((127.0 - sl).reshape(128, 1)),
    )
    xs = np.asarray(L0_attn_x, dtype=np.float32)
    in_maps = [dict(consts, x=np.ascontiguousarray(xs[b])) for b in range(B)]
    nc, es = build_nc()
    try:
        res = run_bass_kernel_spmd(nc, in_maps, core_ids=list(range(B)))
    finally:
        es.close()
    return np.stack([np.asarray(res.results[b]["out"], dtype=np.float32) for b in range(B)], axis=0)


if __name__ == "__main__":
    nc, es = build_nc()
    print("build ok; instructions:", nc.n_instructions())
